# version 04
# speedup vs baseline: 1.0222x; 1.0165x over previous
import numpy as np
from contextlib import ExitStack

import concourse.bass as bass
import concourse.mybir as mybir
from concourse.bass_utils import run_bass_kernel_spmd

F32 = mybir.dt.float32
BF16 = mybir.dt.bfloat16
AF = mybir.ActivationFunctionType
ALU = mybir.AluOpType
AX = mybir.AxisListType

PAIRS = [[0, 1], [2, 3], [4, 5], [6, 7]]
ALL8 = [[0, 1, 2, 3, 4, 5, 6, 7]]

NT = 2048
D = 1024
NCH = D // 128
MAGIC = 12582912.0
C22 = 4194304.0

ENGS = ("sync", "gpsimd", "vector", "scalar", "tensor")

_EMIT = {"tensor_copy", "tensor_tensor", "tensor_add", "tensor_sub", "tensor_mul",
         "tensor_max", "tensor_scalar", "tensor_single_scalar", "tensor_scalar_add",
         "tensor_scalar_sub", "tensor_scalar_mul", "tensor_scalar_max",
         "tensor_scalar_min", "tensor_relu", "tensor_reduce", "reduce_max",
         "reduce_sum", "scalar_tensor_tensor", "tensor_tensor_scan", "memset",
         "memzero", "reciprocal", "select", "copy_predicated", "activation",
         "sign", "square", "sqrt", "mul", "copy", "add", "bn_stats", "bn_aggr",
         "max", "max_index", "pool", "pool_avg", "pool_max", "stream_shuffle",
         "match_replace", "transpose", "engine_nop"}


class _DrainWrap:

    def __init__(self, e):
        object.__setattr__(self, "_e", e)

    def __getattr__(self, name):
        e = object.__getattribute__(self, "_e")
        a = getattr(e, name)
        if name in _EMIT:
            def w(*args, **kw):
                e.drain()
                return a(*args, **kw)
            return w
        return a


class Prog:

    def __init__(self):
        self.nc = bass.Bass("TRN2", target_bir_lowering=False, debug=False,
                            detect_race_conditions=False)
        self.es = ExitStack()
        self.px = ""
        self._phys = {}
        self._physcnt = {}
        self._physfree = []
        self._map = {}
        self.cnt = {}
        self.q = {e: [] for e in ENGS}
        self._scoped = {}
        b = (self.nc.sbuf_base + 255) // 256 * 256
        self._free = [(b, self.nc.sbuf_top)]
        self._sbal = {}
        self._sbhist = {}
        self._psorder = []

    def _n(self, name):
        return name[1:] if name.startswith("/") else self.px + name

    def _alloc_bytes(self, n):
        best = None
        for i, (lo, hi) in enumerate(self._free):
            if hi - lo >= n and (best is None or hi - lo < self._free[best][1] - self._free[best][0]):
                best = i
        if best is None:
            raise MemoryError(f"SBUF pool exhausted: need {n}, free={self._free}")
        lo, hi = self._free[best]
        if hi - lo == n:
            self._free.pop(best)
        else:
            self._free[best] = (lo + n, hi)
        return lo

    def _free_bytes(self, lo, n):
        self._free.append((lo, lo + n))
        self._free.sort()
        merged = []
        for seg in self._free:
            if merged and merged[-1][1] >= seg[0]:
                merged[-1] = (merged[-1][0], max(merged[-1][1], seg[1]))
            else:
                merged.append(list(seg))
        self._free = [tuple(x) for x in merged]

    def sb(self, name, shape, dtype=F32):
        fn = self._n(name)
        isz = 2 if dtype == BF16 else 4
        n = 1
        for d in shape[1:]:
            n *= d
        n = (n * isz + 255) // 256 * 256
        off = self._alloc_bytes(n)
        self._sbal[fn] = (off, n)
        self._sbhist[fn] = (off, n)
        return self.nc.alloc_sbuf_tensor_at(fn, shape, dtype, offset=off)

    sbx = sb

    def reclaim(self, name):
        fn = self._n(name)
        if fn in self._sbal:
            return
        off, n = self._sbhist[fn]
        for i, (lo, hi) in enumerate(self._free):
            if lo <= off and off + n <= hi:
                self._free.pop(i)
                if lo < off:
                    self._free.append((lo, off))
                if off + n < hi:
                    self._free.append((off + n, hi))
                self._free.sort()
                self._sbal[fn] = (off, n)
                return
        raise MemoryError(f"reclaim {fn}: bytes not free")

    def psx(self, name, shape, dtype=F32):
        fn = self._n(name)
        cm = self.nc.psum_tensor(fn, shape, dtype)
        t = cm.__enter__()
        self._scoped[fn] = cm
        self._psorder.append(fn)
        return t

    def rel(self, *names):
        fns = []
        for name in names:
            fn = self._n(name)
            if fn not in self._scoped and fn not in self._sbal and (
                    name in self._scoped or name in self._sbal):
                fn = name
            fns.append(fn)
        ps = [n for n in fns if n in self._scoped]
        for name in sorted(ps, key=lambda n: -self._psorder.index(n)):
            cm = self._scoped.pop(name)
            cm.__exit__(None, None, None)
        for name in fns:
            if name in self._sbal:
                lo, n = self._sbal.pop(name)
                self._free_bytes(lo, n)

    def ps(self, name, shape, dtype=F32):
        return self.es.enter_context(
            self.nc.psum_tensor(self._n(name), shape, dtype))

    def dram(self, name, shape, dtype=F32, **kw):
        return self.nc.dram_tensor(self._n(name), shape, dtype,
                                   kind="Internal", **kw).ap()

    def sem(self, name):
        fn = self._n(name)
        if fn not in self._map:
            if self._physfree:
                ph = self._physfree.pop(0)
            else:
                ph = f"S{len(self._phys)}"
                self._phys[ph] = self.es.enter_context(self.nc.semaphore(ph))
            self._map[fn] = (ph, self._physcnt.get(ph, 0))
            self.cnt.setdefault(fn, 0)
        return self._phys[self._map[fn][0]]

    def av(self, name, v):
        self.sem(name)
        return self._map[self._n(name)][1] + v

    def mark(self, name, k):
        fn = self._n(name)
        self.sem(name)
        self.cnt[fn] = self.cnt.get(fn, 0) + k
        ph = self._map[fn][0]
        self._physcnt[ph] = self._physcnt.get(ph, 0) + k

    def free_sems(self, prefix):
        for fn in list(self._map):
            if fn.startswith(prefix):
                ph, _ = self._map.pop(fn)
                self._physfree.append(ph)

    def c(self, name):
        return self.cnt.get(self._n(name), 0)

    def step(self, eng, body, waits=(), incs=()):
        wv = []
        for (n, v) in waits:
            s = self.sem(n)
            fn = self._n(n)
            lv = self.cnt.get(fn, 0) if v is None else v
            wv.append((s, self._map[fn][1] + lv, lv))
        iv = []
        for (n, k) in incs:
            s = self.sem(n)
            self.mark(n, k)
            iv.append((s, k))

        def run(e):
            for s, av_, lv in wv:
                if lv > 0:
                    e.wait_ge(s, av_)
            inst = body(_DrainWrap(e) if eng in ("vector", "scalar") else e)
            if isinstance(inst, list):
                for s, k in iv:
                    kk = k // len(inst)
                    for it in inst:
                        it.then_inc(s, kk)
            elif inst is not None:
                for s, k in iv:
                    inst.then_inc(s, k)
        self.q[eng].append(run)

    def emit(self):
        with self.nc.Block() as blk:
            for e in ENGS:
                fns = self.q[e]
                if fns:
                    getattr(blk, e)(lambda eng, fns=fns: [f(eng) for f in fns])
        self.q = {e: [] for e in ENGS}

    def close(self):
        for name in sorted(self._scoped, key=lambda n: -self._psorder.index(n)):
            self._scoped[name].__exit__(None, None, None)
        self._scoped = {}
        self.es.close()



def emit_common(p):
    p.ident_bf = p.sb("ident_bf", [128, 128], BF16)
    p.ident_f = p.sb("ident_f", [128, 128], F32)
    p.ones_r = p.sb("ones_r", [1, 128], F32)
    p.ones_col = p.sb("ones_col", [128, 1], F32)
    p.iota_f = p.sb("iota_f", [128, 128], F32)
    p.iota_p = p.sb("iota_p", [128, 1], F32)

    def io(g):
        g.iota(p.iota_f[:], pattern=[[1, 128]], base=0, channel_multiplier=0,
               allow_small_or_imprecise_dtypes=True)
        return g.iota(p.iota_p[:], pattern=[[0, 1]], base=0, channel_multiplier=1,
                      allow_small_or_imprecise_dtypes=True)
    p.step("gpsimd", io, incs=[("io", 1)])

    def mk(v):
        v.tensor_scalar(p.ident_bf[:], p.iota_f[:], p.iota_p[:], None, ALU.is_equal)
        v.tensor_scalar(p.ident_f[:], p.iota_f[:], p.iota_p[:], None, ALU.is_equal)
        v.memset(p.ones_r[:], 1.0)
        return v.memset(p.ones_col[:], 1.0)
    p.step("vector", mk, waits=[("io", None)], incs=[("common", 1)])


def emit_ln(p, tag, x_sb, xn_bf, x_ok, want_absmax=False):
    t = tag
    sumx = p.sb(t + "sumx", [1, NT])
    sumsq = p.sb(t + "sumsq", [1, NT])
    rows = p.sb(t + "rows", [1, 2 * NT])
    A_r, B_r = rows[0:1, 0:NT], rows[0:1, NT:2 * NT]
    mu_r, ms_r = sumx[0:1, :], sumsq[0:1, :]
    sqs = p.sbx(t + "sqs", [128, 2 * 512])
    ab_b = p.sbx(t + "ab", [128, 2 * NT])
    tmp = p.sbx(t + "tmp", [128, NT])
    mcol = p.sb(t + "mcol", [128, 1])
    psA = p.psx(t + "psA", [1, 512])
    psB = p.psx(t + "psB", [1, 512])
    psR = p.psx(t + "psR", [128, 512])
    out = {"A_r": A_r, "B_r": B_r, "mu_r": mu_r, "ms_r": ms_r}
    psums = [t + "psA", t + "psB", t + "psR"]
    if want_absmax:
        out["runmax"] = p.sb(t + "rmax", [128, 1])
        psT = p.psx(t + "psT", [1, 128])
        psums.append(t + "psT")

    for n in range(4):
        def mmx(te, n=n):
            for k in range(NCH):
                inst = te.matmul(psA[:], p.ones_col[:],
                                 x_sb[:, k * NT + n * 512:k * NT + (n + 1) * 512],
                                 start=(k == 0), stop=(k == NCH - 1))
            return inst
        p.step("tensor", mmx,
               waits=[(x_ok, None), ("common", 1)] + ([(t + "sev", None)] if n else []),
               incs=[(t + "smm", 1)])
        for k in range(NCH):
            idx = n * NCH + k

            def sq(sc, n=n, k=k, idx=idx):
                return sc.activation(sqs[:, (k % 2) * 512:(k % 2) * 512 + 512],
                                     x_sb[:, k * NT + n * 512:k * NT + (n + 1) * 512],
                                     AF.Square)
            w = [(x_ok, None)]
            if idx >= 2:
                w.append((t + "smm2", idx - 1))
            p.step("scalar", sq, waits=w, incs=[(t + "sq", 1)])

            def mmq(te, n=n, k=k):
                return te.matmul(psB[:], p.ones_col[:],
                                 sqs[:, (k % 2) * 512:(k % 2) * 512 + 512],
                                 start=(k == 0), stop=(k == NCH - 1))
            w = [(t + "sq", idx + 1)]
            if n and k == 0:
                w.append((t + "sev", None))
            p.step("tensor", mmq, waits=w, incs=[(t + "smm2", 1)])

        def ev(sc, n=n):
            sc.copy(sumx[0:1, n * 512:(n + 1) * 512], psA[:])
            return sc.copy(sumsq[0:1, n * 512:(n + 1) * 512], psB[:])
        p.step("scalar", ev, waits=[(t + "smm", None), (t + "smm2", None)],
               incs=[(t + "sev", 1)])

    def rows_ve(v):
        v.tensor_scalar(mu_r, sumx[:], 1.0 / D, None, ALU.mult)
        v.tensor_scalar(ms_r, sumsq[:], 1.0 / D, None, ALU.mult)
        v.tensor_mul(A_r, mu_r, mu_r)
        v.tensor_sub(A_r, ms_r, A_r)
        return v.tensor_scalar(A_r, A_r, 1e-5, None, ALU.add)
    p.step("vector", rows_ve, waits=[(t + "sev", None)], incs=[(t + "var", 1)])

    def rows_sc(sc):
        return sc.activation(A_r, A_r, AF.Ln)
    p.step("scalar", rows_sc, waits=[(t + "var", None)], incs=[(t + "Aln", 1)])

    def rows_ve2(v):
        return v.tensor_scalar(A_r, A_r, -0.5, None, ALU.mult)
    p.step("vector", rows_ve2, waits=[(t + "Aln", None)], incs=[(t + "Am", 1)])

    def rows_sc2(sc):
        return sc.activation(A_r, A_r, AF.Exp)
    p.step("scalar", rows_sc2, waits=[(t + "Am", None)], incs=[(t + "A", 1)])

    def rows_b(v):
        return v.tensor_mul(B_r, mu_r, A_r)
    p.step("vector", rows_b, waits=[(t + "A", None)], incs=[(t + "B", 1)])

    for j, row in ((0, A_r), (1, B_r)):
        for n in range(4):
            def rep(te, n=n, row=row):
                return te.matmul(psR[:], p.ones_r[:], row[0:1, n * 512:(n + 1) * 512],
                                 start=True, stop=True)
            p.step("tensor", rep, waits=[(t + "B", None), (t + "repev", None)],
                   incs=[(t + "rep", 1)])

            def repev(sc, j=j, n=n):
                return sc.copy(ab_b[:, j * NT + n * 512:j * NT + (n + 1) * 512], psR[:])
            p.step("scalar", repev, waits=[(t + "rep", None)], incs=[(t + "repev", 1)])

    for k in range(NCH):
        def xn(v, k=k):
            v.tensor_mul(tmp[:], x_sb[:, k * NT:(k + 1) * NT], ab_b[:, 0:NT])
            v.tensor_sub(tmp[:], tmp[:], ab_b[:, NT:2 * NT])
            inst = v.tensor_copy(xn_bf[:, k * NT:(k + 1) * NT], tmp[:])
            if want_absmax:
                if k == 0:
                    inst = v.tensor_reduce(out["runmax"][:], tmp[:], AX.X, ALU.max,
                                           apply_absolute_value=True)
                else:
                    v.tensor_reduce(mcol[:], tmp[:], AX.X, ALU.max,
                                    apply_absolute_value=True)
                    inst = v.tensor_max(out["runmax"][:], out["runmax"][:], mcol[:])
            return inst
        p.step("vector", xn, waits=[(t + "repev", None)], incs=[(t + "xn", 1)])

    if want_absmax:
        def trm(te):
            return te.transpose(psT[:], out["runmax"][:], p.ident_f[:])
        p.step("tensor", trm, waits=[(t + "xn", None)], incs=[(t + "tr", 1)])
        out["gmax"] = p.sb(t + "gmax", [1, 1])

        def redm(v):
            return v.tensor_reduce(out["gmax"][:], psT[:], AX.X, ALU.max)
        p.step("vector", redm, waits=[(t + "tr", None)], incs=[(t + "gm", 1)])
    out["scoped"] = [t + "sqs", t + "ab", t + "tmp"]
    out["psums"] = psums
    return out


def emit_armax_scalar(p, tag, src_sb, src_sem, dst_sb):
    t = tag
    ci = p.dram(t + "ci", [1, 1])
    co = p.dram(t + "co", [1, 1])

    s1, v1 = p.sem(t + "d1"), p.av(t + "d1", 16)
    s2, v2 = p.sem(t + "cc"), p.av(t + "cc", 1)
    p.mark(t + "d1", 16)
    p.mark(t + "cc", 1)

    def go(g, s1=s1, v1=v1, s2=s2, v2=v2):
        g.dma_start(out=ci[:], in_=src_sb[:]).then_inc(s1, 16)
        g.wait_ge(s1, v1)
        g.collective_compute("AllReduce", ALU.max, replica_groups=ALL8,
                             ins=[ci.opt()], outs=[co.opt()]).then_inc(s2, 1)
        g.wait_ge(s2, v2)
        return g.dma_start(out=dst_sb[:], in_=co[:])
    p.step("gpsimd", go, waits=[(src_sem, None)], incs=[(t + "out", 16)])
    return t + "out"



def build_L0(stop="full", fx=None):
    p = Prog() if fx is None else fx["p"]
    nc = p.nc
    inp = {}

    def ein(name, shape, dtype=F32):
        inp[name] = nc.dram_tensor(p.px + name, shape, dtype,
                                   kind="ExternalInput").ap()
        return inp[name]

    if fx is None:
        x_in = ein("x", [D, NT])
    ein("pm_col", [128, 1])
    ein("pmc_col", [128, 1])
    wiT = ein("wiT", [D, 2048])
    wgT = ein("wgT", [D, 2048])
    woT = ein("woT", [2048, D])
    ein("bi_c", [128, 16]); ein("bg_c", [128, 16]); ein("bo_c", [128, 8])
    ein("loga_c", [128, 16]); ein("clip", [1, 2])
    wqg = ein("wqg", [2816, D])
    wqu = ein("wqu", [2816, D])
    wqd = ein("wqd", [D, 2816])
    if fx is None:
        xo = nc.dram_tensor("xo", [D, NT], F32, kind="ExternalOutput").ap()

    u_spill = p.dram("u_spill", [2048, NT])
    hg_spill = p.dram("hg_spill", [2048, NT], BF16)
    h2_spill = p.dram("h2_spill", [2816, NT])
    wgq_t = p.dram("wgq_t", [D, 2816], BF16)
    wuq_t = p.dram("wuq_t", [D, 2816], BF16)
    wdq_t = p.dram("wdq_t", [2816, D], BF16)
    cc_ci = p.dram("cc_ci", [128, 16])
    cc_co = p.dram("cc_co", [128, 16])

    if fx is None:
        x_sb = p.sb("x_sb", [128, NCH * NT])
        xn_bf = p.sbx("xn_bf", [128, NCH * NT], BF16)
    else:
        x_sb, xn_bf = fx["x_sb"], fx["xn_bf"]
    small = p.sb("small", [128, 96])
    bi_s, bg_s = small[:, 0:16], small[:, 16:32]
    la_s, bo_s = small[:, 32:48], small[:, 48:56]
    a_s, lna_s = small[:, 64:80], small[:, 80:96]
    pm_s = p.sb("pm_s", [128, 2])
    clip_s = p.sb("clip_s", [1, 2])
    onecol = p.sb("onecol", [1, 9])
    carr = p.sb("carr", [128, 16])
    cpm = p.sb("cpm", [128, 16])
    scol_g = p.sb("scol_g", [128, 22])
    scol_u = p.sb("scol_u", [128, 22])
    scol_d = p.sb("scol_d", [128, 8])
    sxr_g = p.sb("sxr_g", [128, 22])
    sxr_u = p.sb("sxr_u", [128, 22])
    sxr_d = p.sb("sxr_d", [128, 8])
    sx_col = p.sb("sx_col", [128, 2])
    sh_col = p.sb("sh_col", [128, 2])
    rm2 = p.sb("rm2", [128, 1])
    rmt = p.sb("rmt", [128, 1])
    gm2 = p.sb("gm2", [1, 1])

    scrA = p.sbx("scrA", [128, NT])
    scrB = p.sbx("scrB", [128, NT])
    scrC = p.sbx("scrC", [128, NT])
    scrD = p.sbx("scrD", [128, NT])
    hg_sb = p.sbx("hg_sb", [128, NT], BF16)
    wi_sl = p.sbx("wi_sl", [128, 2 * 8 * 128], BF16)
    rcol = p.sb("rcol", [128, 1])

    pb = [p.psx(f"pb{i}", [128, 512]) for i in range(4)]


    def _finish(wl):
        if fx is not None:
            return wl
        xo3_ = xo.rearrange("(k p) n -> k p n", p=128)

        def _st(sy):
            return [sy.dma_start(out=xo3_[k], in_=x_sb[:, k * NT:(k + 1) * NT])
                    for k in range(NCH)]
        p.step("sync", _st, waits=wl, incs=[("st", 16 * NCH)])
        p.step("sync", lambda e: None, waits=[("st", None)])
        p.emit()
        return p

    if fx is None:
        emit_common(p)

        x3 = x_in.rearrange("(k p) n -> k p n", p=128)

        def ldx(sy):
            return [sy.dma_start(out=x_sb[:, k * NT:(k + 1) * NT], in_=x3[k])
                    for k in range(NCH)]
        p.step("sync", ldx, incs=[("x_ok", 16 * NCH)])

    def ldsmall(sy):
        return [sy.dma_start(out=bi_s, in_=inp["bi_c"][:]),
                sy.dma_start(out=bg_s, in_=inp["bg_c"][:]),
                sy.dma_start(out=la_s, in_=inp["loga_c"][:]),
                sy.dma_start(out=bo_s, in_=inp["bo_c"][:]),
                sy.dma_start(out=clip_s[:], in_=inp["clip"][:]),
                sy.dma_start(out=pm_s[:, 0:1], in_=inp["pm_col"][:]),
                sy.dma_start(out=pm_s[:, 1:2], in_=inp["pmc_col"][:])]
    p.step("sync", ldsmall, incs=[("sm_ok", 16 * 7)])

    def acols(sc):
        sc.activation(a_s, la_s, AF.Sigmoid)
        return sc.activation(lna_s, a_s, AF.Ln)
    p.step("scalar", acols, waits=[("sm_ok", None)], incs=[("acols", 1)])

    ln = emit_ln(p, "ln0", x_sb, xn_bf, "x_ok", want_absmax=True)
    g1 = onecol[0:1, 3:4]
    emit_armax_scalar(p, "g1", ln["gmax"], "ln0gm", g1)
    p.rel(*ln["scoped"])

    if stop == "ln":
        return _finish([("ln0xn", None), ("g1out", None)])

    p.rel(*ln["psums"])
    psTR = [p.psx("psTRa", [128, 128], BF16), p.psx("psTRb", [128, 128], BF16)]
    wnat = p.sbx("wnat", [128, 2816])
    wint = p.sbx("wint", [128, 2816], BF16)
    wT_st = p.sbx("wT_st", [128, 2816], BF16)

    def emit_quant(wsrc, wdst, scol, nchk, cw, qt, prev_tev=None, prev_st=None):
        w3 = wsrc.rearrange("(c p) m -> c p m", p=128)
        wd3 = wdst.rearrange("(j p) m -> j p m", p=128)
        dbuf = 2 * cw <= 2816
        chunks = []
        for c in range(nchk):
            def chunk(c=c):
                sl = (c % 2) * cw if dbuf else 0

                def ldn(sc, c=c, sl=sl):
                    return sc.dma_start(out=wnat[:, sl:sl + cw], in_=w3[c])
                if c >= 2 or (c == 1 and not dbuf):
                    w = [(qt + "vq", c - 1 if dbuf else c)]
                elif c == 1:
                    w = []
                elif qt == "qg":
                    w = [("ln0xn", 8), ("ln0sev", 4), ("ln0repev", 8),
                         ("ln0gm", 1)]
                elif qt == "qu":
                    w = [("qgvq", 22)]
                else:
                    w = [("quvq", 22)]
                p.step("scalar", ldn, waits=w, incs=[(qt + "ld", 16)])

                def vq(v, c=c, sl=sl):
                    wv = wnat[:, sl:sl + cw]
                    v.tensor_reduce(scol[:, c:c + 1], wv, AX.X, ALU.max,
                                    apply_absolute_value=True)
                    v.reciprocal(rcol[:], scol[:, c:c + 1])
                    v.tensor_scalar(scol[:, c:c + 1], scol[:, c:c + 1],
                                    1.0 / 127.0, None, ALU.mult)
                    v.tensor_scalar(wv, wv, rcol[:], 127.0, ALU.mult, ALU.mult)
                    return v.tensor_scalar(wint[:, sl:sl + cw], wv, MAGIC, MAGIC,
                                           ALU.add, ALU.subtract)
                wts = [(qt + "ld", 16 * (c + 1))]
                if dbuf and c >= 2:
                    wts.append((qt + "tev", (c - 1) * (cw // 128)))
                elif not dbuf and c >= 1:
                    wts.append((qt + "tev", c * (cw // 128)))
                p.step("vector", vq, waits=wts, incs=[(qt + "vq", 1)])

                for j in range(cw // 128):
                    gidx = c * (cw // 128) + j
                    bk = gidx

                    def tr(te, c=c, j=j, bk=bk, sl=sl):
                        return te.transpose(psTR[bk % 2][:],
                                            wint[:, sl + j * 128:sl + (j + 1) * 128],
                                            p.ident_bf[:])
                    w = [(qt + "vq", c + 1)]
                    if gidx >= 2:
                        w.append((qt + "tev", gidx - 2 + 1))
                    elif prev_tev is not None:
                        w.append(prev_tev)
                    p.step("tensor", tr, waits=w, incs=[(qt + "tr", 1)])

                    def tev(sc, j=j, bk=bk):
                        return sc.copy(wT_st[:, j * 128:(j + 1) * 128],
                                       psTR[bk % 2][:])
                    w = [(qt + "tr", None)]
                    if j == 0:
                        if c >= 1:
                            w.append((qt + "st", 16 * (cw // 128) * c))
                        elif prev_st is not None:
                            w.append(prev_st)
                    p.step("scalar", tev, waits=w, incs=[(qt + "tev", 1)])

                def st(sy, c=c):
                    return [sy.dma_start(out=wd3[j][:, c * 128:(c + 1) * 128],
                                         in_=wT_st[:, j * 128:(j + 1) * 128])
                            for j in range(cw // 128)]
                p.step("sync", st, waits=[(qt + "tev", None)],
                       incs=[(qt + "st", 16 * (cw // 128))])
            chunks.append(chunk)
        return chunks

    qchunks = (emit_quant(wqg, wgq_t, scol_g, 22, 1024, "qg")
               + emit_quant(wqu, wuq_t, scol_u, 22, 1024, "qu",
                            prev_tev=("qgtev", 176), prev_st=("qgst", 16 * 176))
               + emit_quant(wqd, wdq_t, scol_d, 8, 2816, "qd",
                            prev_tev=("qutev", 176), prev_st=("qust", 16 * 176)))

    u_sb, apow = scrA, scrB
    wi3 = wiT.rearrange("(k p) h -> p k h", p=128)
    for g in range(16):
        buf = (g % 2) * 8 * 128

        def ldwi(gp, g=g, buf=buf):
            return gp.dma_start(out=wi_sl[:, buf:buf + 8 * 128],
                                in_=wi3[:, :, g * 128:(g + 1) * 128])
        p.step("gpsimd", ldwi,
               waits=([("mm_u", 4 * (g - 1))] if g >= 2 else []),
               incs=[("ld_wi", 16)])

        for n in range(4):
            def umm(te, g=g, n=n, buf=buf):
                for k in range(NCH):
                    inst = te.matmul(pb[n][:],
                                     wi_sl[:, buf + k * 128:buf + (k + 1) * 128],
                                     xn_bf[:, k * NT + n * 512:k * NT + (n + 1) * 512],
                                     start=(k == 0), stop=(k == NCH - 1))
                return inst
            w = [("ld_wi", 16 * (g + 1)), ("ln0xn", None)]
            if g >= 1:
                w.append(("ev_u", 4 * (g - 1) + n + 1))
            p.step("tensor", umm, waits=w, incs=[("mm_u", 1)])

            def uev(sc, g=g, n=n):
                return sc.activation(u_sb[:, n * 512:(n + 1) * 512], pb[n][:],
                                     AF.Identity, bias=bi_s[:, g:g + 1])
            w = [("mm_u", None), ("sm_ok", None)]
            if g >= 1 and n == 0:
                w += [("sy_u", 16 * g), ("ve_ca", g)]
            p.step("scalar", uev, waits=w, incs=[("ev_u", 1)])

        def spill(sy, g=g):
            return sy.dma_start(out=u_spill[g * 128:(g + 1) * 128, :], in_=u_sb[:])
        p.step("sync", spill, waits=[("ev_u", None)], incs=[("sy_u", 16)])

        def apwio(gp, g=g):
            return gp.iota(apow[:], pattern=[[-1, NT]], base=NT - 1, channel_multiplier=0,
                           allow_small_or_imprecise_dtypes=True)
        w = []
        if g >= 1:
            w.append(("ve_ca", g))
        p.step("gpsimd", apwio, waits=w, incs=[("io_ap", 1)])

        def apw(sc, g=g):
            return sc.activation(apow[:], apow[:], AF.Exp, scale=lna_s[:, g:g + 1])
        w = [("io_ap", None), ("acols", None)]
        if g >= 1:
            w.append(("ve_ca", g))
        p.step("scalar", apw, waits=w, incs=[("sc_ap", 1)])

        def carry(v, g=g):
            v.tensor_mul(apow[:], u_sb[:], apow[:])
            return v.tensor_reduce(carr[:, g:g + 1], apow[:], AX.X, ALU.add)
        p.step("vector", carry, waits=[("sc_ap", None), ("ev_u", None)],
               incs=[("ve_ca", 1)])

        for _ in range(4):
            if qchunks:
                qchunks.pop(0)()

    def contrib(v):
        return v.tensor_scalar(cpm[:], carr[:], pm_s[:, 1:2], None, ALU.mult)
    p.step("vector", contrib, waits=[("ve_ca", None), ("sm_ok", None)],
           incs=[("ve_cc", 1)])

    _sd, _vd = p.sem("cg_d"), p.av("cg_d", 16)
    _sc, _vc = p.sem("cg_c"), p.av("cg_c", 1)
    p.mark("cg_d", 16)
    p.mark("cg_c", 1)
    def ccarry(g, _sd=_sd, _vd=_vd, _sc=_sc, _vc=_vc):
        g.dma_start(out=cc_ci[:], in_=cpm[:]).then_inc(_sd, 16)
        g.wait_ge(_sd, _vd)
        g.collective_compute("AllReduce", ALU.add, replica_groups=PAIRS,
                             ins=[cc_ci.opt()], outs=[cc_co.opt()]).then_inc(_sc, 1)
        g.wait_ge(_sc, _vc)
        return g.dma_start(out=carr[:], in_=cc_co[:])
    p.step("gpsimd", ccarry, waits=[("ve_cc", None)], incs=[("cg_o", 16)])

    def cpm_mk(v):
        return v.tensor_scalar(cpm[:], carr[:], pm_s[:, 0:1], None, ALU.mult)
    p.step("vector", cpm_mk, waits=[("cg_o", None)], incs=[("ve_cpm", 1)])

    if stop == "loopA":
        return _finish([("sy_u", None), ("cg_o", None), ("ve_cpm", None),
                        ("g1out", None)])

    p.rel("wnat", "wint", "wT_st")

    if stop == "quant":
        return _finish([("qgst", None), ("qust", None), ("qdst", None),
                        ("cg_o", None), ("ve_cpm", None), ("g1out", None)])

    scrE = p.sbx("scrE", [128, NT])
    u_rb = [scrA, scrE]
    gate_sb, h_sb, arep = scrC, scrD, scrB
    wg3 = wgT.rearrange("(k p) h -> p k h", p=128)
    for g in range(16):
        buf = (g % 2) * 8 * 128

        def ldwg(gp, g=g, buf=buf):
            return gp.dma_start(out=wi_sl[:, buf:buf + 8 * 128],
                                in_=wg3[:, :, g * 128:(g + 1) * 128])
        p.step("gpsimd", ldwg,
               waits=[("mm_g", 4 * (g - 1))] if g >= 2 else [("mm_u", 64)],
               incs=[("ld_wg", 16)])

        def ldu(sy, g=g):
            return sy.dma_start(out=u_rb[g % 2][:],
                                in_=u_spill[g * 128:(g + 1) * 128, :])
        w = [("sy_u", 16 * 16), ("ve_ca", 16)]
        if g >= 2:
            w.append(("ve_scan", g - 1))
        p.step("sync", ldu, waits=w, incs=[("sy_ur", 16)])

        for n in range(4):
            def gmm(te, g=g, n=n, buf=buf):
                for k in range(NCH):
                    inst = te.matmul(pb[n][:],
                                     wi_sl[:, buf + k * 128:buf + (k + 1) * 128],
                                     xn_bf[:, k * NT + n * 512:k * NT + (n + 1) * 512],
                                     start=(k == 0), stop=(k == NCH - 1))
                return inst
            w = [("ld_wg", 16 * (g + 1))]
            if g >= 1:
                w.append(("ev_g", 4 * (g - 1) + n + 1))
            else:
                w.append(("ev_u", 64))
            p.step("tensor", gmm, waits=w, incs=[("mm_g", 1)])

            def gev(sc, g=g, n=n):
                return sc.activation(gate_sb[:, n * 512:(n + 1) * 512], pb[n][:],
                                     AF.Sigmoid, bias=bg_s[:, g:g + 1])
            w = [("mm_g", None)]
            if g >= 1 and n == 0:
                w.append(("ve_hg", g))
            if g == 0 and n == 0:
                w.append(("ve_ca", 16))
            p.step("scalar", gev, waits=w, incs=[("ev_g", 1)])

        def mkarep(v, g=g):
            v.memset(arep[:], 1.0)
            return v.tensor_scalar(arep[:], arep[:], a_s[:, g:g + 1], None, ALU.mult)
        w = [("acols", None)]
        if g == 0:
            w.append(("ve_ca", 16))
        if g >= 1:
            w.append(("ve_scan", g))
        p.step("vector", mkarep, waits=w, incs=[("ve_ar", 1)])

        def scan(v, g=g):
            return v.tensor_tensor_scan(h_sb[:], arep[:], u_rb[g % 2][:],
                                        cpm[:, g:g + 1], ALU.mult, ALU.add)
        w = [("ve_ar", None), ("sy_ur", 16 * (g + 1)), ("ve_cpm", 1)]
        if g >= 1:
            w.append(("ve_hg", g))
        p.step("vector", scan, waits=w, incs=[("ve_scan", 1)])

        def hg(v, g=g):
            return v.tensor_mul(hg_sb[:], h_sb[:], gate_sb[:])
        w = [("ve_scan", None), ("ev_g", 4 * g + 4)]
        if g >= 1:
            w.append(("sy_hg", 16 * g))
        p.step("vector", hg, waits=w, incs=[("ve_hg", 1)])

        def sthg(sy, g=g):
            return sy.dma_start(out=hg_spill[g * 128:(g + 1) * 128, :], in_=hg_sb[:])
        p.step("sync", sthg, waits=[("ve_hg", None)], incs=[("sy_hg", 16)])

    if stop == "loopB":
        return _finish([("sy_hg", None), ("qdst", None), ("g1out", None)])

    sigx, sxv, rinvx = onecol[0:1, 0:1], onecol[0:1, 1:2], onecol[0:1, 2:3]

    def xscal(sc):
        return sc.activation(sigx, clip_s[0:1, 0:1], AF.Sigmoid)
    p.step("scalar", xscal, waits=[("sm_ok", None)], incs=[("sc_sig", 1)])

    def xscal2(v):
        v.tensor_mul(sxv, sigx, g1)
        v.tensor_scalar(sxv, sxv, 1.0 / 127.0, None, ALU.mult)
        return v.reciprocal(rinvx, sxv)
    p.step("vector", xscal2, waits=[("sc_sig", None), ("g1out", 16)],
           incs=[("ve_sx", 1)])

    psR2 = p.psx("psR2", [128, 512])
    psC = psR2[:, 0:1]

    def reps(te):
        return te.matmul(psC, p.ones_r[:], sxv, start=True, stop=True)
    p.step("tensor", reps, waits=[("ve_sx", None), ("ln0repev", 8), ("ln0gm", 1)],
           incs=[("mm_sx", 1)])

    def repsev(sc):
        return sc.copy(sx_col[:, 0:1], psC)
    p.step("scalar", repsev, waits=[("mm_sx", None)], incs=[("ev_sx", 1)])

    ab2 = p.sbx("ab2", [128, 2 * NT])
    A2_r = ln["mu_r"]
    B2_r = ln["ms_r"]

    def mkab2(v):
        v.tensor_scalar(A2_r, ln["A_r"], rinvx[0:1, 0:1], None, ALU.mult)
        return v.tensor_scalar(B2_r, ln["B_r"], rinvx[0:1, 0:1], None, ALU.mult)
    p.step("vector", mkab2, waits=[("ve_sx", None)], incs=[("ve_ab2", 1)])

    for j in range(2):
        for n in range(4):
            def rep2(te, j=j, n=n):
                row = A2_r if j == 0 else B2_r
                return te.matmul(psR2[:], p.ones_r[:],
                                 row[0:1, n * 512:(n + 1) * 512],
                                 start=True, stop=True)
            p.step("tensor", rep2,
                   waits=[("ve_ab2", None), ("ev_ab2", None), ("ln0repev", 8),
                          ("qdst", p.c("qdst"))],
                   incs=[("mm_ab2", 1)])

            def rep2ev(sc, j=j, n=n):
                return sc.copy(ab2[:, j * NT + n * 512:j * NT + (n + 1) * 512], psR2[:])
            p.step("scalar", rep2ev, waits=[("mm_ab2", None)], incs=[("ev_ab2", 1)])

    p.rel("xn_bf")
    xq_bf = p.sbx("xq_bf", [128, NCH * NT], BF16)
    qtmp = scrB
    for k in range(NCH):
        def qx(v, k=k):
            v.tensor_mul(qtmp[:], x_sb[:, k * NT:(k + 1) * NT], ab2[:, 0:NT])
            v.tensor_sub(qtmp[:], qtmp[:], ab2[:, NT:2 * NT])
            v.tensor_scalar(qtmp[:], qtmp[:], C22, -C22, ALU.min, ALU.max)
            v.tensor_scalar(qtmp[:], qtmp[:], MAGIC, MAGIC, ALU.add, ALU.subtract)
            return v.tensor_scalar(xq_bf[:, k * NT:(k + 1) * NT], qtmp[:],
                                   -128.0, 127.0, ALU.max, ALU.min)
        w = [("ev_ab2", None)]
        if k == 0:
            w += [("ve_ar", 16), ("ve_scan", 16), ("mm_g", 64),
                  ("qdst", p.c("qdst"))]
        p.step("vector", qx, waits=w, incs=[("ve_xq", 1)])

    def mksxr(v):
        v.tensor_scalar(sxr_g[:], scol_g[:], sx_col[:, 0:1], None, ALU.mult)
        return v.tensor_scalar(sxr_u[:], scol_u[:], sx_col[:, 0:1], None, ALU.mult)
    p.step("vector", mksxr, waits=[("ev_sx", None), ("qgvq", 22), ("quvq", 22)],
           incs=[("ve_sxr", 1)])
    p.rel("wi_sl", "ab2", "ln0rows", "ln0sumx", "ln0sumsq")

    if stop == "xq":
        return _finish([("ve_xq", None), ("ve_sxr", None), ("sy_hg", None)])

    wg_i = p.sbx("wg_i", [128, 2 * 8 * 128], BF16)
    wu_i = p.sbx("wu_i", [128, 2 * 8 * 128], BF16)
    gsil, upv = scrA, scrB
    h2f = [scrC, scrD]
    wgq3 = wgq_t.rearrange("(k p) m -> p k m", p=128)
    wuq3 = wuq_t.rearrange("(k p) m -> p k m", p=128)
    for i in range(22):
        buf = (i % 2) * 8 * 128

        def ldwgi(sy, i=i, buf=buf):
            return [sy.dma_start(out=wg_i[:, buf:buf + 8 * 128],
                                 in_=wgq3[:, :, i * 128:(i + 1) * 128]),
                    sy.dma_start(out=wu_i[:, buf:buf + 8 * 128],
                                 in_=wuq3[:, :, i * 128:(i + 1) * 128])]
        w = [("qgst", 16 * 8 * (i + 1)), ("qust", 16 * 8 * (i + 1))]
        if i == 0:
            w.append(("ve_xq", 8))
        if i >= 2:
            w.append(("mm_h2", 8 * (i - 1)))
        p.step("sync", ldwgi, waits=w, incs=[("ld_wq", 32)])

        for n in range(4):
            def h2mm(te, i=i, n=n, buf=buf):
                for k in range(NCH):
                    te.matmul(pb[n % 2][:], wg_i[:, buf + k * 128:buf + (k + 1) * 128],
                              xq_bf[:, k * NT + n * 512:k * NT + (n + 1) * 512],
                              start=(k == 0), stop=(k == NCH - 1))
                for k in range(NCH):
                    inst = te.matmul(pb[2 + n % 2][:],
                                     wu_i[:, buf + k * 128:buf + (k + 1) * 128],
                                     xq_bf[:, k * NT + n * 512:k * NT + (n + 1) * 512],
                                     start=(k == 0), stop=(k == NCH - 1))
                return inst
            w = [("ld_wq", 32 * (i + 1)), ("ve_xq", 8)]
            prev = (i * 4 + n) - 2
            if prev >= 0:
                w.append(("ev_h2", 2 * prev + 2))
            elif i == 0 and n == 0:
                w.append(("ev_g", 64))
            p.step("tensor", h2mm, waits=w, incs=[("mm_h2", 2)])

            def h2ev(sc, i=i, n=n):
                sc.activation(gsil[:, n * 512:(n + 1) * 512], pb[n % 2][:],
                              AF.Silu, scale=sxr_g[:, i:i + 1])
                return sc.activation(upv[:, n * 512:(n + 1) * 512], pb[2 + n % 2][:],
                                     AF.Identity, scale=sxr_u[:, i:i + 1])
            w = [("mm_h2", None), ("ve_sxr", 1)]
            if i == 0 and n == 0:
                w.append(("ve_scan", 16))
            if n == 0 and i >= 1:
                w.append(("ve_h2", i))
            p.step("scalar", h2ev, waits=w, incs=[("ev_h2", 2)])

        def h2mul(v, i=i):
            hb = h2f[i % 2]
            v.tensor_mul(hb[:], gsil[:], upv[:])
            v.tensor_reduce(rmt[:], hb[:], AX.X, ALU.max, apply_absolute_value=True)
            if i == 0:
                return v.tensor_copy(rm2[:], rmt[:])
            return v.tensor_max(rm2[:], rm2[:], rmt[:])
        w = [("ev_h2", None)]
        if i == 0:
            w.append(("ve_hg", 16))
        if i >= 2:
            w.append(("sy_h2", 16 * (i - 1)))
        p.step("vector", h2mul, waits=w, incs=[("ve_h2", 1)])

        def sth2(sy, i=i):
            return sy.dma_start(out=h2_spill[i * 128:(i + 1) * 128, :], in_=h2f[i % 2][:])
        p.step("sync", sth2, waits=[("ve_h2", None)], incs=[("sy_h2", 16)])

    if stop == "h2":
        return _finish([("sy_h2", None)])

    wo_dd = p.sbx("wo_dd", [128, 2 * 16 * 128], BF16)
    hg_st = p.sbx("hg_st", [128, 16 * 512], BF16)
    wo3 = woT.rearrange("(g p) m -> p g m", p=128)
    hg3 = hg_spill.rearrange("(g p) n -> g p n", p=128)
    for n in range(4):
        def ldhg(sy, n=n):
            return [sy.dma_start(out=hg_st[:, g * 512:(g + 1) * 512],
                                 in_=hg3[g][:, n * 512:(n + 1) * 512])
                    for g in range(16)]
        w = [("sy_hg", 16 * 16)]
        if n == 0:
            w += [("ve_xq", 8), ("qdst", p.c("qdst")), ("mm_g", 64)]
        else:
            w.append(("mm_y", 8 * n))
        p.step("sync", ldhg, waits=w, incs=[("sy_hgl", 16 * 16)])

        for dd in range(8):
            idx = n * 8 + dd
            wbuf = (idx % 2) * 16 * 128

            def ldwo(gp, dd=dd, wbuf=wbuf):
                return gp.dma_start(out=wo_dd[:, wbuf:wbuf + 16 * 128],
                                    in_=wo3[:, :, dd * 128:(dd + 1) * 128])
            w = []
            if idx == 0:
                w = [("mm_g", 64), ("ve_xq", 8), ("qdst", p.c("qdst")),
                     ("ev_ab2", 8), ("ln0xn", 8)]
            elif idx >= 2:
                w = [("mm_y", idx - 1)]
            p.step("gpsimd", ldwo, waits=w, incs=[("ld_wo", 16)])

            def ymm(te, n=n, dd=dd, wbuf=wbuf):
                for g in range(16):
                    inst = te.matmul(pb[dd % 4][:],
                                     wo_dd[:, wbuf + g * 128:wbuf + (g + 1) * 128],
                                     hg_st[:, g * 512:(g + 1) * 512],
                                     start=(g == 0), stop=(g == 15))
                return inst
            w = [("sy_hgl", 16 * 16 * (n + 1)), ("ld_wo", 16 * (idx + 1))]
            prev = idx - 4
            if prev >= 0:
                w.append(("ev_y", prev + 1))
            elif idx == 0:
                w.append(("ev_h2", 176))
            p.step("tensor", ymm, waits=w, incs=[("mm_y", 1)])

            def yev(v, n=n, dd=dd):
                return v.scalar_tensor_tensor(
                    x_sb[:, dd * NT + n * 512:dd * NT + (n + 1) * 512],
                    pb[dd % 4][:], bo_s[:, dd:dd + 1],
                    x_sb[:, dd * NT + n * 512:dd * NT + (n + 1) * 512],
                    ALU.add, ALU.add)
            p.step("vector", yev, waits=[("mm_y", None)], incs=[("ev_y", 1)])

    if stop == "loopC":
        return _finish([("ev_y", None), ("sy_h2", None)])

    psT2 = p.psx("psT2", [1, 128])

    def trm2(te):
        return te.transpose(psT2[:], rm2[:], p.ident_f[:])
    p.step("tensor", trm2, waits=[("ve_h2", 22), ("ln0gm", 1)], incs=[("tr2", 1)])

    def redm2(v):
        return v.tensor_reduce(gm2[:], psT2[:], AX.X, ALU.max)
    p.step("vector", redm2, waits=[("tr2", None)], incs=[("gm2s", 1)])

    g2 = onecol[0:1, 7:8]
    emit_armax_scalar(p, "g2", gm2, "gm2s", g2)

    sigd, shv, rinvh = onecol[0:1, 4:5], onecol[0:1, 5:6], onecol[0:1, 6:7]

    def hscal(sc):
        return sc.activation(sigd, clip_s[0:1, 1:2], AF.Sigmoid)
    p.step("scalar", hscal, waits=[("sm_ok", None)], incs=[("sc_sigd", 1)])

    def hscal2(v):
        v.tensor_mul(shv, sigd, g2)
        v.tensor_scalar(shv, shv, 1.0 / 127.0, None, ALU.mult)
        return v.reciprocal(rinvh, shv)
    p.step("vector", hscal2, waits=[("sc_sigd", None), ("g2out", 16)],
           incs=[("ve_sh", 1)])

    def reph(te):
        return te.matmul(psR2[:, 0:2], p.ones_r[:],
                         onecol[0:1, 5:7], start=True, stop=True)
    p.step("tensor", reph, waits=[("ve_sh", None), ("ev_ab2", 8), ("ev_sx", 1)],
           incs=[("mm_sh", 1)])

    def rephev(sc):
        return sc.copy(sh_col[:, 0:2], psR2[:, 0:2])
    p.step("scalar", rephev, waits=[("mm_sh", None)], incs=[("ev_sh", 1)])

    def mksxrd(v):
        return v.tensor_scalar(sxr_d[:], scol_d[:], sh_col[:, 0:1], None, ALU.mult)
    p.step("vector", mksxrd, waits=[("ev_sh", None), ("qdvq", 8)], incs=[("ve_sxrd", 1)])

    if stop == "scales":
        return _finish([("ve_sxrd", None), ("ev_y", None), ("sy_h2", None)])

    p.rel("wg_i", "wu_i", "scrA", "scrB", "scrC", "scrD", "scrE", "hg_sb",
          "wo_dd", "hg_st")
    h2i = p.sbx("h2i", [128, 22 * 512], BF16)
    h2r = p.sbx("h2r", [128, 2 * 512])
    wd_sb = p.sbx("wd_sb", [128, 22 * 1024], BF16)
    wdq3 = wdq_t.rearrange("(k p) m -> k p m", p=128)

    def ldwd_all(gp):
        return [gp.dma_start(out=wd_sb[:, ki * 1024:(ki + 1) * 1024], in_=wdq3[ki])
                for ki in range(22)]
    p.step("gpsimd", ldwd_all,
           waits=[("qdst", p.c("qdst")), ("mm_y", 32), ("sy_h2", 16 * 22),
                  ("sy_hg", 16 * 16)],
           incs=[("ld_wd", 16 * 22)])

    for n in range(1 if stop == "down1" else 4):
        for ki in range(22):
            def ldh2(sy, n=n, ki=ki):
                return sy.dma_start(out=h2r[:, (ki % 2) * 512:(ki % 2) * 512 + 512],
                                    in_=h2_spill[ki * 128:(ki + 1) * 128,
                                                 n * 512:(n + 1) * 512])
            w = [("sy_h2", 16 * 22)]
            if n == 0 and ki == 0:
                w.append(("mm_y", 32))
            if n * 22 + ki >= 2:
                w.append(("ve_q2", n * 22 + ki - 1))
            p.step("sync", ldh2, waits=w, incs=[("sy_h2r", 16)])

            def qh2(v, n=n, ki=ki):
                src = h2r[:, (ki % 2) * 512:(ki % 2) * 512 + 512]
                v.tensor_scalar(src, src, sh_col[:, 1:2], None, ALU.mult)
                v.tensor_scalar(src, src, C22, -C22, ALU.min, ALU.max)
                v.tensor_scalar(src, src, MAGIC, MAGIC, ALU.add, ALU.subtract)
                return v.tensor_scalar(h2i[:, ki * 512:(ki + 1) * 512], src,
                                       -128.0, 127.0, ALU.max, ALU.min)
            w = [("sy_h2r", 16 * (n * 22 + ki + 1)), ("ev_sh", None)]
            if n > 0 and ki == 0:
                w.append(("mm_d", 8 * n))
            p.step("vector", qh2, waits=w, incs=[("ve_q2", 1)])

        for dd in range(8):
            def dmm(te, n=n, dd=dd):
                for ki in range(22):
                    inst = te.matmul(pb[dd % 4][:],
                                     wd_sb[:, ki * 1024 + dd * 128:ki * 1024 + (dd + 1) * 128],
                                     h2i[:, ki * 512:(ki + 1) * 512],
                                     start=(ki == 0), stop=(ki == 21))
                return inst
            w = [("ve_q2", 22 * (n + 1)), ("ld_wd", None)]
            prev = n * 8 + dd - 4
            if prev >= 0:
                w.append(("ev_d", prev + 1))
            elif n == 0 and dd < 4:
                w.append(("ev_y", 32))
            p.step("tensor", dmm, waits=w, incs=[("mm_d", 1)])

            def dev(v, n=n, dd=dd):
                return v.scalar_tensor_tensor(
                    x_sb[:, dd * NT + n * 512:dd * NT + (n + 1) * 512],
                    pb[dd % 4][:], sxr_d[:, dd:dd + 1],
                    x_sb[:, dd * NT + n * 512:dd * NT + (n + 1) * 512],
                    ALU.mult, ALU.add)
            p.step("vector", dev, waits=[("mm_d", None), ("ve_sxrd", 1), ("ev_y", 32)],
                   incs=[("ev_d", 1)])

    if stop == "down1":
        return _finish([("ev_d", None), ("ev_y", None)])

    if fx is not None:
        return [("ev_d", None), ("ev_y", None)]

    xo3 = xo.rearrange("(k p) n -> k p n", p=128)

    def store(sy):
        return [sy.dma_start(out=xo3[k], in_=x_sb[:, k * NT:(k + 1) * NT])
                for k in range(NCH)]
    p.step("sync", store, waits=[("ev_d", None), ("ev_y", None)],
           incs=[("st", 16 * NCH)])
    p.step("sync", lambda e: None, waits=[("st", None)])

    p.emit()
    return p




def build_L1(stop="full", fx=None):
    p = Prog() if fx is None else fx["p"]
    nc = p.nc
    inp = {}

    def ein(name, shape, dtype=F32):
        inp[name] = nc.dram_tensor(p.px + name, shape, dtype,
                                   kind="ExternalInput").ap()
        return inp[name]

    if fx is None:
        x_in = ein("x", [D, NT])
    ein("pm_col", [128, 1])
    ein("pmc_col", [128, 1])
    wwT = ein("wwT", [D, D])
    wrT = ein("wrT", [D, D])
    ein("dec_la", [1, 2])
    wg16T = ein("wg16T", [D, 16])
    w1 = ein("w1", [16, D, D])
    w2 = ein("w2", [16, 512, D])
    ws1 = ein("ws1", [D, 2 * D])
    ws2 = ein("ws2", [D, D])
    xo = nc.dram_tensor("xo", [D, NT], F32,
                        kind="ExternalOutput").ap() if fx is None else None

    cc1i = p.dram("cc1i", [128, 8])
    cc1o = p.dram("cc1o", [128, 8])
    cc2i = p.dram("cc2i", [D, D])
    cc2o = p.dram("cc2o", [D, D])

    if fx is None:
        x_sb = p.sb("x_sb", [128, NCH * NT])
        xn_bf = p.sb("xn_bf", [128, NCH * NT], BF16)
    else:
        x_sb, xn_bf = fx["x_sb"], fx["xn_bf"]
    pm_s = p.sb("pm_s", [128, 2])
    dl_s = p.sb("dl_s", [1, 8])
    cols = p.sb("cols", [128, 8])
    mask0 = p.sb("mask0", [128, 128])
    gcol = p.sb("gcol", [128, 2])
    xl = p.sb("xl", [128, 8])
    wk0 = p.sb("wk0", [128, 8])

    pb = [p.psx(f"pb{i}", [128, 512]) for i in range(4)]

    if fx is None:
        emit_common(p)
        x3 = x_in.rearrange("(k p) n -> k p n", p=128)

        def ldx(sy):
            return [sy.dma_start(out=x_sb[:, k * NT:(k + 1) * NT], in_=x3[k])
                    for k in range(NCH)]
        p.step("sync", ldx, incs=[("x_ok", 16 * NCH)])

    def ldsmall(sy):
        return [sy.dma_start(out=dl_s[0:1, 0:2], in_=inp["dec_la"][:]),
                sy.dma_start(out=pm_s[:, 0:1], in_=inp["pm_col"][:]),
                sy.dma_start(out=pm_s[:, 1:2], in_=inp["pmc_col"][:])]
    p.step("sync", ldsmall, waits=[("common", 1)], incs=[("sm_ok", 16 * 3)])

    def mkmask(v):
        return v.tensor_scalar(mask0[:], p.iota_f[:], p.iota_p[:], None, ALU.is_gt)
    p.step("vector", mkmask, waits=[("common", 1)], incs=[("msk", 1)])

    ln = emit_ln(p, "ln1", x_sb, xn_bf, "x_ok")
    p.rel(*ln["scoped"])
    p.rel(*ln["psums"])
    p.rel("ln1rows", "ln1sumx", "ln1sumsq")
    tsA = p.psx("tsA", [128, 128], BF16)
    tsB = p.psx("tsB", [128, 128], BF16)
    tsF = p.psx("tsF", [128, 128], F32)
    psR2 = p.psx("psR2", [128, 512], F32)

    gam, lng, alp = dl_s[0:1, 2:3], dl_s[0:1, 3:4], dl_s[0:1, 4:5]

    def sc1(sc):
        sc.activation(gam, dl_s[0:1, 0:1], AF.Sigmoid)
        sc.activation(lng, gam, AF.Ln)
        return sc.activation(alp, dl_s[0:1, 1:2], AF.Exp)
    p.step("scalar", sc1, waits=[("sm_ok", None)], incs=[("sc1", 1)])

    def repsc(te):
        return te.matmul(psR2[:, 0:2], p.ones_r[:], dl_s[0:1, 3:5],
                         start=True, stop=True)
    p.step("tensor", repsc,
           waits=[("sc1", None), ("common", 1), ("ln1repev", 8), ("ln1sev", 4)],
           incs=[("mmsc", 1)])

    def repev(sc):
        return sc.copy(cols[:, 0:2], psR2[:, 0:2])
    p.step("scalar", repev, waits=[("mmsc", None)], incs=[("evsc", 1)])

    def negl(v):
        return v.tensor_scalar(cols[:, 2:3], cols[:, 0:1], -1.0, None, ALU.mult)
    p.step("vector", negl, waits=[("evsc", None)], incs=[("nlng", 1)])

    def mkxl(v):
        for k in range(NCH):
            v.tensor_copy(xl[:, k:k + 1], xn_bf[:, k * NT + NT - 1:k * NT + NT])
        return v.tensor_scalar(xl[:], xl[:], pm_s[:, 1:2], None, ALU.mult)
    p.step("vector", mkxl, waits=[("ln1xn", None), ("sm_ok", None)], incs=[("vxl", 1)])

    _sd, _vd = p.sem("c1d"), p.av("c1d", 16)
    _sc, _vc = p.sem("c1c"), p.av("c1c", 1)
    p.mark("c1d", 16)
    p.mark("c1c", 1)
    def ccxl(g, _sd=_sd, _vd=_vd, _sc=_sc, _vc=_vc):
        g.dma_start(out=cc1i[:], in_=xl[:]).then_inc(_sd, 16)
        g.wait_ge(_sd, _vd)
        g.collective_compute("AllReduce", ALU.add, replica_groups=PAIRS,
                             ins=[cc1i.opt()], outs=[cc1o.opt()]).then_inc(_sc, 1)
        g.wait_ge(_sc, _vc)
        return g.dma_start(out=xl[:], in_=cc1o[:])
    p.step("gpsimd", ccxl, waits=[("vxl", None)], incs=[("c1o", 16)])

    def mkwk0(v):
        return v.tensor_scalar(wk0[:], xl[:], pm_s[:, 0:1], None, ALU.mult)
    p.step("vector", mkwk0, waits=[("c1o", None)], incs=[("vwk0", 1)])

    wk0t = p.sb("wk0t", [128, NCH * 128], BF16)

    def mkwk(v):
        for k in range(NCH):
            v.tensor_copy(wk0t[:, k * 128 + 1:(k + 1) * 128],
                          xn_bf[:, k * NT:k * NT + 127])
            inst = v.tensor_copy(wk0t[:, k * 128:k * 128 + 1], wk0[:, k:k + 1])
        return inst
    p.step("vector", mkwk, waits=[("vwk0", None)], incs=[("vwk", 1)])

    def wk_sl(sc, k):
        if sc == 0:
            return wk0t[:, k * 128:(k + 1) * 128]
        return xn_bf[:, k * NT + sc * 128 - 1:k * NT + (sc + 1) * 128 - 1]

    ww_sb = p.sbx("ww_sb", [128, NCH * D], BF16)
    ww3 = wwT.rearrange("(k p) m -> p k m", p=128)

    def ldww(g):
        return g.dma_start(out=ww_sb[:], in_=ww3[:, :, :])
    p.step("gpsimd", ldww, waits=[("common", 1)], incs=[("ldww", 16)])

    v_tm = p.sbx("v_tm", [128, 16 * D], BF16)
    for tc in range(16):
        for h in range(2):
            idx = tc * 2 + h

            def vmm(te, tc=tc, h=h):
                for k in range(NCH):
                    inst = te.matmul(pb[h][:],
                                     xn_bf[:, k * NT + tc * 128:k * NT + (tc + 1) * 128],
                                     ww_sb[:, k * D + h * 512:k * D + (h + 1) * 512],
                                     start=(k == 0), stop=(k == NCH - 1))
                return inst
            w = [("ldww", 16), ("ln1xn", None)]
            if idx >= 2:
                w.append(("vev", idx - 1))
            p.step("tensor", vmm, waits=w, incs=[("vmm", 1)])

            def vev(sc, tc=tc, h=h):
                return sc.copy(v_tm[:, tc * D + h * 512:tc * D + (h + 1) * 512],
                               pb[h][:])
            p.step("scalar", vev, waits=[("vmm", None)], incs=[("vev", 1)])

    p.rel("ww_sb")
    wk_tm2 = [p.sbx("wk_tmA", [128, 8 * D], BF16), p.sbx("wk_tmB", [128, 8 * D], BF16)]

    def wk_tm_sl(sc, a, b):
        return wk_tm2[sc // 8][:, (sc % 8) * D + a:(sc % 8) * D + b]
    for sc_ in range(16):
        for kc in range(NCH):
            gidx = sc_ * NCH + kc

            def wtr(te, sc_=sc_, kc=kc, gidx=gidx):
                return te.transpose(
                    (tsA if gidx % 2 == 0 else tsB)[:],
                    wk_sl(sc_, kc), p.ident_bf[:])
            w = [("vwk", 1)]
            if gidx >= 2:
                w.append(("wtev", gidx - 1))
            p.step("tensor", wtr, waits=w, incs=[("wtr", 1)])

            def wtev(v, sc_=sc_, kc=kc, gidx=gidx):
                return v.tensor_copy(wk_tm_sl(sc_, kc * 128, (kc + 1) * 128),
                                     (tsA if gidx % 2 == 0 else tsB)[:])
            p.step("vector", wtev, waits=[("wtr", None)], incs=[("wtev", 1)])

    for sc_ in range(16):
        def gscale(v, sc_=sc_):
            v.tensor_scalar(gcol[:, 0:1], cols[:, 0:1], float(2047 - sc_ * 128),
                            None, ALU.mult)
            return v.scalar_tensor_tensor(gcol[:, 1:2], p.iota_p[:], cols[:, 2:3],
                                          gcol[:, 0:1], ALU.mult, ALU.add)
        p.step("vector", gscale, waits=[("nlng", 1), ("wtev", None)],
               incs=[("vgc", 1)])

        def gexp(sc, sc_=sc_):
            return sc.activation(gcol[:, 1:2], gcol[:, 1:2], AF.Exp)
        p.step("scalar", gexp, waits=[("vgc", None)], incs=[("sgc", 1)])

        def wkscale(v, sc_=sc_):
            return v.tensor_scalar(wk_tm_sl(sc_, 0, D), wk_tm_sl(sc_, 0, D),
                                   gcol[:, 1:2], None, ALU.mult)
        p.step("vector", wkscale, waits=[("sgc", None)], incs=[("vgs", 1)])

    wt_bf = p.sbx("wt_bf", [128, NCH * D], BF16)
    for jc in range(NCH):
        for h in range(2):
            idx = jc * 2 + h

            def wbmm(te, jc=jc, h=h):
                for s_ in range(16):
                    inst = te.matmul(pb[2 + h][:],
                                     wk_tm_sl(s_, jc * 128, (jc + 1) * 128),
                                     v_tm[:, s_ * D + h * 512:s_ * D + (h + 1) * 512],
                                     start=(s_ == 0), stop=(s_ == 15))
                return inst
            w = [("vgs", 16), ("vev", 32)]
            if idx >= 2:
                w.append(("wbev", idx - 1))
            p.step("tensor", wbmm, waits=w, incs=[("wbmm", 1)])

            def wbev(sc, jc=jc, h=h):
                return sc.copy(wt_bf[:, jc * D + h * 512:jc * D + (h + 1) * 512],
                               pb[2 + h][:])
            p.step("scalar", wbev, waits=[("wbmm", None)], incs=[("wbev", 1)])

    def wtmask(v):
        for jc in range(NCH):
            inst = v.tensor_scalar(wt_bf[:, jc * D:(jc + 1) * D],
                                   wt_bf[:, jc * D:(jc + 1) * D],
                                   pm_s[:, 1:2], None, ALU.mult)
        return inst
    p.step("vector", wtmask, waits=[("wbev", None)], incs=[("wtm", 1)])

    cc2i3 = cc2i.rearrange("(k p) m -> k p m", p=128)
    cc2o3 = cc2o.rearrange("(k p) m -> k p m", p=128)

    _sd, _vd = p.sem("c2d"), p.av("c2d", 16 * NCH)
    _sc, _vc = p.sem("c2c"), p.av("c2c", 1)
    p.mark("c2d", 16 * NCH)
    p.mark("c2c", 1)
    def ccwt(g, _sd=_sd, _vd=_vd, _sc=_sc, _vc=_vc):
        for jc in range(NCH):
            g.dma_start(out=cc2i3[jc], in_=wt_bf[:, jc * D:(jc + 1) * D]).then_inc(
                _sd, 16)
        g.wait_ge(_sd, _vd)
        g.collective_compute("AllReduce", ALU.add, replica_groups=PAIRS,
                             ins=[cc2i.opt()], outs=[cc2o.opt()]).then_inc(_sc, 1)
        g.wait_ge(_sc, _vc)
        return [g.dma_start(out=wt_bf[:, jc * D:(jc + 1) * D], in_=cc2o3[jc])
                for jc in range(NCH)]
    p.step("gpsimd", ccwt, waits=[("wtm", None)], incs=[("c2o", 16 * NCH)])

    if stop == "attn1":
        return _mk_finish(p, x_sb, xo)([("c2o", None), ("vev", 32)])

    p.rel("wk_tmA", "wk_tmB")
    reads2 = [p.sbx("readsA", [128, 8 * D], BF16), p.sbx("readsB", [128, 8 * D], BF16)]

    def reads_sl(tc, a, b):
        return reads2[tc // 8][:, (tc % 8) * D + a:(tc % 8) * D + b]
    dec_f = p.sbx("dec_f", [128, 2 * 128])
    pbf = p.sbx("pbf", [128, 2 * 128], BF16)
    pair = 0
    for tc in range(16):
        for s_ in range(tc + 1):
            par = pair % 2

            def amm(te, tc=tc, s_=s_):
                for k in range(NCH):
                    inst = te.matmul(pb[0][:, 0:128],
                                     wk_sl(s_, k),
                                     xn_bf[:, k * NT + tc * 128:k * NT + (tc + 1) * 128],
                                     start=(k == 0), stop=(k == NCH - 1),
                                     skip_group_check=True)
                return inst
            w = [("vwk", 1)]
            if pair == 0:
                w.append(("vev", 32))
            if pair >= 1:
                w.append(("vp", pair))
            p.step("tensor", amm, waits=w, incs=[("amm", 1)])

            def mkdec(v, tc=tc, s_=s_, par=par):
                d0 = float(tc * 128 - 1 - s_ * 128)
                v.tensor_scalar(gcol[:, 0:1], p.iota_p[:], -1.0, d0,
                                ALU.mult, ALU.add)
                v.tensor_mul(gcol[:, 0:1], gcol[:, 0:1], cols[:, 0:1])
                dsl = dec_f[:, par * 128:(par + 1) * 128]
                v.tensor_scalar(dsl, p.iota_f[:], cols[:, 0:1], gcol[:, 0:1],
                                ALU.mult, ALU.add)
                return v.tensor_scalar(dsl, dsl, 0.0, None, ALU.min)
            w = [("nlng", 1)]
            if pair >= 2:
                w.append(("vp", pair - 1))
            p.step("vector", mkdec, waits=w, incs=[("vdc", 1)])

            def dexp(sc, par=par):
                return sc.activation(dec_f[:, par * 128:(par + 1) * 128],
                                     dec_f[:, par * 128:(par + 1) * 128], AF.Exp)
            p.step("scalar", dexp, waits=[("vdc", None)], incs=[("sdc", 1)])

            def mkp(v, tc=tc, s_=s_, par=par):
                dslice = dec_f[:, par * 128:(par + 1) * 128]
                if tc == s_:
                    v.tensor_mul(dslice, dslice, mask0[:])
                return v.tensor_tensor(pbf[:, par * 128:(par + 1) * 128],
                                       pb[0][:, 0:128], dslice, ALU.mult)
            w = [("sdc", None), ("amm", None)]
            if pair >= 2:
                w.append(("rmm", 2 * (pair - 1)))
            p.step("vector", mkp, waits=w, incs=[("vp", 1)])

            def rmm(te, tc=tc, s_=s_, par=par):
                for h in range(2):
                    inst = te.matmul(pb[1 + h][:],
                                     pbf[:, par * 128:(par + 1) * 128],
                                     v_tm[:, s_ * D + h * 512:s_ * D + (h + 1) * 512],
                                     start=(s_ == 0), stop=(s_ == tc),
                                     skip_group_check=True)
                return inst
            w = [("vp", None)]
            if tc == 0:
                w.append(("vev", 32))
            if tc >= 1 and s_ == 0:
                w.append(("rev", 2 * tc))
            p.step("tensor", rmm, waits=w, incs=[("rmm", 2)])
            pair += 1

        def rev(sc, tc=tc):
            sc.copy(reads_sl(tc, 0, 512), pb[1][:])
            return sc.copy(reads_sl(tc, 512, D), pb[2][:])
        p.step("scalar", rev, waits=[("rmm", None)], incs=[("rev", 2)])

    p.rel("v_tm")

    if stop == "quad":
        return _mk_finish(p, x_sb, xo)([("rev", None), ("c2o", None)])

    gam_b = p.sbx("gam_b", [128, NT])

    def gio(g):
        return g.iota(gam_b[:], pattern=[[1, NT]], base=0, channel_multiplier=0,
                      allow_small_or_imprecise_dtypes=True)
    p.step("gpsimd", gio, waits=[("common", 1)], incs=[("gio", 1)])

    def gexp2(sc):
        return sc.activation(gam_b[:], gam_b[:], AF.Exp, scale=cols[:, 0:1])
    p.step("scalar", gexp2, waits=[("gio", None), ("evsc", 1)], incs=[("sg2", 1)])

    def gpm(v):
        return v.tensor_scalar(gam_b[:], gam_b[:], pm_s[:, 0:1], None, ALU.mult)
    p.step("vector", gpm, waits=[("sg2", None)], incs=[("vg2", 1)])

    rfm2 = [p.sbx("rfmA", [128, 4 * NT], BF16), p.sbx("rfmB", [128, 4 * NT], BF16)]

    def rfm_sl(ec, a, b):
        return rfm2[ec // 4][:, (ec % 4) * NT + a:(ec % 4) * NT + b]
    for ic in range(NCH):
        for n in range(4):
            idx = ic * 4 + n

            def imm(te, ic=ic, n=n):
                for jc in range(NCH):
                    inst = te.matmul(pb[3][:],
                                     wt_bf[:, jc * D + ic * 128:jc * D + (ic + 1) * 128],
                                     xn_bf[:, jc * NT + n * 512:jc * NT + (n + 1) * 512],
                                     start=(jc == 0), stop=(jc == NCH - 1))
                return inst
            w = [("c2o", None)]
            if idx >= 1:
                w.append(("iev", idx))
            p.step("tensor", imm, waits=w, incs=[("imm", 1)])

            def iev(v, ic=ic, n=n):
                return v.tensor_tensor(rfm_sl(ic, n * 512, (n + 1) * 512),
                                       pb[3][:], gam_b[:, n * 512:(n + 1) * 512],
                                       ALU.mult)
            p.step("vector", iev, waits=[("imm", None), ("vg2", 1)], incs=[("iev", 1)])

    for tc in range(16):
        for ec in range(NCH):
            gidx = tc * NCH + ec

            def rtr(te, tc=tc, ec=ec, gidx=gidx):
                return te.transpose((tsA if gidx % 2 == 0 else tsB)[:],
                                    reads_sl(tc, ec * 128, (ec + 1) * 128),
                                    p.ident_bf[:])
            w = [("rev", 32), ("wtev", 128)]
            if gidx >= 2:
                w.append(("rtev", gidx - 1))
            p.step("tensor", rtr, waits=w, incs=[("rtr", 1)])

            def rtev(v, tc=tc, ec=ec, gidx=gidx):
                sl = rfm_sl(ec, tc * 128, (tc + 1) * 128)
                return v.tensor_tensor(sl, sl, (tsA if gidx % 2 == 0 else tsB)[:],
                                       ALU.add)
            p.step("vector", rtev, waits=[("rtr", None), ("iev", 32)],
                   incs=[("rtev", 1)])

    if stop == "dbgrfm":
        def cprfm(v):
            for k in range(NCH):
                inst = v.tensor_copy(x_sb[:, k * NT:(k + 1) * NT],
                                     rfm_sl(k, 0, NT))
            return inst
        p.step("vector", cprfm, waits=[("rtev", None), ("iev", None)],
               incs=[("dbg", 1)])
        return _mk_finish(p, x_sb, xo)([("dbg", 1)])

    p.rel("readsA", "readsB", "dec_f", "pbf")

    wr_sb = p.sbx("wr_sb", [128, NCH * D], BF16)
    wr3 = wrT.rearrange("(k p) m -> p k m", p=128)

    def ldwr(g):
        return g.dma_start(out=wr_sb[:], in_=wr3[:, :, :])
    p.step("gpsimd", ldwr, waits=[("rtr", 128)], incs=[("ldwr", 16)])

    def xadd(v):
        for k in range(NCH):
            inst = v.tensor_tensor(x_sb[:, k * NT:(k + 1) * NT],
                                   x_sb[:, k * NT:(k + 1) * NT],
                                   xn_bf[:, k * NT:(k + 1) * NT], ALU.add)
        return inst
    p.step("vector", xadd, waits=[("ln1xn", None)], incs=[("vxa", 1)])

    for dc in range(NCH):
        for n in range(4):
            idx = dc * 4 + n

            def romm(te, dc=dc, n=n, idx=idx):
                for kc in range(NCH):
                    inst = te.matmul(pb[idx % 2][:],
                                     wr_sb[:, kc * D + dc * 128:kc * D + (dc + 1) * 128],
                                     rfm_sl(kc, n * 512, (n + 1) * 512),
                                     start=(kc == 0), stop=(kc == NCH - 1))
                return inst
            w = [("ldwr", 16), ("rtev", 128)]
            if idx >= 2:
                w.append(("roev", idx - 1))
            p.step("tensor", romm, waits=w, incs=[("romm", 1)])

            def roev(v, dc=dc, n=n):
                return v.scalar_tensor_tensor(
                    x_sb[:, dc * NT + n * 512:dc * NT + (n + 1) * 512],
                    pb[(dc * 4 + n) % 2][:],
                    cols[:, 1:2],
                    x_sb[:, dc * NT + n * 512:dc * NT + (n + 1) * 512],
                    ALU.mult, ALU.add)
            p.step("vector", roev, waits=[("romm", None), ("vxa", 1)],
                   incs=[("roev", 1)])

    p.rel("rfmA", "rfmB", "wr_sb", "gam_b", "wt_bf")

    if stop == "attn":
        return _mk_finish(p, x_sb, xo)([("roev", None)])

    wg_sb = p.sb("wg_sb", [128, NCH * 16], BF16)
    wg3 = wg16T.rearrange("(k p) m -> p k m", p=128)

    def ldwg16(g):
        return g.dma_start(out=wg_sb[:], in_=wg3[:, :, :])
    p.step("gpsimd", ldwg16, waits=[("romm", 32), ("roev", 32)],
           incs=[("ldwg16", 16)])

    scFM = p.sb("scFM", [16, NT])
    for n in range(4):
        def smm(te, n=n):
            for kc in range(NCH):
                inst = te.matmul(pb[2][0:16, 0:512],
                                 wg_sb[:, kc * 16:(kc + 1) * 16],
                                 xn_bf[:, kc * NT + n * 512:kc * NT + (n + 1) * 512],
                                 start=(kc == 0), stop=(kc == NCH - 1))
            return inst
        w = [("ldwg16", 16), ("roev", 32)]
        if n >= 1:
            w.append(("sev2", n))
        p.step("tensor", smm, waits=w, incs=[("smm2", 1)])

        def sev2(sc, n=n):
            return sc.copy(scFM[0:16, n * 512:(n + 1) * 512], pb[2][0:16, :])
        p.step("scalar", sev2, waits=[("smm2", None)], incs=[("sev2", 1)])

    cw_tm = p.sb("cw_tm", [128, 16 * 16])
    rnk = p.sb("rnk", [128, 16])
    scc = p.sb("scc", [128, 4])
    for tc in range(16):
        def str_(te, tc=tc):
            return te.transpose(tsF[:, 0:16],
                                scFM[0:16, tc * 128:(tc + 1) * 128],
                                p.ident_f[0:16, 0:16])
        w = [("sev2", 4)]
        if tc >= 1:
            w.append(("stv", tc))
        p.step("tensor", str_, waits=w, incs=[("str", 1)])

        def stv(v, tc=tc):
            return v.tensor_copy(cw_tm[:, tc * 16:(tc + 1) * 16], tsF[:, 0:16])
        p.step("vector", stv, waits=[("str", None)], incs=[("stv", 1)])

        def smax(v, tc=tc):
            t16 = cw_tm[:, tc * 16:(tc + 1) * 16]
            v.tensor_reduce(scc[:, 0:1], t16, AX.X, ALU.max)
            return v.tensor_scalar(scc[:, 1:2], scc[:, 0:1], -1.0, None, ALU.mult)
        p.step("vector", smax, waits=[("stv", None)], incs=[("vsm", 1)])

        def sexp(sc, tc=tc):
            t16 = cw_tm[:, tc * 16:(tc + 1) * 16]
            return sc.activation(t16, t16, AF.Exp, bias=scc[:, 1:2])
        p.step("scalar", sexp, waits=[("vsm", None)], incs=[("ssx", 1)])

        def snorm(v, tc=tc):
            t16 = cw_tm[:, tc * 16:(tc + 1) * 16]
            v.tensor_reduce(scc[:, 2:3], t16, AX.X, ALU.add)
            v.reciprocal(scc[:, 3:4], scc[:, 2:3])
            v.tensor_scalar(t16, t16, scc[:, 3:4], None, ALU.mult)
            v.memset(rnk[:], 0.0)
            for e in range(16):
                v.scalar_tensor_tensor(rnk[:], t16, t16[:, e:e + 1], rnk[:],
                                       ALU.is_lt, ALU.add)
            v.tensor_scalar(rnk[:], rnk[:], 2.0, None, ALU.is_lt)
            return v.tensor_mul(t16, t16, rnk[:])
        p.step("vector", snorm, waits=[("ssx", None)], incs=[("vsn", 1)])

    cwFM = p.sb("cwFM", [16, NT])
    for tc in range(16):
        def ctr(te, tc=tc):
            return te.transpose(tsF[0:16, :], cw_tm[:, tc * 16:(tc + 1) * 16],
                                p.ident_f[:])
        w = [("vsn", 16)]
        if tc >= 1:
            w.append(("ctv", tc))
        p.step("tensor", ctr, waits=w, incs=[("ctr", 1)])

        def ctv(v, tc=tc):
            return v.tensor_copy(cwFM[0:16, tc * 128:(tc + 1) * 128], tsF[0:16, :])
        p.step("vector", ctv, waits=[("ctr", None)], incs=[("ctv", 1)])

    if stop == "route":
        return _mk_finish(p, x_sb, xo)([("ctv", None), ("roev", None)])

    p.rel("scFM")
    cwrow = p.sb("cwrow", [1, NT])
    cwrep = p.sb("cwrep", [128, NT])
    xfc = p.sb("xfc", [128, NCH * NT], BF16)
    he_bf = p.sb("he_bf", [128, 4 * NT], BF16)
    gs_f = p.sb("gs_f", [128, 512])
    wsl1 = p.sb("wsl1", [128, 2 * NCH * 128], BF16)
    wsl2 = p.sb("wsl2", [128, 2 * 4 * 128], BF16)
    w13 = w1.rearrange("e (k p) m -> e p k m", p=128)
    w23 = w2.rearrange("e (k p) m -> e p k m", p=128)

    for e in range(16):
        def cwr(sy, e=e):
            return sy.dma_start(out=cwrow[:], in_=cwFM[e:e + 1, :])
        w = [("ctv", 16)]
        if e >= 1:
            w.append(("vxf", e * NCH))
        p.step("sync", cwr, waits=w, incs=[("cwr", 16)])

        for n in range(4):
            def crep(te, e=e, n=n):
                return te.matmul(psR2[:], p.ones_r[:],
                                 cwrow[0:1, n * 512:(n + 1) * 512],
                                 start=True, stop=True)
            w = [("cwr", 16 * (e + 1))]
            if e * 4 + n >= 1:
                w.append(("crev", e * 4 + n))
            if e >= 1 and n == 0:
                w.append(("vxf", NCH * e))
            p.step("tensor", crep, waits=w, incs=[("crmm", 1)])

            def crev(sc, e=e, n=n):
                return sc.copy(cwrep[:, n * 512:(n + 1) * 512], psR2[:])
            p.step("scalar", crev, waits=[("crmm", None)], incs=[("crev", 1)])

        def mkxfc(v, e=e):
            for k in range(NCH):
                inst = v.tensor_tensor(xfc[:, k * NT:(k + 1) * NT],
                                       xn_bf[:, k * NT:(k + 1) * NT],
                                       cwrep[:], ALU.mult)
            return inst
        w = [("crev", 4 * (e + 1))]
        if e >= 1:
            w.append(("umm", e * 32))
        p.step("vector", mkxfc, waits=w, incs=[("vxf", NCH)])

        for m in range(8):
            gm = e * 8 + m
            wbuf = (gm % 2) * NCH * 128

            def ldw1(g, e=e, m=m, wbuf=wbuf):
                return g.dma_start(out=wsl1[:, wbuf:wbuf + NCH * 128],
                                   in_=w13[e][:, :, m * 128:(m + 1) * 128])
            w = []
            if gm >= 2:
                w.append(("umm", (gm - 1) * 4))
            p.step("gpsimd", ldw1, waits=w, incs=[("ldw1", 16)])

            for n in range(4):
                idx = m * 4 + n

                def umm(te, e=e, m=m, n=n, wbuf=wbuf):
                    rhs_src = xfc if m < 4 else xn_bf
                    for k in range(NCH):
                        inst = te.matmul(pb[n % 2][:],
                                         wsl1[:, wbuf + k * 128:wbuf + (k + 1) * 128],
                                         rhs_src[:, k * NT + n * 512:k * NT + (n + 1) * 512],
                                         start=(k == 0), stop=(k == NCH - 1))
                    return inst
                w = [("ldw1", 16 * (gm + 1))]
                if m < 4:
                    w.append(("vxf", NCH * (e + 1)))
                prev = e * 32 + idx - 2
                if prev >= e * 32:
                    w.append(("uev", prev + 2))
                elif e >= 1:
                    w.append(("uev", e * 32))
                p.step("tensor", umm, waits=w, incs=[("umm", 1)])

                if m < 4:
                    def uev(sc, e=e, m=m, n=n):
                        return sc.copy(he_bf[:, m * NT + n * 512:m * NT + (n + 1) * 512],
                                       pb[n % 2][:])
                    w2_ = [("umm", None)]
                    if e >= 1 and m == 0 and n == 0:
                        w2_.append(("zmm", 32 * e))
                    p.step("scalar", uev, waits=w2_, incs=[("uev", 1)])
                else:
                    def uev(sc, e=e, m=m, n=n):
                        return sc.activation(gs_f[:], pb[n % 2][:], AF.Silu)
                    p.step("scalar", uev,
                           waits=[("umm", None), ("vhe", e * 16 + (m - 4) * 4 + n)],
                           incs=[("uev", 1)])

                    def hemul(v, e=e, m=m, n=n):
                        sl = he_bf[:, (m - 4) * NT + n * 512:(m - 4) * NT + (n + 1) * 512]
                        return v.tensor_tensor(sl, sl, gs_f[:], ALU.mult)
                    p.step("vector", hemul, waits=[("uev", None)], incs=[("vhe", 1)])

        for dc in range(NCH):
            gd = e * 8 + dc
            zbuf = (gd % 2) * 4 * 128

            def ldw2(g, e=e, dc=dc, zbuf=zbuf):
                return g.dma_start(out=wsl2[:, zbuf:zbuf + 4 * 128],
                                   in_=w23[e][:, :, dc * 128:(dc + 1) * 128])
            w = []
            if gd >= 2:
                w.append(("zmm", (gd - 1) * 4))
            p.step("gpsimd", ldw2, waits=w, incs=[("ldw2", 16)])

            for n in range(4):
                idx = e * 32 + dc * 4 + n

                def zmm(te, e=e, dc=dc, n=n, zbuf=zbuf):
                    for k in range(4):
                        inst = te.matmul(pb[2 + n % 2][:],
                                         wsl2[:, zbuf + k * 128:zbuf + (k + 1) * 128],
                                         he_bf[:, k * NT + n * 512:k * NT + (n + 1) * 512],
                                         start=(k == 0), stop=(k == 3))
                    return inst
                w = [("ldw2", 16 * (gd + 1)), ("vhe", 16 * (e + 1))]
                prev = idx - 2
                if prev >= 0:
                    w.append(("zev", prev + 1))
                p.step("tensor", zmm, waits=w, incs=[("zmm", 1)])

                def zev(v, e=e, dc=dc, n=n):
                    return v.scalar_tensor_tensor(
                        x_sb[:, dc * NT + n * 512:dc * NT + (n + 1) * 512],
                        pb[2 + n % 2][:], 0.0,
                        x_sb[:, dc * NT + n * 512:dc * NT + (n + 1) * 512],
                        ALU.add, ALU.add)
                p.step("vector", zev, waits=[("zmm", None)], incs=[("zev", 1)])

    ws13 = ws1.rearrange("(k p) m -> p k m", p=128)
    ws23 = ws2.rearrange("(k p) m -> p k m", p=128)
    hs_bf = xfc

    for m in range(16):
        wbuf = (m % 2) * NCH * 128

        def ldws1(g, m=m, wbuf=wbuf):
            return g.dma_start(out=wsl1[:, wbuf:wbuf + NCH * 128],
                               in_=ws13[:, :, m * 128:(m + 1) * 128])
        w = [("umm", 16 * 32)] if m < 2 else [("ummS", (m - 1) * 4)]
        p.step("gpsimd", ldws1, waits=w, incs=[("ldws1", 16)])

        for n in range(4):
            idx = m * 4 + n

            def ummS(te, m=m, n=n, wbuf=wbuf):
                for k in range(NCH):
                    inst = te.matmul(pb[n % 2][:],
                                     wsl1[:, wbuf + k * 128:wbuf + (k + 1) * 128],
                                     xn_bf[:, k * NT + n * 512:k * NT + (n + 1) * 512],
                                     start=(k == 0), stop=(k == NCH - 1))
                return inst
            w = [("ldws1", 16 * (m + 1)), ("zmm", 16 * 32), ("vxf", 16 * NCH)]
            if idx == 0:
                w.append(("uev", 512))
            prev = idx - 2
            if prev >= 0:
                w.append(("uevS", prev + 2))
            p.step("tensor", ummS, waits=w, incs=[("ummS", 1)])

            if m < 8:
                def uevS(sc, m=m, n=n):
                    return sc.copy(hs_bf[:, m * NT + n * 512:m * NT + (n + 1) * 512],
                                   pb[n % 2][:])
                p.step("scalar", uevS, waits=[("ummS", None)], incs=[("uevS", 1)])
            else:
                def uevS(sc, m=m, n=n):
                    return sc.activation(gs_f[:], pb[n % 2][:], AF.Silu)
                p.step("scalar", uevS, waits=[("ummS", None), ("vheS", (m - 8) * 4 + n)],
                       incs=[("uevS", 1)])

                def hemulS(v, m=m, n=n):
                    sl = hs_bf[:, (m - 8) * NT + n * 512:(m - 8) * NT + (n + 1) * 512]
                    return v.tensor_tensor(sl, sl, gs_f[:], ALU.mult)
                p.step("vector", hemulS, waits=[("uevS", None)], incs=[("vheS", 1)])

    for dc in range(NCH):
        zbuf = (dc % 2) * 4 * 128

        def ldws2(g, dc=dc, zbuf=zbuf):
            return g.dma_start(out=wsl1[:, zbuf * 2:zbuf * 2 + NCH * 128],
                               in_=ws23[:, :, dc * 128:(dc + 1) * 128])
        w = [("ummS", 64)] if dc < 2 else [("zmmS", (dc - 1) * 4)]
        p.step("gpsimd", ldws2, waits=w, incs=[("ldws2", 16)])

        for n in range(4):
            idx = dc * 4 + n

            def zmmS(te, dc=dc, n=n, zbuf=zbuf):
                for k in range(NCH):
                    inst = te.matmul(pb[2 + n % 2][:],
                                     wsl1[:, zbuf * 2 + k * 128:zbuf * 2 + (k + 1) * 128],
                                     hs_bf[:, k * NT + n * 512:k * NT + (n + 1) * 512],
                                     start=(k == 0), stop=(k == NCH - 1))
                return inst
            w = [("ldws2", 16 * (dc + 1)), ("vheS", 32)]
            if idx == 0:
                w.append(("zev", 512))
            prev = idx - 2
            if prev >= 0:
                w.append(("zevS", prev + 1))
            p.step("tensor", zmmS, waits=w, incs=[("zmmS", 1)])

            def zevS(v, dc=dc, n=n):
                return v.scalar_tensor_tensor(
                    x_sb[:, dc * NT + n * 512:dc * NT + (n + 1) * 512],
                    pb[2 + n % 2][:], 0.0,
                    x_sb[:, dc * NT + n * 512:dc * NT + (n + 1) * 512],
                    ALU.add, ALU.add)
            p.step("vector", zevS, waits=[("zmmS", None), ("zev", 16 * 32)],
                   incs=[("zevS", 1)])

    return _mk_finish(p, x_sb, xo)([("zevS", None), ("roev", None)])




def build_L2(stop="full", fx=None):
    p = Prog() if fx is None else fx["p"]
    nc = p.nc
    inp = {}

    def ein(name, shape, dtype=F32):
        inp[name] = nc.dram_tensor(p.px + name, shape, dtype,
                                   kind="ExternalInput").ap()
        return inp[name]

    if fx is None:
        x_in = ein("x", [D, NT])
    wqT = ein("wqT", [D, D])
    wkT = ein("wkT", [D, D])
    wvT = ein("wvT", [D, D])
    woT = ein("woT", [D, D])
    ein("bq_c", [128, 8]); ein("bk_c", [128, 8])
    ein("bv_c", [128, 8]); ein("bo_c", [128, 8])
    ein("projT", [64, 256])
    ein("gwT", [D, 16])
    ein("ebias", [1, 16])
    wge = ein("wge", [16, D, D])
    wue = ein("wue", [16, D, D])
    wde = ein("wde", [16, D, D])
    wgs = ein("wgs", [D, D])
    wus = ein("wus", [D, D])
    wds = ein("wds", [D, D])
    xo = nc.dram_tensor("xo", [D, NT], F32,
                        kind="ExternalOutput").ap() if fx is None else None

    ccci = p.dram("ccci", [128, 2080])
    ccco = p.dram("ccco", [128, 2080])

    if fx is None:
        x_sb = p.sb("x_sb", [128, NCH * NT])
        xn_bf = p.sb("xn_bf", [128, NCH * NT], BF16)
    else:
        x_sb, xn_bf = fx["x_sb"], fx["xn_bf"]
    bcols = p.sb("bcols", [128, 32])
    bq_s, bk_s = bcols[:, 0:8], bcols[:, 8:16]
    bv_s, bo_s = bcols[:, 16:24], bcols[:, 24:32]
    dn_col = p.sb("dn_col", [128, 1])
    stab_c = p.sb("stab_c", [128, 2])
    dg_all = p.sb("dg_all", [128, 16 * 16])
    onec = p.sb("onec", [1, 4])
    prj_sb = p.sb("prj_sb", [128, 256], BF16)
    ctx_all = p.sb("ctx_all", [128, 16 * 130])
    ctx_bf = p.sb("ctx_bf", [128, 16 * 130], BF16)

    pb = [p.psx(f"pb{i}", [128, 512]) for i in range(4)]

    if fx is None:
        emit_common(p)
        x3 = x_in.rearrange("(k p) n -> k p n", p=128)

        def ldx(sy):
            return [sy.dma_start(out=x_sb[:, k * NT:(k + 1) * NT], in_=x3[k])
                    for k in range(NCH)]
        p.step("sync", ldx, incs=[("x_ok", 16 * NCH)])

    def ldsmall(sy):
        return [sy.dma_start(out=bq_s, in_=inp["bq_c"][:]),
                sy.dma_start(out=bk_s, in_=inp["bk_c"][:]),
                sy.dma_start(out=bv_s, in_=inp["bv_c"][:]),
                sy.dma_start(out=bo_s, in_=inp["bo_c"][:])]
    p.step("sync", ldsmall, waits=[("common", 1)], incs=[("sm_ok", 16 * 4)])

    def ldprj(g):
        return [g.dma_start(out=prj_sb[0:64, :], in_=inp["projT"][:]),
                g.dma_start(out=prj_sb[64:128, :], in_=inp["projT"][:])]
    p.step("gpsimd", ldprj, waits=[("common", 1)], incs=[("ldprj", 32)])

    def mkdn(v):
        return v.memset(dn_col[:], 64.0 ** -0.25)
    p.step("vector", mkdn, waits=[("common", 1)], incs=[("mkdn", 1)])

    ln = emit_ln(p, "ln2", x_sb, xn_bf, "x_ok")
    p.rel(*ln["scoped"])
    p.rel(*ln["psums"])
    p.rel("ln2rows", "ln2sumx", "ln2sumsq")
    tsA = p.psx("tsA", [128, 128], BF16)
    tsB = p.psx("tsB", [128, 128], BF16)
    tsF = p.psx("tsF", [128, 128], F32)
    psR2 = p.psx("psR2", [128, 512], F32)
    psT2 = tsF[0:1, 0:128]

    k_fm = p.sbx("k_fm", [128, NCH * NT], BF16)
    wk_sb = p.sbx("wk_sb", [128, NCH * D], BF16)
    wk3 = wkT.rearrange("(k p) m -> p k m", p=128)
    wv3 = wvT.rearrange("(k p) m -> p k m", p=128)

    def ldwk(g):
        return g.dma_start(out=wk_sb[:], in_=wk3[:, :, :])
    p.step("gpsimd", ldwk, waits=[("ln2xn", 8)], incs=[("ldwk", 16)])

    for oc in range(NCH):
        for n in range(4):
            idx = oc * 4 + n

            def kmm(te, oc=oc, n=n, idx=idx):
                for kc in range(NCH):
                    inst = te.matmul(pb[idx % 2][:],
                                     wk_sb[:, kc * D + oc * 128:kc * D + (oc + 1) * 128],
                                     xn_bf[:, kc * NT + n * 512:kc * NT + (n + 1) * 512],
                                     start=(kc == 0), stop=(kc == NCH - 1))
                return inst
            w = [("ldwk", 16), ("ln2xn", None)]
            if idx >= 2:
                w.append(("kev", idx - 1))
            p.step("tensor", kmm, waits=w, incs=[("kmm", 1)])

            def kev(sc, oc=oc, n=n, idx=idx):
                return sc.activation(k_fm[:, oc * NT + n * 512:oc * NT + (n + 1) * 512],
                                     pb[idx % 2][:], AF.Identity,
                                     bias=bk_s[:, oc:oc + 1])
            p.step("scalar", kev, waits=[("kmm", None), ("sm_ok", None)],
                   incs=[("kev", 1)])

    v_tm = p.sbx("v_tm", [128, 16 * D], BF16)

    def ldwv(g):
        return g.dma_start(out=wk_sb[:], in_=wv3[:, :, :])
    p.step("gpsimd", ldwv, waits=[("kmm", 32)], incs=[("ldwv", 16)])

    for tc in range(16):
        for h2 in range(2):
            idx = tc * 2 + h2

            def vmm(te, tc=tc, h2=h2, idx=idx):
                for kc in range(NCH):
                    inst = te.matmul(pb[2 + idx % 2][:],
                                     xn_bf[:, kc * NT + tc * 128:kc * NT + (tc + 1) * 128],
                                     wk_sb[:, kc * D + h2 * 512:kc * D + (h2 + 1) * 512],
                                     start=(kc == 0), stop=(kc == NCH - 1))
                return inst
            w = [("ldwv", 16)]
            if idx >= 2:
                w.append(("vev", idx - 1))
            p.step("tensor", vmm, waits=w, incs=[("vmm", 1)])

            def vev(sc, tc=tc, h2=h2, idx=idx):
                return sc.copy(v_tm[:, tc * D + h2 * 512:tc * D + (h2 + 1) * 512],
                               pb[2 + idx % 2][:])
            p.step("scalar", vev, waits=[("vmm", None)], incs=[("vev", 1)])

    p.rel("wk_sb")

    def k_h(h, a, b):
        return k_fm[(h % 2) * 64:(h % 2) * 64 + 64,
                    (h // 2) * NT + a:(h // 2) * NT + b]

    ksq = p.sbx("ksq", [128, NT], BF16)
    for h in range(16):
        hb = (h % 2) * 64

        def sqk(sc, h=h, hb=hb):
            return sc.activation(ksq[hb:hb + 64, :], k_h(h, 0, NT), AF.Square)
        w = [("kev", 32)]
        if h >= 1:
            w.append(("dtr", h * 16))
        p.step("scalar", sqk, waits=w, incs=[("sqk", 1)])

        for tc in range(16):
            gidx = h * 16 + tc

            def dtr(te, tc=tc, gidx=gidx, hb=hb):
                return te.transpose((tsA if gidx % 2 == 0 else tsB)[:, 0:64],
                                    ksq[hb:hb + 64, tc * 128:(tc + 1) * 128],
                                    p.ident_bf[hb:hb + 64, hb:hb + 64])
            w = [("sqk", h + 1)]
            if gidx >= 2:
                w.append(("dred", gidx - 1))
            p.step("tensor", dtr, waits=w, incs=[("dtr", 1)])

            def dred(v, h=h, tc=tc, gidx=gidx):
                return v.tensor_reduce(dg_all[:, tc * 16 + h:tc * 16 + h + 1],
                                       (tsA if gidx % 2 == 0 else tsB)[:, 0:64],
                                       AX.X, ALU.add)
            p.step("vector", dred, waits=[("dtr", None)], incs=[("dred", 1)])

    def locmin(v):
        return v.tensor_reduce(stab_c[:, 1:2], dg_all[:], AX.X, ALU.min)
    p.step("vector", locmin, waits=[("dred", None)], incs=[("lmn", 1)])

    def trmn(te):
        return te.transpose(psT2, stab_c[:, 1:2], p.ident_f[:])
    p.step("tensor", trmn, waits=[("lmn", None), ("ln2gm", 0)], incs=[("trmn", 1)])

    def redmn(v):
        v.tensor_reduce(onec[0:1, 0:1], psT2, AX.X, ALU.min)
        return v.tensor_scalar(onec[0:1, 1:2], onec[0:1, 0:1], -0.5, None, ALU.mult)
    p.step("vector", redmn, waits=[("trmn", None)], incs=[("redmn", 1)])

    emit_armax_scalar(p, "st2", onec[0:1, 1:2], "redmn", onec[0:1, 2:3])

    def repst(te):
        return te.matmul(psR2[:, 0:1], p.ones_r[:], onec[0:1, 2:3],
                         start=True, stop=True)
    p.step("tensor", repst, waits=[("st2out", 16), ("ln2repev", 8)],
           incs=[("repst", 1)])

    def repstev(sc):
        return sc.copy(stab_c[:, 0:1], psR2[:, 0:1])
    p.step("scalar", repstev, waits=[("repst", None)], incs=[("evst", 1)])

    def mkball(v):
        v.tensor_scalar(dg_all[:], dg_all[:], -0.5, None, ALU.mult)
        return v.tensor_scalar(dg_all[:], dg_all[:], stab_c[:, 0:1], None,
                               ALU.subtract)
    p.step("vector", mkball, waits=[("evst", None)], incs=[("ball", 1)])

    kfT = p.sbx("kfT", [128, 16 * 256], BF16)
    vhx = p.sbx("vhx", [128, 16 * 65], BF16)
    for h in range(16):
        def mkvhx(v, h=h):
            v.memset(vhx[:], 1.0)
            for tc in range(16):
                inst = v.tensor_copy(
                    vhx[:, tc * 65:tc * 65 + 64],
                    v_tm[:, tc * D + (h // 2) * 128 + (h % 2) * 64:
                         tc * D + (h // 2) * 128 + (h % 2) * 64 + 64])
            return inst
        w = [("vev", 32)]
        if h >= 1:
            w.append(("cmm", 2 * h))
        p.step("vector", mkvhx, waits=w, incs=[("vhx", 1)])

        for tc in range(16):
            gidx = h * 16 + tc

            def kfmm(te, h=h, tc=tc):
                hb = (h % 2) * 64
                return te.matmul(pb[0][:, 0:256], k_h(h, tc * 128, (tc + 1) * 128),
                                 prj_sb[hb:hb + 64, :], start=True, stop=True)
            w = [("ldprj", 32), ("ball", 1)]
            if gidx >= 1:
                w.append(("kfev", gidx))
            p.step("tensor", kfmm, waits=w, incs=[("kfmm", 1)])

            def kfev(sc, h=h, tc=tc):
                return sc.activation(kfT[:, tc * 256:(tc + 1) * 256],
                                     pb[0][:, 0:256], AF.Exp,
                                     bias=dg_all[:, tc * 16 + h:tc * 16 + h + 1],
                                     scale=dn_col[:, 0:1])
            p.step("scalar", kfev, waits=[("kfmm", None)], incs=[("kfev", 1)])

            def kfeps(v, tc=tc):
                sl = kfT[:, tc * 256:(tc + 1) * 256]
                return v.tensor_scalar(sl, sl, 1e-3, 0.0625, ALU.add, ALU.mult)
            p.step("vector", kfeps, waits=[("kfev", None)], incs=[("kfeps", 1)])

        for mc in range(2):
            def cmm(te, h=h, mc=mc):
                for tc in range(16):
                    inst = te.matmul(pb[1 + mc][:, 0:65],
                                     kfT[:, tc * 256 + mc * 128:tc * 256 + (mc + 1) * 128],
                                     vhx[:, tc * 65:(tc + 1) * 65],
                                     start=(tc == 0), stop=(tc == 15))
                return inst
            w = [("kfeps", 16 * (h + 1)), ("vhx", h + 1)]
            if h >= 1:
                w.append(("cev", 2 * h))
            p.step("tensor", cmm, waits=w, incs=[("cmm", 1)])

            def cev(sc, h=h, mc=mc):
                return sc.copy(ctx_all[:, h * 130 + mc * 65:h * 130 + (mc + 1) * 65],
                               pb[1 + mc][:, 0:65])
            p.step("scalar", cev, waits=[("cmm", None)], incs=[("cev", 1)])

    p.rel("ksq", "v_tm")

    _sd, _vd = p.sem("ccd"), p.av("ccd", 16)
    _sc, _vc = p.sem("ccc"), p.av("ccc", 1)
    p.mark("ccd", 16)
    p.mark("ccc", 1)
    def ccctx(g, _sd=_sd, _vd=_vd, _sc=_sc, _vc=_vc):
        g.dma_start(out=ccci[:, 0:2080], in_=ctx_all[:]).then_inc(_sd, 16)
        g.wait_ge(_sd, _vd)
        g.collective_compute("AllReduce", ALU.add, replica_groups=PAIRS,
                             ins=[ccci.opt()], outs=[ccco.opt()]).then_inc(_sc, 1)
        g.wait_ge(_sc, _vc)
        return g.dma_start(out=ctx_all[:], in_=ccco[:, 0:2080])
    p.step("gpsimd", ccctx, waits=[("cev", None)], incs=[("cco", 16)])

    def ctxbf(v):
        return v.tensor_copy(ctx_bf[:], ctx_all[:])
    p.step("vector", ctxbf, waits=[("cco", None)], incs=[("cbf", 1)])

    if stop == "ctx":
        return _mk_finish(p, x_sb, xo)([("cbf", None)])

    p.rel("k_fm", "kfT", "vhx")
    q_fm = p.sbx("q_fm", [128, NCH * NT], BF16)
    wq_2 = [p.sbx("wq_sbA", [128, 4 * D], BF16), p.sbx("wq_sbB", [128, 4 * D], BF16)]

    def wq_sl(kc, a, b):
        return wq_2[kc // 4][:, (kc % 4) * D + a:(kc % 4) * D + b]
    wq3 = wqT.rearrange("(k p) m -> p k m", p=128)

    def ldwq(g):
        return [g.dma_start(out=wq_2[0][:], in_=wq3[:, 0:4, :]),
                g.dma_start(out=wq_2[1][:], in_=wq3[:, 4:8, :])]
    p.step("gpsimd", ldwq, waits=[("kfmm", 256), ("cmm", 32), ("dtr", 256)],
           incs=[("ldwq", 32)])

    for oc in range(NCH):
        for n in range(4):
            idx = oc * 4 + n

            def qmm(te, oc=oc, n=n, idx=idx):
                for kc in range(NCH):
                    inst = te.matmul(pb[idx % 2][:],
                                     wq_sl(kc, oc * 128, (oc + 1) * 128),
                                     xn_bf[:, kc * NT + n * 512:kc * NT + (n + 1) * 512],
                                     start=(kc == 0), stop=(kc == NCH - 1))
                return inst
            w = [("ldwq", 32), ("cev", 32)]
            if idx >= 2:
                w.append(("qev", idx - 1))
            p.step("tensor", qmm, waits=w, incs=[("qmm", 1)])

            def qev(sc, oc=oc, n=n, idx=idx):
                return sc.activation(q_fm[:, oc * NT + n * 512:oc * NT + (n + 1) * 512],
                                     pb[idx % 2][:], AF.Identity,
                                     bias=bq_s[:, oc:oc + 1])
            p.step("scalar", qev, waits=[("qmm", None), ("sm_ok", None)],
                   incs=[("qev", 1)])

    def q_h(h, a, b):
        return q_fm[(h % 2) * 64:(h % 2) * 64 + 64,
                    (h // 2) * NT + a:(h // 2) * NT + b]

    attn_fm = p.sbx("attn_fm", [128, NCH * NT], BF16)
    qf_sb = p.sbx("qf_sb", [128, 2 * NT], BF16)
    onum_sb = p.sbx("onum_sb", [128, 512], BF16)
    rrep_sb = p.sbx("rrep_sb", [128, 512])
    den_r = p.sb("den_r", [1, 512])

    for h in range(16):
        for mc in range(2):
            for n in range(4):
                idx = h * 8 + mc * 4 + n

                def qfmm(te, h=h, mc=mc, n=n):
                    hb = (h % 2) * 64
                    return te.matmul(pb[0][:],
                                     prj_sb[hb:hb + 64, mc * 128:(mc + 1) * 128],
                                     q_h(h, n * 512, (n + 1) * 512),
                                     start=True, stop=True)
                w = [("qev", 32)]
                if idx >= 1:
                    w.append(("qfev", idx))
                p.step("tensor", qfmm, waits=w, incs=[("qfmm", 1)])

                def qfev(sc, mc=mc, n=n):
                    return sc.activation(qf_sb[:, mc * NT + n * 512:mc * NT + (n + 1) * 512],
                                         pb[0][:], AF.Exp, scale=dn_col[:, 0:1])
                p.step("scalar", qfev, waits=[("qfmm", None)], incs=[("qfev", 1)])

                def qfeps(v, mc=mc, n=n):
                    sl = qf_sb[:, mc * NT + n * 512:mc * NT + (n + 1) * 512]
                    return v.tensor_scalar(sl, sl, 1e-3, 0.0625, ALU.add, ALU.mult)
                p.step("vector", qfeps, waits=[("qfev", None)], incs=[("qfeps", 1)])

        for n in range(4):
            sidx = h * 4 + n

            def onmm(te, h=h, n=n):
                hb = (h % 2) * 64
                for mc in range(2):
                    te.matmul(pb[1][hb:hb + 64, :],
                              ctx_bf[:, h * 130 + mc * 65:h * 130 + mc * 65 + 64],
                              qf_sb[:, mc * NT + n * 512:mc * NT + (n + 1) * 512],
                              start=(mc == 0), stop=(mc == 1), skip_group_check=True,
                              tile_position=(0, hb))
                for mc in range(2):
                    inst = te.matmul(pb[2][0:1, :],
                                     ctx_bf[:, h * 130 + mc * 65 + 64:h * 130 + mc * 65 + 65],
                                     qf_sb[:, mc * NT + n * 512:mc * NT + (n + 1) * 512],
                                     start=(mc == 0), stop=(mc == 1),
                                     skip_group_check=True)
                return inst
            w = [("qfeps", 8 * (h + 1)), ("cbf", 1)]
            if sidx >= 1:
                w.append(("att", sidx))
            p.step("tensor", onmm, waits=w, incs=[("onmm", 1)])

            def onev(sc, h=h, n=n):
                hb = (h % 2) * 64
                return sc.copy(onum_sb[hb:hb + 64, 0:512], pb[1][hb:hb + 64, 0:512])
            p.step("scalar", onev, waits=[("onmm", None)], incs=[("onev", 1)])

            def denev(v, n=n):
                v.tensor_copy(den_r[0:1, :], pb[2][0:1, :])
                return v.reciprocal(den_r[0:1, :], den_r[0:1, :])
            p.step("vector", denev, waits=[("onmm", None)], incs=[("denev", 1)])

            def rrep(te, h=h, n=n):
                hb = (h % 2) * 64
                return te.matmul(pb[3][hb:hb + 64, :], p.ones_r[0:1, 0:64],
                                 den_r[0:1, :], start=True, stop=True,
                                 tile_position=(0, hb))
            p.step("tensor", rrep, waits=[("denev", None)], incs=[("rrep", 1)])

            def rrepev(sc, h=h, n=n):
                hb = (h % 2) * 64
                return sc.copy(rrep_sb[hb:hb + 64, :], pb[3][hb:hb + 64, :])
            p.step("scalar", rrepev, waits=[("rrep", None)], incs=[("rrepev", 1)])

            def attv(v, h=h, n=n):
                hb = (h % 2) * 64
                sl = attn_fm[hb:hb + 64,
                             (h // 2) * NT + n * 512:(h // 2) * NT + (n + 1) * 512]
                v.tensor_tensor(sl, onum_sb[hb:hb + 64, 0:512],
                                rrep_sb[hb:hb + 64, :], ALU.mult)
                return v.tensor_scalar(
                    sl, sl,
                    bv_s[(h % 2) * 64:(h % 2) * 64 + 64, h // 2:h // 2 + 1],
                    None, ALU.add)
            p.step("vector", attv, waits=[("onev", None), ("rrepev", None)],
                   incs=[("att", 1)])


    if stop == "heads":
        return _mk_finish(p, x_sb, xo)([("att", None)])

    p.rel("qf_sb")
    wo3 = woT.rearrange("(k p) m -> p k m", p=128)

    def ldwo(g):
        return [g.dma_start(out=wq_2[0][:], in_=wo3[:, 0:4, :]),
                g.dma_start(out=wq_2[1][:], in_=wo3[:, 4:8, :])]
    p.step("gpsimd", ldwo, waits=[("qmm", 32)], incs=[("ldwo", 32)])

    for dc in range(NCH):
        for n in range(4):
            idx = dc * 4 + n

            def womm(te, dc=dc, n=n, idx=idx):
                for kc in range(NCH):
                    inst = te.matmul(pb[idx % 2][:],
                                     wq_sl(kc, dc * 128, (dc + 1) * 128),
                                     attn_fm[:, kc * NT + n * 512:kc * NT + (n + 1) * 512],
                                     start=(kc == 0), stop=(kc == NCH - 1))
                return inst
            w = [("ldwo", 32), ("att", 64)]
            if idx >= 2:
                w.append(("woev", idx - 1))
            p.step("tensor", womm, waits=w, incs=[("womm", 1)])

            def woev(v, dc=dc, n=n, idx=idx):
                return v.scalar_tensor_tensor(
                    x_sb[:, dc * NT + n * 512:dc * NT + (n + 1) * 512],
                    pb[idx % 2][:], bo_s[:, dc:dc + 1],
                    x_sb[:, dc * NT + n * 512:dc * NT + (n + 1) * 512],
                    ALU.add, ALU.add)
            p.step("vector", woev, waits=[("womm", None)], incs=[("woev", 1)])

    if stop == "attn":
        return _mk_finish(p, x_sb, xo)([("woev", None)])

    p.rel("attn_fm", "onum_sb", "rrep_sb", "wq_sbA", "wq_sbB", "q_fm",
          "ctx_all", "ctx_bf")
    wg_sb = p.sb("wg_sb", [128, NCH * 16], BF16)
    ebias_b = p.sb("ebias_b", [128, 16])
    wgg3 = inp["gwT"].rearrange("(k p) m -> p k m", p=128)

    def ldwg16(g):
        return g.dma_start(out=wg_sb[:], in_=wgg3[:, :, :])
    p.step("gpsimd", ldwg16, waits=[("womm", 32), ("woev", 32)],
           incs=[("ldwg16", 16)])

    def ldeb(sy):
        return sy.dma_start(out=ebias_b[0:1, :], in_=inp["ebias"][:])
    p.step("sync", ldeb, waits=[("womm", 32), ("att", 64)], incs=[("ldeb", 16)])

    def repeb(te):
        return te.matmul(psR2[:, 0:16], p.ones_r[:], ebias_b[0:1, :],
                         start=True, stop=True)
    p.step("tensor", repeb, waits=[("ldeb", 16), ("repst", 1)], incs=[("repeb", 1)])

    def repebv(sc):
        return sc.copy(ebias_b[:], psR2[:, 0:16])
    p.step("scalar", repebv, waits=[("repeb", None)], incs=[("repebv", 1)])

    scFM = p.sb("scFM", [16, NT])
    for n in range(4):
        def smm(te, n=n):
            for kc in range(NCH):
                inst = te.matmul(pb[2][0:16, 0:512],
                                 wg_sb[:, kc * 16:(kc + 1) * 16],
                                 xn_bf[:, kc * NT + n * 512:kc * NT + (n + 1) * 512],
                                 start=(kc == 0), stop=(kc == NCH - 1))
            return inst
        w = [("ldwg16", 16), ("woev", 32)]
        if n >= 1:
            w.append(("sev2", n))
        p.step("tensor", smm, waits=w, incs=[("smm2", 1)])

        def sev2(sc, n=n):
            return sc.copy(scFM[0:16, n * 512:(n + 1) * 512], pb[2][0:16, :])
        p.step("scalar", sev2, waits=[("smm2", None)], incs=[("sev2", 1)])

    cw_tm = p.sb("cw_tm", [128, 16 * 16])
    sel_tm = p.sb("sel_tm", [128, 16])
    rnk = p.sb("rnk", [128, 16])
    scc = p.sb("scc", [128, 4])
    for tc in range(16):
        def str_(te, tc=tc):
            return te.transpose(tsF[:, 0:16],
                                scFM[0:16, tc * 128:(tc + 1) * 128],
                                p.ident_f[0:16, 0:16])
        w = [("sev2", 4), ("dtr", 256)]
        if tc >= 1:
            w.append(("stv", tc))
        p.step("tensor", str_, waits=w, incs=[("str", 1)])

        def stv(v, tc=tc):
            return v.tensor_copy(cw_tm[:, tc * 16:(tc + 1) * 16], tsF[:, 0:16])
        p.step("vector", stv, waits=[("str", None)], incs=[("stv", 1)])

        def sgn(sc, tc=tc):
            t16 = cw_tm[:, tc * 16:(tc + 1) * 16]
            return sc.activation(t16, t16, AF.Sigmoid)
        p.step("scalar", sgn, waits=[("stv", None)], incs=[("sgn", 1)])

        def snorm(v, tc=tc):
            t16 = cw_tm[:, tc * 16:(tc + 1) * 16]
            v.tensor_tensor(sel_tm[:], t16, ebias_b[:], ALU.add)
            v.memset(rnk[:], 0.0)
            for e in range(16):
                v.scalar_tensor_tensor(rnk[:], sel_tm[:], sel_tm[:, e:e + 1], rnk[:],
                                       ALU.is_lt, ALU.add)
            v.tensor_scalar(rnk[:], rnk[:], 8.0, None, ALU.is_lt)
            v.tensor_mul(t16, t16, rnk[:])
            v.tensor_reduce(scc[:, 0:1], t16, AX.X, ALU.add)
            v.reciprocal(scc[:, 1:2], scc[:, 0:1])
            v.tensor_scalar(scc[:, 1:2], scc[:, 1:2], 2.826, None, ALU.mult)
            return v.tensor_scalar(t16, t16, scc[:, 1:2], None, ALU.mult)
        p.step("vector", snorm, waits=[("sgn", None), ("repebv", 1)],
               incs=[("vsn", 1)])

    cwFM = p.sb("cwFM", [16, NT])
    for tc in range(16):
        def ctr(te, tc=tc):
            return te.transpose(tsF[0:16, :], cw_tm[:, tc * 16:(tc + 1) * 16],
                                p.ident_f[:])
        w = [("vsn", 16)]
        if tc >= 1:
            w.append(("ctv", tc))
        p.step("tensor", ctr, waits=w, incs=[("ctr", 1)])

        def ctv(v, tc=tc):
            return v.tensor_copy(cwFM[0:16, tc * 128:(tc + 1) * 128], tsF[0:16, :])
        p.step("vector", ctv, waits=[("ctr", None)], incs=[("ctv", 1)])

    if stop == "route":
        return _mk_finish(p, x_sb, xo)([("ctv", None), ("woev", None)])

    p.rel("scFM")
    cwrow = p.sb("cwrow", [1, NT])
    cwrep = p.sb("cwrep", [128, NT])
    xfc2 = [p.sb("xfcA", [128, 4 * NT], BF16), p.sb("xfcB", [128, 4 * NT], BF16)]
    he2 = [p.sb("heA", [128, 4 * NT], BF16), p.sb("heB", [128, 4 * NT], BF16)]

    def xfc_sl(k, a, b):
        return xfc2[k // 4][:, (k % 4) * NT + a:(k % 4) * NT + b]

    def he_sl(k, a, b):
        return he2[k // 4][:, (k % 4) * NT + a:(k % 4) * NT + b]
    gs_f = p.sb("gs_f", [128, 512])
    wsl1 = p.sb("wsl1", [128, 2 * NCH * 128], BF16)
    wge4 = wge.rearrange("e (k p) m -> e p k m", p=128)
    wue4 = wue.rearrange("e (k p) m -> e p k m", p=128)
    wde4 = wde.rearrange("e (k p) m -> e p k m", p=128)
    wgs3 = wgs.rearrange("(k p) m -> p k m", p=128)
    wus3 = wus.rearrange("(k p) m -> p k m", p=128)
    wds3 = wds.rearrange("(k p) m -> p k m", p=128)

    for e in range(17):
        shared = (e == 16)
        if not shared:
            def cwr(sy, e=e):
                return sy.dma_start(out=cwrow[:], in_=cwFM[e:e + 1, :])
            w = [("ctv", 16)]
            if e >= 1:
                w.append(("vxf", e * NCH))
            p.step("sync", cwr, waits=w, incs=[("cwr", 16)])

            for n in range(4):
                def crep(te, e=e, n=n):
                    return te.matmul(psR2[:], p.ones_r[:],
                                     cwrow[0:1, n * 512:(n + 1) * 512],
                                     start=True, stop=True)
                w = [("cwr", 16 * (e + 1)), ("repebv", 1)]
                if e * 4 + n >= 1:
                    w.append(("crev", e * 4 + n))
                if e >= 1 and n == 0:
                    w.append(("vxf", NCH * e))
                p.step("tensor", crep, waits=w, incs=[("crmm", 1)])

                def crev(sc, e=e, n=n):
                    return sc.copy(cwrep[:, n * 512:(n + 1) * 512], psR2[:])
                p.step("scalar", crev, waits=[("crmm", None)], incs=[("crev", 1)])

            def mkxfc(v, e=e):
                for k in range(NCH):
                    inst = v.tensor_tensor(xfc_sl(k, 0, NT),
                                           xn_bf[:, k * NT:(k + 1) * NT],
                                           cwrep[:], ALU.mult)
                return inst
            w = [("crev", 4 * (e + 1))]
            if e >= 1:
                w.append(("umm2", 32 * e))
            p.step("vector", mkxfc, waits=w, incs=[("vxf", NCH)])

        for m in range(NCH):
            gm = e * NCH + m
            wbuf = (gm % 2) * NCH * 128

            def ldg(g, e=e, m=m, wbuf=wbuf):
                src = wgs3[:, :, m * 128:(m + 1) * 128] if e == 16 else \
                    wge4[e][:, :, m * 128:(m + 1) * 128]
                return g.dma_start(out=wsl1[:, wbuf:wbuf + NCH * 128], in_=src)
            w = []
            if gm == 0:
                w = [("womm", 32), ("woev", 32), ("att", 64)]
            if gm >= 2:
                w.append(("umm", (gm - 1) * 4))
            if e >= 1 and m < 2:
                w.append(("zmm", 32 * e))
            p.step("gpsimd", ldg, waits=w, incs=[("ldw1", 16)])

            for n in range(4):
                def umm(te, e=e, m=m, n=n, wbuf=wbuf, gm=gm):
                    for k in range(NCH):
                        inst = te.matmul(pb[n % 2][:],
                                         wsl1[:, wbuf + k * 128:wbuf + (k + 1) * 128],
                                         xn_bf[:, k * NT + n * 512:k * NT + (n + 1) * 512],
                                         start=(k == 0), stop=(k == NCH - 1))
                    return inst
                w = [("ldw1", 16 * (gm + 1))]
                if gm == 0 and n < 2:
                    w.append(("woev", 32))
                if e >= 1 and m == 0 and n < 2:
                    w.append(("zev", 32 * e))
                t = gm * 4 + n - 2
                if t >= 0:
                    w.append(("uev", t + 1))
                p.step("tensor", umm, waits=w, incs=[("umm", 1)])

                def uev(sc, e=e, m=m, n=n):
                    return sc.activation(gs_f[:], pb[n % 2][:], AF.Silu)
                wv = [("umm", None)]
                if e * 32 + m * 4 + n >= 1:
                    wv.append(("vhe", e * 32 + m * 4 + n))
                p.step("scalar", uev, waits=wv, incs=[("uev", 1)])

                def hemul(v, e=e, m=m, n=n):
                    sl = he_sl(m, n * 512, (n + 1) * 512)
                    return v.tensor_copy(sl, gs_f[:])
                wv = [("uev", None)]
                if e >= 1 and m == 0 and n == 0:
                    wv.append(("zmm", 32 * e))
                p.step("vector", hemul, waits=wv, incs=[("vhe", 1)])

        for m in range(NCH):
            gm = e * NCH + m
            wbuf = (gm % 2) * NCH * 128

            def ldu(g, e=e, m=m, wbuf=wbuf):
                src = wus3[:, :, m * 128:(m + 1) * 128] if e == 16 else \
                    wue4[e][:, :, m * 128:(m + 1) * 128]
                return g.dma_start(out=wsl1[:, wbuf:wbuf + NCH * 128], in_=src)
            w = [("umm", (e + 1) * 32)] if m < 2 else [("umm2", (e * NCH + m - 1) * 4)]
            p.step("gpsimd", ldu, waits=w, incs=[("ldw2", 16)])

            for n in range(4):
                def umm2(te, e=e, m=m, n=n, wbuf=wbuf, gm=gm, shared=shared):
                    for k in range(NCH):
                        src = (xn_bf[:, k * NT + n * 512:k * NT + (n + 1) * 512]
                               if shared else xfc_sl(k, n * 512, (n + 1) * 512))
                        inst = te.matmul(pb[2 + n % 2][:],
                                         wsl1[:, wbuf + k * 128:wbuf + (k + 1) * 128],
                                         src,
                                         start=(k == 0), stop=(k == NCH - 1))
                    return inst
                w = [("ldw2", 16 * (gm + 1))]
                if not shared:
                    w.append(("vxf", NCH * (e + 1)))
                t = gm * 4 + n - 2
                if t >= 0:
                    w.append(("uev2", t + 1))
                elif e == 0 and m == 0:
                    w.append(("att", 64))
                p.step("tensor", umm2, waits=w, incs=[("umm2", 1)])

                def uev2(v, e=e, m=m, n=n):
                    sl = he_sl(m, n * 512, (n + 1) * 512)
                    return v.tensor_tensor(sl, sl, pb[2 + n % 2][:], ALU.mult)
                p.step("vector", uev2, waits=[("umm2", None), ("vhe", e * 32 + m * 4 + n + 1)],
                       incs=[("uev2", 1)])

        for dc in range(NCH):
            gd = e * NCH + dc
            zbuf = (gd % 2) * NCH * 128

            def ldd(g, e=e, dc=dc, zbuf=zbuf):
                src = wds3[:, :, dc * 128:(dc + 1) * 128] if e == 16 else \
                    wde4[e][:, :, dc * 128:(dc + 1) * 128]
                return g.dma_start(out=wsl1[:, zbuf:zbuf + NCH * 128], in_=src)
            w = [("umm2", (e + 1) * 32)] if dc < 2 else [("zmm", (e * NCH + dc - 1) * 4)]
            p.step("gpsimd", ldd, waits=w, incs=[("ldw3", 16)])

            for n in range(4):
                idx = e * 32 + dc * 4 + n

                def zmm(te, e=e, dc=dc, n=n, zbuf=zbuf, gd=gd):
                    for k in range(NCH):
                        inst = te.matmul(pb[n % 2][:],
                                         wsl1[:, zbuf + k * 128:zbuf + (k + 1) * 128],
                                         he_sl(k, n * 512, (n + 1) * 512),
                                         start=(k == 0), stop=(k == NCH - 1))
                    return inst
                w = [("ldw3", 16 * (gd + 1)), ("uev2", 32 * (e + 1))]
                prev = idx - 2
                if prev >= 0:
                    w.append(("zev", prev + 1))
                p.step("tensor", zmm, waits=w, incs=[("zmm", 1)])

                def zev(v, e=e, dc=dc, n=n):
                    return v.scalar_tensor_tensor(
                        x_sb[:, dc * NT + n * 512:dc * NT + (n + 1) * 512],
                        pb[n % 2][:], 0.0,
                        x_sb[:, dc * NT + n * 512:dc * NT + (n + 1) * 512],
                        ALU.add, ALU.add)
                p.step("vector", zev, waits=[("zmm", None)], incs=[("zev", 1)])

    return _mk_finish(p, x_sb, xo)([("zev", None), ("woev", None)])



def build_L3(stop="full", fx=None):
    p = Prog() if fx is None else fx["p"]
    nc = p.nc
    inp = {}

    def ein(name, shape, dtype=F32):
        inp[name] = nc.dram_tensor(p.px + name, shape, dtype,
                                   kind="ExternalInput").ap()
        return inp[name]

    if fx is None:
        x_in = ein("x", [D, NT])
    wq = ein("wq", [D, 64])
    wk = ein("wk", [D, 64])
    wv = ein("wv", [D, 64])
    rwT = ein("rwT", [D, 8])
    gwe = ein("gwe", [8, D, 4096])
    uwe = ein("uwe", [8, D, 4096])
    dwe = ein("dwe", [8, 4096, D])
    xo = nc.dram_tensor("xo", [D, NT], F32,
                        kind="ExternalOutput").ap() if fx is None else None

    kag_i = p.dram("kagi", [64, NT])
    kag_o = p.dram("kago", [128, NT])
    vag_i = p.dram("vagi", [64, NT])
    vag_o = p.dram("vago", [128, NT])
    mx_i = p.dram("mxi", [128, 32])
    mx_o = p.dram("mxo", [128, 32])
    es_i = p.dram("esi", [128, 32])
    es_o = p.dram("eso", [128, 32])
    cst_i = p.dram("csti", [1, 16])
    cst_o = p.dram("csto", [1, 16])

    if fx is None:
        x_sb = p.sb("x_sb", [128, NCH * NT])
        xn_bf = p.sb("xn_bf", [128, NCH * NT], BF16)
    else:
        x_sb, xn_bf = fx["x_sb"], fx["xn_bf"]

    if fx is None:
        emit_common(p)
        x3 = x_in.rearrange("(k p) n -> k p n", p=128)

        def ldx(sy):
            return [sy.dma_start(out=x_sb[:, k * NT:(k + 1) * NT], in_=x3[k])
                    for k in range(NCH)]
        p.step("sync", ldx, incs=[("x_ok", 16 * NCH)])

    ln = emit_ln(p, "ln3", x_sb, xn_bf, "x_ok")
    p.rel(*ln["scoped"])
    p.rel(*ln["psums"])
    p.rel("ln3rows", "ln3sumx", "ln3sumsq")

    if stop == "ln":
        return _mk_finish(p, x_sb, xo)([("ln3xn", 8)])

    wqkv_sb = p.sbx("wqkv_sb", [128, 3 * NCH * 64], BF16)
    q2 = p.sbx("q2", [128, NT], BF16)
    k_loc = p.sbx("k_loc", [64, NT])
    v_loc = p.sbx("v_loc", [64, NT])
    kag_sb = p.sbx("kag_sb", [128, NT], BF16)
    vag_sb = p.sbx("vag_sb", [128, NT], BF16)
    vT_sb = p.sbx("vT_sb", [128, 32 * 64], BF16)
    vTs = p.sbx("vTs", [128, 32 * 64], BF16)
    av2 = p.sbx("av2", [128, NT])
    es2 = p.sbx("es2", [128, 2 * 512])
    stats = p.sbx("stats", [128, 128])
    rs_sb = p.sbx("rs_sb", [128, 32])
    scw = p.sbx("scw", [128, 8])
    PT2 = [p.sbx("ptA", [128, 512], BF16), p.sbx("ptB", [128, 512], BF16)]

    pa = [p.psx("pa0", [128, 512]), p.psx("pa1", [128, 512])]
    pav = p.psx("pav", [64, 512])
    ptr = [p.psx("ptr0", [128, 64], BF16), p.psx("ptr1", [128, 64], BF16)]

    wq3 = wq.rearrange("(k p) m -> p k m", p=128)
    wk3 = wk.rearrange("(k p) m -> p k m", p=128)
    wv3 = wv.rearrange("(k p) m -> p k m", p=128)

    def ldqkv(g):
        return [g.dma_start(out=wqkv_sb[:, 0:NCH * 64], in_=wq3[:, :, :]),
                g.dma_start(out=wqkv_sb[:, NCH * 64:2 * NCH * 64], in_=wk3[:, :, :]),
                g.dma_start(out=wqkv_sb[:, 2 * NCH * 64:3 * NCH * 64], in_=wv3[:, :, :])]
    p.step("gpsimd", ldqkv, waits=[("common", 1)], incs=[("ldqkv", 48)])

    for gi in range(12):
        which, s = gi // 4, gi % 4

        def qkvm(te, which=which, s=s, gi=gi):
            for kc in range(NCH):
                inst = te.matmul(pa[s % 2][0:64, :],
                                 wqkv_sb[:, which * NCH * 64 + kc * 64:
                                         which * NCH * 64 + (kc + 1) * 64],
                                 xn_bf[:, kc * NT + s * 512:kc * NT + (s + 1) * 512],
                                 start=(kc == 0), stop=(kc == NCH - 1))
            return inst
        w = [("ldqkv", 48), ("ln3xn", 8)]
        if gi >= 2:
            w.append(("qkv_ev", gi - 1))
        p.step("tensor", qkvm, waits=w, incs=[("qkvm", 1)])

        def qkve(sc, which=which, s=s):
            if which == 0:
                return sc.copy(q2[0:64, s * 512:(s + 1) * 512], pa[s % 2][0:64, :])
            dst = k_loc if which == 1 else v_loc
            return sc.copy(dst[:, s * 512:(s + 1) * 512], pa[s % 2][0:64, :])
        p.step("scalar", qkve, waits=[("qkvm", None)], incs=[("qkv_ev", 1)])

    def q2cp(sy):
        return sy.dma_start(out=q2[64:128, :], in_=q2[0:64, :])
    p.step("sync", q2cp, waits=[("qkv_ev", 4)], incs=[("qcp", 16)])

    _sd, _vd = p.sem("agkd"), p.av("agkd", 16)
    _sc, _vc = p.sem("agkc"), p.av("agkc", 1)
    p.mark("agkd", 16)
    p.mark("agkc", 1)
    def agk(g, _sd=_sd, _vd=_vd, _sc=_sc, _vc=_vc):
        g.dma_start(out=kag_i[:], in_=k_loc[:]).then_inc(_sd, 16)
        g.wait_ge(_sd, _vd)
        g.collective_compute("AllGather", ALU.bypass, replica_groups=PAIRS,
                             ins=[kag_i.opt()], outs=[kag_o.opt()]).then_inc(
            _sc, 1)
        g.wait_ge(_sc, _vc)
        return g.dma_start(out=kag_sb[:], in_=kag_o[:])
    p.step("gpsimd", agk, waits=[("qkv_ev", 8)], incs=[("agk", 16)])

    _sd, _vd = p.sem("agvd"), p.av("agvd", 16)
    _sc, _vc = p.sem("agvc"), p.av("agvc", 1)
    p.mark("agvd", 16)
    p.mark("agvc", 1)
    def agv(g, _sd=_sd, _vd=_vd, _sc=_sc, _vc=_vc):
        g.dma_start(out=vag_i[:], in_=v_loc[:]).then_inc(_sd, 16)
        g.wait_ge(_sd, _vd)
        g.collective_compute("AllGather", ALU.bypass, replica_groups=PAIRS,
                             ins=[vag_i.opt()], outs=[vag_o.opt()]).then_inc(
            _sc, 1)
        g.wait_ge(_sc, _vc)
        return g.dma_start(out=vag_sb[:], in_=vag_o[:])
    p.step("gpsimd", agv, waits=[("qkv_ev", 12)], incs=[("agv", 16)])

    def kchunk(c):
        hb, cc = (c // 16) * 64, c % 16
        return kag_sb[hb:hb + 64, cc * 128:(cc + 1) * 128], hb

    for c in range(32):
        for s in range(4):
            a1 = c * 4 + s
            ksl, hb = kchunk(c)

            def am1(te, ksl=ksl, hb=hb, s=s):
                return te.matmul(pa[s % 2][:], ksl,
                                 q2[hb:hb + 64, s * 512:(s + 1) * 512],
                                 start=True, stop=True)
            w = []
            if a1 < 2:
                w = [("agk", 16), ("qcp", 16), ("qkv_ev", 12)]
            else:
                w.append(("p1v", a1 - 1))
            p.step("tensor", am1, waits=w, incs=[("p1m", 1)])

            def rv1(v, c=c, s=s):
                return v.tensor_reduce(scw[:, s:s + 1], pa[s % 2][:], AX.X,
                                       ALU.max)
            p.step("vector", rv1, waits=[("p1m", None)], incs=[("p1v", 1)])
            if s == 3:
                def rv1c(v, c=c):
                    return v.tensor_reduce(stats[:, c:c + 1], scw[:, 0:4],
                                           AX.X, ALU.max)
                p.step("vector", rv1c, incs=[("p1c", 1)])

    _sd, _vd = p.sem("amxd"), p.av("amxd", 16)
    _sc, _vc = p.sem("amxc"), p.av("amxc", 1)
    p.mark("amxd", 16)
    p.mark("amxc", 1)
    def armx(g, _sd=_sd, _vd=_vd, _sc=_sc, _vc=_vc):
        g.dma_start(out=mx_i[:], in_=stats[:, 0:32]).then_inc(_sd, 16)
        g.wait_ge(_sd, _vd)
        g.collective_compute("AllReduce", ALU.max, replica_groups=PAIRS,
                             ins=[mx_i.opt()], outs=[mx_o.opt()]).then_inc(
            _sc, 1)
        g.wait_ge(_sc, _vc)
        return g.dma_start(out=stats[:, 32:64], in_=mx_o[:])
    p.step("gpsimd", armx, waits=[("p1c", 32)], incs=[("amx", 16)])

    def vng(v):
        return v.tensor_scalar(stats[:, 64:96], stats[:, 32:64], -1.0, None,
                               ALU.mult)
    p.step("vector", vng, waits=[("amx", 16)], incs=[("vng", 1)])

    for c in range(32):
        for s in range(4):
            a2 = c * 4 + s
            ksl, hb = kchunk(c)

            def am2(te, ksl=ksl, hb=hb, s=s):
                return te.matmul(pa[s % 2][:], ksl,
                                 q2[hb:hb + 64, s * 512:(s + 1) * 512],
                                 start=True, stop=True)
            w = []
            if a2 < 2:
                w = [("p1v", 128)]
            else:
                w.append(("p2a", a2 - 1))
            p.step("tensor", am2, waits=w, incs=[("p2m", 1)])

            def pa2(sc, c=c, s=s, a2=a2):
                return sc.activation(es2[:, (a2 % 2) * 512:(a2 % 2) * 512 + 512],
                                     pa[s % 2][:], AF.Exp,
                                     bias=stats[:, 64 + c:64 + c + 1])
            w = [("p2m", None), ("vng", 1)]
            if a2 >= 1:
                w.append(("p2v", a2 - 1))
            p.step("scalar", pa2, waits=w, incs=[("p2a", 1)])

            def pv2(v, c=c, s=s, a2=a2):
                return v.tensor_reduce(scw[:, 4 + s:5 + s],
                                       es2[:, (a2 % 2) * 512:(a2 % 2) * 512 + 512],
                                       AX.X, ALU.add)
            p.step("vector", pv2, waits=[("p2a", None)], incs=[("p2v", 1)])
            if s == 3:
                def pv2c(v, c=c):
                    return v.tensor_reduce(stats[:, 96 + c:97 + c], scw[:, 4:8],
                                           AX.X, ALU.add)
                p.step("vector", pv2c, incs=[("p2c", 1)])

    _sd, _vd = p.sem("aesd"), p.av("aesd", 16)
    _sc, _vc = p.sem("aesc"), p.av("aesc", 1)
    p.mark("aesd", 16)
    p.mark("aesc", 1)
    def ares(g, _sd=_sd, _vd=_vd, _sc=_sc, _vc=_vc):
        g.dma_start(out=es_i[:], in_=stats[:, 96:128]).then_inc(_sd, 16)
        g.wait_ge(_sd, _vd)
        g.collective_compute("AllReduce", ALU.add, replica_groups=PAIRS,
                             ins=[es_i.opt()], outs=[es_o.opt()]).then_inc(
            _sc, 1)
        g.wait_ge(_sc, _vc)
        return g.dma_start(out=stats[:, 96:128], in_=es_o[:])
    p.step("gpsimd", ares, waits=[("p2c", 32)], incs=[("aes", 16)])

    def vrs(v):
        return v.reciprocal(rs_sb[:], stats[:, 96:128])
    p.step("vector", vrs, waits=[("aes", 16)], incs=[("vrs", 1)])

    for c in range(32):
        hb, cc = (c // 16) * 64, c % 16

        def vtr(te, hb=hb, cc=cc, c=c):
            return te.transpose(ptr[c % 2][:],
                                vag_sb[hb:hb + 64, cc * 128:(cc + 1) * 128],
                                p.ident_bf[hb:hb + 64, hb:hb + 64])
        w = [("agv", 16), ("p2m", 128)]
        if c >= 2:
            w.append(("vtv", c - 1))
        p.step("tensor", vtr, waits=w, incs=[("vtr", 1)])

        def vtv(v, c=c):
            return v.tensor_copy(vT_sb[:, c * 64:(c + 1) * 64], ptr[c % 2][:])
        p.step("vector", vtv, waits=[("vtr", None)], incs=[("vtv", 1)])

    def vts(v):
        for c in range(32):
            inst = v.tensor_scalar(vTs[:, c * 64:(c + 1) * 64],
                                   vT_sb[:, c * 64:(c + 1) * 64],
                                   rs_sb[:, c:c + 1], None, ALU.mult)
        return inst
    p.step("vector", vts, waits=[("vtv", 32), ("vrs", 1)], incs=[("vts", 1)])

    for s in range(4):
        def emit_avm(c, s=s):
            i3 = s * 32 + c

            def avm(te, c=c, s=s):
                return te.matmul(pav[:], vTs[:, c * 64:(c + 1) * 64],
                                 PT2[c % 2][:], start=(c == 0), stop=(c == 31),
                                 skip_group_check=True)
            w = [("p3e", i3 + 1), ("vts", 1)]
            if c == 0 and s >= 1:
                w.append(("avev", s))
            p.step("tensor", avm, waits=w, incs=[("avm", 1)])

        for c in range(32):
            i3 = s * 32 + c
            ksl, hb = kchunk(c)

            def am3(te, ksl=ksl, hb=hb, c=c, s=s):
                return te.matmul(pa[c % 2][:], ksl,
                                 q2[hb:hb + 64, s * 512:(s + 1) * 512],
                                 start=True, stop=True, skip_group_check=True)
            w = []
            if i3 < 2:
                w = [("p2a", 128)]
            else:
                w.append(("p3e", i3 - 1))
            p.step("tensor", am3, waits=w, incs=[("p3m", 1)])

            def pe3(sc, c=c, s=s):
                return sc.activation(PT2[c % 2][:], pa[c % 2][:], AF.Exp,
                                     bias=stats[:, 64 + c:64 + c + 1])
            w = [("p3m", None)]
            if i3 >= 2:
                w.append(("avm", i3 - 1))
            p.step("scalar", pe3, waits=w, incs=[("p3e", 1)])

            if c >= 1:
                emit_avm(c - 1)
        emit_avm(31)

        def avev(sc, s=s):
            return sc.copy(av2[0:64, s * 512:(s + 1) * 512], pav[:])
        p.step("scalar", avev, waits=[("avm", (s + 1) * 32)], incs=[("avev", 1)])

    def avcp(sy):
        return sy.dma_start(out=av2[64:128, :], in_=av2[0:64, :])
    p.step("sync", avcp, waits=[("avev", 4)], incs=[("avcp", 16)])

    def attadd(v):
        for kc in range(NCH):
            inst = v.tensor_add(x_sb[:, kc * NT:(kc + 1) * NT],
                                x_sb[:, kc * NT:(kc + 1) * NT], av2[:])
        return inst
    p.step("vector", attadd, waits=[("avcp", 16)], incs=[("att", 8)])

    p.rel("wqkv_sb", "q2", "k_loc", "v_loc", "kag_sb", "vag_sb", "vT_sb",
          "vTs", "av2", "es2", "stats", "rs_sb", "scw", "ptA", "ptB")
    p.rel("ptr1", "ptr0", "pav", "pa1", "pa0")

    if stop == "attn":
        return _mk_finish(p, x_sb, xo)([("att", 8)])

    psR2 = p.psx("psR2", [128, 512])
    tsF = p.psx("tsF", [128, 128])
    psc = p.psx("psc", [8, 512])
    psCt = p.psx("psCt", [1, 8])
    psWs = p.psx("psWs", [1, 8])

    rwT_sb = p.sbx("rwT_sb", [128, 64], BF16)
    scFM = p.sbx("scFM", [8, NT])
    rt_s = p.sbx("rt_s", [128, 64])
    rcol3 = p.sbx("rcol3", [128, 4])
    mask_tm = p.sbx("mask_tm", [128, 128])
    wr_tm = p.sbx("wr_tm", [128, 128])
    cw_tm2 = p.sbx("cw_tm2", [128, 128])
    crow = p.sbx("crow", [1, 32])
    mwrep = p.sbx("mwrep", [128, 8])
    cwFM = p.sb("cwFM", [8, NT])
    rw3 = rwT.rearrange("(k p) m -> p k m", p=128)

    def ldrw(g):
        return g.dma_start(out=rwT_sb[:], in_=rw3[:, :, :])
    p.step("gpsimd", ldrw, waits=[("att", 8)], incs=[("ldrw", 16)])

    for s in range(4):
        def scm(te, s=s):
            for kc in range(NCH):
                inst = te.matmul(psc[:], rwT_sb[:, kc * 8:(kc + 1) * 8],
                                 xn_bf[:, kc * NT + s * 512:kc * NT + (s + 1) * 512],
                                 start=(kc == 0), stop=(kc == NCH - 1))
            return inst
        w = [("ldrw", 16)]
        if s == 0:
            w += [("avev", 4), ("vtv", 32)]
        else:
            w.append(("sce", s))
        p.step("tensor", scm, waits=w, incs=[("scm", 1)])

        def sce(sc, s=s):
            return sc.copy(scFM[0:8, s * 512:(s + 1) * 512], psc[:])
        p.step("scalar", sce, waits=[("scm", None)], incs=[("sce", 1)])

    for t in range(16):
        def rtt(te, t=t):
            return te.transpose(tsF[:, 0:8], scFM[0:8, t * 128:(t + 1) * 128],
                                p.ident_f[0:8, 0:8])
        w = [("sce", 4)]
        if t == 0:
            w += [("avev", 4), ("vtv", 32)]
        else:
            w.append(("rtv1", t))
        p.step("tensor", rtt, waits=w, incs=[("rtt", 1)])

        def rtv1(v, t=t):
            v.tensor_copy(rt_s[:, 0:8], tsF[:, 0:8])
            v.tensor_reduce(rcol3[:, 0:1], rt_s[:, 0:8], AX.X, ALU.max)
            return v.tensor_scalar(rcol3[:, 1:2], rcol3[:, 0:1], -1.0, None,
                                   ALU.mult)
        w = [("rtt", None)]
        if t >= 1:
            w.append(("rte", t))
        p.step("vector", rtv1, waits=w, incs=[("rtv1", 1)])

        def rte(sc, t=t):
            return sc.activation(rt_s[:, 8:16], rt_s[:, 0:8], AF.Exp,
                                 bias=rcol3[:, 1:2])
        p.step("scalar", rte, waits=[("rtv1", None)], incs=[("rte", 1)])

        def rtv2(v, t=t):
            v.tensor_reduce(rcol3[:, 2:3], rt_s[:, 8:16], AX.X, ALU.add)
            v.reciprocal(rcol3[:, 3:4], rcol3[:, 2:3])
            v.tensor_scalar(rt_s[:, 16:24], rt_s[:, 8:16], rcol3[:, 3:4], None,
                            ALU.mult)
            v.memset(rt_s[:, 24:32], 0.0)
            for e in range(8):
                v.scalar_tensor_tensor(rt_s[:, 24:32], rt_s[:, 16:24],
                                       rt_s[:, 16 + e:17 + e], rt_s[:, 24:32],
                                       ALU.is_lt, ALU.add)
            v.tensor_scalar(mask_tm[:, t * 8:(t + 1) * 8], rt_s[:, 24:32],
                            2.0, None, ALU.is_lt)
            v.tensor_mul(rt_s[:, 16:24], rt_s[:, 16:24],
                         mask_tm[:, t * 8:(t + 1) * 8])
            v.tensor_reduce(rcol3[:, 2:3], rt_s[:, 16:24], AX.X, ALU.add)
            v.reciprocal(rcol3[:, 3:4], rcol3[:, 2:3])
            return v.tensor_scalar(wr_tm[:, t * 8:(t + 1) * 8], rt_s[:, 16:24],
                                   rcol3[:, 3:4], None, ALU.mult)
        p.step("vector", rtv2, waits=[("rte", None)], incs=[("rtv2", 1)])

    for t in range(16):
        def rsm(te, t=t):
            te.matmul(psCt[:], p.ones_col[:], mask_tm[:, t * 8:(t + 1) * 8],
                      start=(t == 0), stop=(t == 15), skip_group_check=True)
            return te.matmul(psWs[:], p.ones_col[:], wr_tm[:, t * 8:(t + 1) * 8],
                             start=(t == 0), stop=(t == 15), skip_group_check=True)
        p.step("tensor", rsm, waits=[("rtv2", t + 1)], incs=[("rsm", 2)])

    def rse(sc):
        sc.copy(crow[0:1, 0:8], psCt[:])
        return sc.copy(crow[0:1, 8:16], psWs[:])
    p.step("scalar", rse, waits=[("rsm", 32)], incs=[("rse", 1)])

    _sd, _vd = p.sem("astd"), p.av("astd", 16)
    _sc, _vc = p.sem("astc"), p.av("astc", 1)
    p.mark("astd", 16)
    p.mark("astc", 1)
    def arst(g, _sd=_sd, _vd=_vd, _sc=_sc, _vc=_vc):
        g.dma_start(out=cst_i[:], in_=crow[0:1, 0:16]).then_inc(_sd, 16)
        g.wait_ge(_sd, _vd)
        g.collective_compute("AllReduce", ALU.add, replica_groups=ALL8,
                             ins=[cst_i.opt()], outs=[cst_o.opt()]).then_inc(
            _sc, 1)
        g.wait_ge(_sc, _vc)
        return g.dma_start(out=crow[0:1, 0:16], in_=cst_o[:])
    p.step("gpsimd", arst, waits=[("rse", 1)], incs=[("ast", 16)])

    def rmw(v):
        v.tensor_scalar(crow[0:1, 16:24], crow[0:1, 0:8], 1.0, None, ALU.max)
        v.reciprocal(crow[0:1, 16:24], crow[0:1, 16:24])
        v.tensor_mul(crow[0:1, 16:24], crow[0:1, 8:16], crow[0:1, 16:24])
        v.tensor_scalar(crow[0:1, 24:32], crow[0:1, 0:8], 1.0, None, ALU.min)
        return v.tensor_mul(crow[0:1, 16:24], crow[0:1, 16:24],
                            crow[0:1, 24:32])
    p.step("vector", rmw, waits=[("ast", 16)], incs=[("rmw", 1)])

    def rmrm(te):
        return te.matmul(psR2[:, 0:8], p.ones_r[:], crow[0:1, 16:24],
                         start=True, stop=True)
    p.step("tensor", rmrm, waits=[("rmw", 1)], incs=[("rmrm", 1)])

    def rmre(sc):
        return sc.copy(mwrep[:], psR2[:, 0:8])
    p.step("scalar", rmre, waits=[("rmrm", 1)], incs=[("rmre", 1)])

    for t in range(16):
        def rcw(v, t=t):
            return v.tensor_mul(cw_tm2[:, t * 8:(t + 1) * 8],
                                mask_tm[:, t * 8:(t + 1) * 8], mwrep[:])
        p.step("vector", rcw, waits=[("rmre", 1)], incs=[("rcw", 1)])

        def rct(te, t=t):
            return te.transpose(tsF[0:8, 0:128], cw_tm2[:, t * 8:(t + 1) * 8],
                                p.ident_f[:])
        w = [("rcw", t + 1)]
        if t >= 1:
            w.append(("rcv", t))
        p.step("tensor", rct, waits=w, incs=[("rct", 1)])

        def rcv(sc, t=t):
            return sc.copy(cwFM[0:8, t * 128:(t + 1) * 128], tsF[0:8, 0:128])
        p.step("scalar", rcv, waits=[("rct", None)], incs=[("rcv", 1)])

    p.rel("rwT_sb", "scFM", "rt_s", "rcol3", "mask_tm", "wr_tm", "cw_tm2")
    p.rel("psWs", "psCt", "psc", "tsF")

    if stop == "route":
        return _mk_finish(p, x_sb, xo)([("rcv", 16), ("att", 8)])

    cwrow = p.sb("cwrow", [1, NT])
    cwrep = p.sb("cwrep", [128, NT])
    xfc2 = [p.sb("xfcA", [128, 4 * NT], BF16), p.sb("xfcB", [128, 4 * NT], BF16)]
    he2 = [p.sb("heA", [128, 4 * NT], BF16), p.sb("heB", [128, 4 * NT], BF16)]

    def xfc_sl(k, a, b):
        return xfc2[k // 4][:, (k % 4) * NT + a:(k % 4) * NT + b]

    def he_sl(k, a, b):
        return he2[k // 4][:, (k % 4) * NT + a:(k % 4) * NT + b]

    gs2 = p.sb("gs2", [128, 2 * 512])
    wsl1 = p.sb("wsl1", [128, 2 * NCH * 128], BF16)
    gwe4 = gwe.rearrange("e (k p) m -> e p k m", p=128)
    uwe4 = uwe.rearrange("e (k p) m -> e p k m", p=128)
    dwe4 = dwe.rearrange("e (k p) m -> e p k m", p=128)

    pb = [p.psx(f"pb{i}", [128, 512]) for i in range(4)]

    wuser = []
    b01u = []
    b23u = []
    nld = [0]
    gsi = [0]

    def _ld(g_src, extra=()):
        j = nld[0]

        def ld(g, g_src=g_src):
            return g.dma_start(out=wsl1[:, (j % 2) * NCH * 128:
                                         (j % 2) * NCH * 128 + NCH * 128],
                               in_=g_src)
        w = list(extra)
        if j >= 2:
            w.append(wuser[j - 2])
        else:
            w.append(("rcv", 16))
        p.step("gpsimd", ld, waits=w, incs=[("ldw", 16)])
        nld[0] += 1
        return j

    for e in range(8):
        def cwr(sy, e=e):
            return sy.dma_start(out=cwrow[:], in_=cwFM[e:e + 1, :])
        w = [("rcv", 16)]
        if e >= 1:
            w.append(("crm", 4 * e))
        p.step("sync", cwr, waits=w, incs=[("cwr", 16)])

        for n in range(4):
            j = 4 * e + n

            def crep(te, e=e, n=n):
                return te.matmul(psR2[:], p.ones_r[:],
                                 cwrow[0:1, n * 512:(n + 1) * 512],
                                 start=True, stop=True)
            w = [("cwr", 16 * (e + 1))]
            w.append(("crv", j) if j >= 1 else ("rmre", 1))
            p.step("tensor", crep, waits=w, incs=[("crm", 1)])

            def crev(sc, e=e, n=n):
                return sc.copy(cwrep[:, n * 512:(n + 1) * 512], psR2[:])
            w = [("crm", None)]
            if e >= 1 and n == 0:
                w.append(("vxf", 8 * e))
            p.step("scalar", crev, waits=w, incs=[("crv", 1)])

        def mkxfc(v, e=e):
            for kc in range(NCH):
                inst = v.tensor_tensor(xfc_sl(kc, 0, NT),
                                       xn_bf[:, kc * NT:(kc + 1) * NT],
                                       cwrep[:], ALU.mult)
            return inst
        p.step("vector", mkxfc, waits=[("crv", 4 * (e + 1)), ("umm", None)],
               incs=[("vxf", NCH)])

        for q in range(4):
            for m in range(NCH):
                mi = q * NCH + m
                j = _ld(gwe4[e][:, :, mi * 128:(mi + 1) * 128])
                for n in range(4):
                    def gmm(te, j=j, n=n):
                        for kc in range(NCH):
                            inst = te.matmul(
                                pb[n % 2][:],
                                wsl1[:, (j % 2) * NCH * 128 + kc * 128:
                                     (j % 2) * NCH * 128 + (kc + 1) * 128],
                                xn_bf[:, kc * NT + n * 512:kc * NT + (n + 1) * 512],
                                start=(kc == 0), stop=(kc == NCH - 1))
                        return inst
                    j01 = len(b01u)
                    w = [("ldw", 16 * (j + 1))]
                    if j01 >= 2:
                        w.append(b01u[j01 - 2])
                    p.step("tensor", gmm, waits=w, incs=[("gmm", 1)])

                    gi = gsi[0]

                    def gev(sc, gi=gi, n=n):
                        return sc.activation(gs2[:, (gi % 2) * 512:
                                                 (gi % 2) * 512 + 512],
                                             pb[n % 2][:], AF.Silu)
                    w = [("gmm", None)]
                    if gi >= 1:
                        w.append(("hev", gi - 1))
                    p.step("scalar", gev, waits=w, incs=[("gev", 1)])
                    b01u.append(("gev", p.c("gev")))

                    def hemul(v, gi=gi, m=m, n=n):
                        return v.tensor_copy(he_sl(m, n * 512, (n + 1) * 512),
                                             gs2[:, (gi % 2) * 512:
                                                 (gi % 2) * 512 + 512])
                    w = [("gev", None)]
                    if m == 0 and n == 0:
                        w.append(("zmm", p.c("zmm")))
                    p.step("vector", hemul, waits=w, incs=[("hev", 1)])
                    gsi[0] += 1
                wuser.append(("gmm", p.c("gmm")))

            for m in range(NCH):
                mi = q * NCH + m
                j = _ld(uwe4[e][:, :, mi * 128:(mi + 1) * 128])
                for n in range(4):
                    def umm(te, j=j, n=n):
                        for kc in range(NCH):
                            inst = te.matmul(
                                pb[2 + n % 2][:],
                                wsl1[:, (j % 2) * NCH * 128 + kc * 128:
                                     (j % 2) * NCH * 128 + (kc + 1) * 128],
                                xfc_sl(kc, n * 512, (n + 1) * 512),
                                start=(kc == 0), stop=(kc == NCH - 1))
                        return inst
                    j23 = len(b23u)
                    w = [("ldw", 16 * (j + 1))]
                    if m == 0 and n == 0:
                        w.append(("vxf", 8 * (e + 1)))
                    if j23 >= 2:
                        w.append(b23u[j23 - 2])
                    p.step("tensor", umm, waits=w, incs=[("umm", 1)])

                    def uev(v, m=m, n=n):
                        sl = he_sl(m, n * 512, (n + 1) * 512)
                        return v.tensor_tensor(sl, sl, pb[2 + n % 2][:], ALU.mult)
                    p.step("vector", uev, waits=[("umm", None), ("hev", None)],
                           incs=[("uev", 1)])
                    b23u.append(("uev", p.c("uev")))
                wuser.append(("umm", p.c("umm")))

            for dc in range(NCH):
                j = _ld(dwe4[e][:, q * NCH:(q + 1) * NCH,
                                dc * 128:(dc + 1) * 128])
                for n in range(4):
                    def zmm(te, j=j, n=n):
                        for kc in range(NCH):
                            inst = te.matmul(
                                pb[n % 2][:],
                                wsl1[:, (j % 2) * NCH * 128 + kc * 128:
                                     (j % 2) * NCH * 128 + (kc + 1) * 128],
                                he_sl(kc, n * 512, (n + 1) * 512),
                                start=(kc == 0), stop=(kc == NCH - 1))
                        return inst
                    j01 = len(b01u)
                    w = [("ldw", 16 * (j + 1))]
                    if dc == 0 and n == 0:
                        w.append(("uev", p.c("uev")))
                    if j01 >= 2:
                        w.append(b01u[j01 - 2])
                    p.step("tensor", zmm, waits=w, incs=[("zmm", 1)])

                    def zev(v, dc=dc, n=n):
                        return v.scalar_tensor_tensor(
                            x_sb[:, dc * NT + n * 512:dc * NT + (n + 1) * 512],
                            pb[n % 2][:], 0.0,
                            x_sb[:, dc * NT + n * 512:dc * NT + (n + 1) * 512],
                            ALU.add, ALU.add)
                    w = [("zmm", None)]
                    if e == 0 and q == 0 and dc == 0 and n == 0:
                        w.append(("att", 8))
                    p.step("vector", zev, waits=w, incs=[("zev", 1)])
                    b01u.append(("zev", p.c("zev")))
                wuser.append(("zmm", p.c("zmm")))

    return _mk_finish(p, x_sb, xo)([("zev", None)])


def _stage_L3(inputs, core, x_cur):
    b, th = core // 2, core % 2
    x = x_cur[b, th * NT:(th + 1) * NT, :]
    tr = lambda a: np.ascontiguousarray(np.asarray(a).T)
    return {
        "x": np.ascontiguousarray(np.asarray(x).T),
        "wq": np.asarray(inputs["L3_attn_Wq"]),
        "wk": np.asarray(inputs["L3_attn_Wk"]),
        "wv": np.asarray(inputs["L3_attn_Wv"]),
        "rwT": tr(inputs["L3_ffn_router_w"]),
        "gwe": np.ascontiguousarray(np.transpose(
            np.asarray(inputs["L3_ffn_gate_w"]), (0, 2, 1))),
        "uwe": np.ascontiguousarray(np.transpose(
            np.asarray(inputs["L3_ffn_up_w"]), (0, 2, 1))),
        "dwe": np.ascontiguousarray(np.transpose(
            np.asarray(inputs["L3_ffn_down_w"]), (0, 2, 1))),
    }


def _stage_L2(inputs, core, x_cur):
    b, th = core // 2, core % 2
    x = x_cur[b, th * NT:(th + 1) * NT, :]
    tr = lambda a: np.ascontiguousarray(np.asarray(a).T)
    return {
        "x": np.ascontiguousarray(x.T),
        "wqT": tr(inputs["L2_attn_Wq"]),
        "wkT": tr(inputs["L2_attn_Wk"]),
        "wvT": tr(inputs["L2_attn_Wv"]),
        "woT": tr(inputs["L2_attn_Wo"]),
        "bq_c": np.ascontiguousarray(inputs["L2_attn_bq"].reshape(8, 128).T),
        "bk_c": np.ascontiguousarray(inputs["L2_attn_bk"].reshape(8, 128).T),
        "bv_c": np.ascontiguousarray(inputs["L2_attn_bv"].reshape(8, 128).T),
        "bo_c": np.ascontiguousarray(inputs["L2_attn_bo"].reshape(8, 128).T),
        "projT": tr(inputs["L2_attn_proj"]),
        "gwT": tr(inputs["L2_ffn_gate_w"]),
        "ebias": inputs["L2_ffn_expert_bias"].reshape(1, 16),
        "wge": np.ascontiguousarray(np.transpose(inputs["L2_ffn_w_gate"], (0, 2, 1))),
        "wue": np.ascontiguousarray(np.transpose(inputs["L2_ffn_w_up"], (0, 2, 1))),
        "wde": np.ascontiguousarray(np.transpose(inputs["L2_ffn_w_down"], (0, 2, 1))),
        "wgs": tr(inputs["L2_ffn_shared_gate_w"]),
        "wus": tr(inputs["L2_ffn_shared_up_w"]),
        "wds": tr(inputs["L2_ffn_shared_down_w"]),
    }


def _mk_finish(p, x_sb, xo):
    def fin(wl):
        if xo is None:
            return wl
        xo3 = xo.rearrange("(k p) n -> k p n", p=128)

        def _st(sy):
            return [sy.dma_start(out=xo3[k], in_=x_sb[:, k * NT:(k + 1) * NT])
                    for k in range(NCH)]
        p.step("sync", _st, waits=wl, incs=[("stf", 16 * NCH)])
        p.step("sync", lambda e: None, waits=[("stf", None)])
        p.emit()
        return p
    return fin



def _stage_L0(inputs, core, x_cur):
    b, th = core // 2, core % 2
    x = x_cur[b, th * NT:(th + 1) * NT, :]
    return {
        "x": np.ascontiguousarray(x.T),
        "pm_col": np.full((128, 1), float(th), np.float32),
        "pmc_col": np.full((128, 1), 1.0 - float(th), np.float32),
        "wiT": np.ascontiguousarray(inputs["L0_attn_Wi"].T),
        "wgT": np.ascontiguousarray(inputs["L0_attn_Wg"].T),
        "woT": np.ascontiguousarray(inputs["L0_attn_Wo"].T),
        "bi_c": np.ascontiguousarray(inputs["L0_attn_bi"].reshape(16, 128).T),
        "bg_c": np.ascontiguousarray(inputs["L0_attn_bg"].reshape(16, 128).T),
        "bo_c": np.ascontiguousarray(inputs["L0_attn_bo"].reshape(8, 128).T),
        "loga_c": np.ascontiguousarray(inputs["L0_attn_log_a"].reshape(16, 128).T),
        "clip": np.stack([inputs["L0_ffn_clip_x"],
                          inputs["L0_ffn_clip_down"]], 1).astype(np.float32),
        "wqg": inputs["L0_ffn_w_gate"],
        "wqu": inputs["L0_ffn_w_up"],
        "wqd": inputs["L0_ffn_w_down"],
    }


def _stage_L1(inputs, core, x_cur):
    b, th = core // 2, core % 2
    x = x_cur[b, th * NT:(th + 1) * NT, :]
    return {
        "x": np.ascontiguousarray(x.T),
        "pm_col": np.full((128, 1), float(th), np.float32),
        "pmc_col": np.full((128, 1), 1.0 - float(th), np.float32),
        "wwT": np.ascontiguousarray(inputs["L1_attn_W_write"].T),
        "wrT": np.ascontiguousarray(inputs["L1_attn_W_read"].T),
        "dec_la": np.array([[inputs["L1_attn_decay"], inputs["L1_attn_log_alpha"]]],
                           np.float32),
        "wg16T": np.ascontiguousarray(inputs["L1_ffn_Wg"].T),
        "w1": inputs["L1_ffn_W1"],
        "w2": inputs["L1_ffn_W2"],
        "ws1": inputs["L1_ffn_Ws1"],
        "ws2": inputs["L1_ffn_Ws2"],
    }


def build_fused(upto=4):
    p = Prog()
    nc = p.nc
    x_in = nc.dram_tensor("x", [D, NT], F32, kind="ExternalInput").ap()
    xo = nc.dram_tensor("xo", [D, NT], F32, kind="ExternalOutput").ap()
    x_sb = p.sb("x_sb", [128, NCH * NT])
    xn_bf = p.sb("xn_bf", [128, NCH * NT], BF16)
    barc = p.sb("barc", [1, 1])
    emit_common(p)
    x3 = x_in.rearrange("(k p) n -> k p n", p=128)
    builders = [build_L0, build_L1, build_L2, build_L3][:upto]
    prev_wl = None
    prev_px = None
    for i, build in enumerate(builders):
        p.px = f"L{i}_"
        if i == 0:
            def ldx(sy):
                return [sy.dma_start(out=x_sb[:, k * NT:(k + 1) * NT],
                                     in_=x3[k]) for k in range(NCH)]
            p.step("sync", ldx, incs=[("x_ok", 16 * NCH)])
        else:
            p.step("vector", lambda v: v.memset(barc[:], 0.0),
                   waits=prev_wl, incs=[("x_ok", 1)])
        p.step("vector", lambda v: v.memset(barc[:], 0.0),
               waits=[("/common", 1)], incs=[("common", 1)])
        if prev_px is not None:
            p.free_sems(prev_px)
        wl = build(fx={"p": p, "x_sb": x_sb, "xn_bf": xn_bf})
        prev_wl = [("/" + p.px + n, p.c(n) if v is None else v)
                   for (n, v) in wl]
        p.rel(*[n for n in list(p._scoped) if n.startswith(p.px)])
        p.rel(*[n for n in list(p._sbal) if n.startswith(p.px)])
        p.reclaim("/xn_bf")
        prev_px = p.px
    p.px = ""

    xo3 = xo.rearrange("(k p) n -> k p n", p=128)

    def store(sy):
        return [sy.dma_start(out=xo3[k], in_=x_sb[:, k * NT:(k + 1) * NT])
                for k in range(NCH)]
    p.step("sync", store, waits=prev_wl, incs=[("stf", 16 * NCH)])
    p.step("sync", lambda e: None, waits=[("stf", None)])
    p.emit()
    return p


def _stage_fused(inputs, core, x_cur, upto=4):
    stages = [_stage_L0, _stage_L1, _stage_L2, _stage_L3][:upto]
    m = {}
    for i, stage in enumerate(stages):
        sm = stage(inputs, core, x_cur)
        for k, v in sm.items():
            if k == "x":
                if i == 0:
                    m["x"] = v
            else:
                m[f"L{i}_{k}"] = v
    return m


def kernel(**inputs):
    x_cur = np.asarray(inputs["L0_attn_x"], np.float32)
    p = build_fused()
    in_maps = [_stage_fused(inputs, c, x_cur) for c in range(8)]
    res = run_bass_kernel_spmd(p.nc, in_maps, core_ids=list(range(8)))
    p.close()
    out = np.empty((4, 4096, 1024), np.float32)
    for c in range(8):
        b, th = c // 2, c % 2
        out[b, th * NT:(th + 1) * NT, :] = res.results[c]["xo"].T
    return out


def kernel_multi(**inputs):
    x_cur = np.asarray(inputs["L0_attn_x"], np.float32)
    layers = [(build_L0, _stage_L0), (build_L1, _stage_L1),
              (build_L2, _stage_L2), (build_L3, _stage_L3)]
    for build, stage in layers:
        p = build()
        in_maps = [stage(inputs, c, x_cur) for c in range(8)]
        res = run_bass_kernel_spmd(p.nc, in_maps, core_ids=list(range(8)))
        p.close()
        out = np.empty((4, 4096, 1024), np.float32)
        for c in range(8):
            b, th = c // 2, c % 2
            out[b, th * NT:(th + 1) * NT, :] = res.results[c]["xo"].T
        x_cur = out
    return x_cur


# version 05
# speedup vs baseline: 1.0237x; 1.0015x over previous
import numpy as np
from contextlib import ExitStack

import concourse.bass as bass
import concourse.mybir as mybir
from concourse.bass_utils import run_bass_kernel_spmd

F32 = mybir.dt.float32
BF16 = mybir.dt.bfloat16
AF = mybir.ActivationFunctionType
ALU = mybir.AluOpType
AX = mybir.AxisListType

PAIRS = [[0, 1], [2, 3], [4, 5], [6, 7]]
ALL8 = [[0, 1, 2, 3, 4, 5, 6, 7]]

NT = 2048
D = 1024
NCH = D // 128
MAGIC = 12582912.0
C22 = 4194304.0

ENGS = ("sync", "gpsimd", "vector", "scalar", "tensor")

_EMIT = {"tensor_copy", "tensor_tensor", "tensor_add", "tensor_sub", "tensor_mul",
         "tensor_max", "tensor_scalar", "tensor_single_scalar", "tensor_scalar_add",
         "tensor_scalar_sub", "tensor_scalar_mul", "tensor_scalar_max",
         "tensor_scalar_min", "tensor_relu", "tensor_reduce", "reduce_max",
         "reduce_sum", "scalar_tensor_tensor", "tensor_tensor_scan", "memset",
         "memzero", "reciprocal", "select", "copy_predicated", "activation",
         "sign", "square", "sqrt", "mul", "copy", "add", "bn_stats", "bn_aggr",
         "max", "max_index", "pool", "pool_avg", "pool_max", "stream_shuffle",
         "match_replace", "transpose", "engine_nop"}


class _DrainWrap:

    def __init__(self, e):
        object.__setattr__(self, "_e", e)

    def __getattr__(self, name):
        e = object.__getattribute__(self, "_e")
        a = getattr(e, name)
        if name in _EMIT:
            def w(*args, **kw):
                e.drain()
                return a(*args, **kw)
            return w
        return a


class Prog:

    def __init__(self):
        self.nc = bass.Bass("TRN2", target_bir_lowering=False, debug=False,
                            detect_race_conditions=False)
        self.es = ExitStack()
        self.px = ""
        self._phys = {}
        self._physcnt = {}
        self._physfree = []
        self._map = {}
        self.cnt = {}
        self.q = {e: [] for e in ENGS}
        self._scoped = {}
        b = (self.nc.sbuf_base + 255) // 256 * 256
        self._free = [(b, self.nc.sbuf_top)]
        self._sbal = {}
        self._sbhist = {}
        self._psorder = []

    def _n(self, name):
        return name[1:] if name.startswith("/") else self.px + name

    def _alloc_bytes(self, n):
        best = None
        for i, (lo, hi) in enumerate(self._free):
            if hi - lo >= n and (best is None or hi - lo < self._free[best][1] - self._free[best][0]):
                best = i
        if best is None:
            raise MemoryError(f"SBUF pool exhausted: need {n}, free={self._free}")
        lo, hi = self._free[best]
        if hi - lo == n:
            self._free.pop(best)
        else:
            self._free[best] = (lo + n, hi)
        return lo

    def _free_bytes(self, lo, n):
        self._free.append((lo, lo + n))
        self._free.sort()
        merged = []
        for seg in self._free:
            if merged and merged[-1][1] >= seg[0]:
                merged[-1] = (merged[-1][0], max(merged[-1][1], seg[1]))
            else:
                merged.append(list(seg))
        self._free = [tuple(x) for x in merged]

    def sb(self, name, shape, dtype=F32):
        fn = self._n(name)
        isz = 2 if dtype == BF16 else 4
        n = 1
        for d in shape[1:]:
            n *= d
        n = (n * isz + 255) // 256 * 256
        off = self._alloc_bytes(n)
        self._sbal[fn] = (off, n)
        self._sbhist[fn] = (off, n)
        return self.nc.alloc_sbuf_tensor_at(fn, shape, dtype, offset=off)

    sbx = sb

    def reclaim(self, name):
        fn = self._n(name)
        if fn in self._sbal:
            return
        off, n = self._sbhist[fn]
        for i, (lo, hi) in enumerate(self._free):
            if lo <= off and off + n <= hi:
                self._free.pop(i)
                if lo < off:
                    self._free.append((lo, off))
                if off + n < hi:
                    self._free.append((off + n, hi))
                self._free.sort()
                self._sbal[fn] = (off, n)
                return
        raise MemoryError(f"reclaim {fn}: bytes not free")

    def psx(self, name, shape, dtype=F32):
        fn = self._n(name)
        cm = self.nc.psum_tensor(fn, shape, dtype)
        t = cm.__enter__()
        self._scoped[fn] = cm
        self._psorder.append(fn)
        return t

    def rel(self, *names):
        fns = []
        for name in names:
            fn = self._n(name)
            if fn not in self._scoped and fn not in self._sbal and (
                    name in self._scoped or name in self._sbal):
                fn = name
            fns.append(fn)
        ps = [n for n in fns if n in self._scoped]
        for name in sorted(ps, key=lambda n: -self._psorder.index(n)):
            cm = self._scoped.pop(name)
            cm.__exit__(None, None, None)
        for name in fns:
            if name in self._sbal:
                lo, n = self._sbal.pop(name)
                self._free_bytes(lo, n)

    def ps(self, name, shape, dtype=F32):
        return self.es.enter_context(
            self.nc.psum_tensor(self._n(name), shape, dtype))

    def dram(self, name, shape, dtype=F32, **kw):
        return self.nc.dram_tensor(self._n(name), shape, dtype,
                                   kind="Internal", **kw).ap()

    def sem(self, name):
        fn = self._n(name)
        if fn not in self._map:
            if self._physfree:
                ph = self._physfree.pop(0)
            else:
                ph = f"S{len(self._phys)}"
                self._phys[ph] = self.es.enter_context(self.nc.semaphore(ph))
            self._map[fn] = (ph, self._physcnt.get(ph, 0))
            self.cnt.setdefault(fn, 0)
        return self._phys[self._map[fn][0]]

    def av(self, name, v):
        self.sem(name)
        return self._map[self._n(name)][1] + v

    def mark(self, name, k):
        fn = self._n(name)
        self.sem(name)
        self.cnt[fn] = self.cnt.get(fn, 0) + k
        ph = self._map[fn][0]
        self._physcnt[ph] = self._physcnt.get(ph, 0) + k

    def free_sems(self, prefix):
        for fn in list(self._map):
            if fn.startswith(prefix):
                ph, _ = self._map.pop(fn)
                self._physfree.append(ph)

    def c(self, name):
        return self.cnt.get(self._n(name), 0)

    def step(self, eng, body, waits=(), incs=()):
        wv = []
        for (n, v) in waits:
            s = self.sem(n)
            fn = self._n(n)
            lv = self.cnt.get(fn, 0) if v is None else v
            wv.append((s, self._map[fn][1] + lv, lv))
        iv = []
        for (n, k) in incs:
            s = self.sem(n)
            self.mark(n, k)
            iv.append((s, k))

        def run(e):
            for s, av_, lv in wv:
                if lv > 0:
                    e.wait_ge(s, av_)
            inst = body(_DrainWrap(e) if eng in ("vector", "scalar") else e)
            if isinstance(inst, list):
                for s, k in iv:
                    kk = k // len(inst)
                    for it in inst:
                        it.then_inc(s, kk)
            elif inst is not None:
                for s, k in iv:
                    inst.then_inc(s, k)
        self.q[eng].append(run)

    def emit(self):
        with self.nc.Block() as blk:
            for e in ENGS:
                fns = self.q[e]
                if fns:
                    getattr(blk, e)(lambda eng, fns=fns: [f(eng) for f in fns])
        self.q = {e: [] for e in ENGS}

    def close(self):
        for name in sorted(self._scoped, key=lambda n: -self._psorder.index(n)):
            self._scoped[name].__exit__(None, None, None)
        self._scoped = {}
        self.es.close()



def emit_common(p):
    p.ident_bf = p.sb("ident_bf", [128, 128], BF16)
    p.ident_f = p.sb("ident_f", [128, 128], F32)
    p.ones_r = p.sb("ones_r", [1, 128], F32)
    p.ones_col = p.sb("ones_col", [128, 1], F32)
    p.iota_f = p.sb("iota_f", [128, 128], F32)
    p.iota_p = p.sb("iota_p", [128, 1], F32)

    def io(g):
        g.iota(p.iota_f[:], pattern=[[1, 128]], base=0, channel_multiplier=0,
               allow_small_or_imprecise_dtypes=True)
        return g.iota(p.iota_p[:], pattern=[[0, 1]], base=0, channel_multiplier=1,
                      allow_small_or_imprecise_dtypes=True)
    p.step("gpsimd", io, incs=[("io", 1)])

    def mk(v):
        v.tensor_scalar(p.ident_bf[:], p.iota_f[:], p.iota_p[:], None, ALU.is_equal)
        v.tensor_scalar(p.ident_f[:], p.iota_f[:], p.iota_p[:], None, ALU.is_equal)
        v.memset(p.ones_r[:], 1.0)
        return v.memset(p.ones_col[:], 1.0)
    p.step("vector", mk, waits=[("io", None)], incs=[("common", 1)])


def emit_ln(p, tag, x_sb, xn_bf, x_ok, want_absmax=False):
    t = tag
    sumx = p.sb(t + "sumx", [1, NT])
    sumsq = p.sb(t + "sumsq", [1, NT])
    rows = p.sb(t + "rows", [1, 2 * NT])
    A_r, B_r = rows[0:1, 0:NT], rows[0:1, NT:2 * NT]
    mu_r, ms_r = sumx[0:1, :], sumsq[0:1, :]
    sqs = p.sbx(t + "sqs", [128, 2 * 512])
    ab_b = p.sbx(t + "ab", [128, 2 * NT])
    tmp = p.sbx(t + "tmp", [128, NT])
    mcol = p.sb(t + "mcol", [128, 1])
    psA = p.psx(t + "psA", [1, 512])
    psB = p.psx(t + "psB", [1, 512])
    psR = p.psx(t + "psR", [128, 512])
    out = {"A_r": A_r, "B_r": B_r, "mu_r": mu_r, "ms_r": ms_r}
    psums = [t + "psA", t + "psB", t + "psR"]
    if want_absmax:
        out["runmax"] = p.sb(t + "rmax", [128, 1])
        psT = p.psx(t + "psT", [1, 128])
        psums.append(t + "psT")

    for n in range(4):
        def mmx(te, n=n):
            for k in range(NCH):
                inst = te.matmul(psA[:], p.ones_col[:],
                                 x_sb[:, k * NT + n * 512:k * NT + (n + 1) * 512],
                                 start=(k == 0), stop=(k == NCH - 1))
            return inst
        p.step("tensor", mmx,
               waits=[(x_ok, None), ("common", 1)] + ([(t + "sev", None)] if n else []),
               incs=[(t + "smm", 1)])
        for k in range(NCH):
            idx = n * NCH + k

            def sq(sc, n=n, k=k, idx=idx):
                return sc.activation(sqs[:, (k % 2) * 512:(k % 2) * 512 + 512],
                                     x_sb[:, k * NT + n * 512:k * NT + (n + 1) * 512],
                                     AF.Square)
            w = [(x_ok, None)]
            if idx >= 2:
                w.append((t + "smm2", idx - 1))
            p.step("scalar", sq, waits=w, incs=[(t + "sq", 1)])

            def mmq(te, n=n, k=k):
                return te.matmul(psB[:], p.ones_col[:],
                                 sqs[:, (k % 2) * 512:(k % 2) * 512 + 512],
                                 start=(k == 0), stop=(k == NCH - 1))
            w = [(t + "sq", idx + 1)]
            if n and k == 0:
                w.append((t + "sev", None))
            p.step("tensor", mmq, waits=w, incs=[(t + "smm2", 1)])

        def ev(sc, n=n):
            sc.copy(sumx[0:1, n * 512:(n + 1) * 512], psA[:])
            return sc.copy(sumsq[0:1, n * 512:(n + 1) * 512], psB[:])
        p.step("scalar", ev, waits=[(t + "smm", None), (t + "smm2", None)],
               incs=[(t + "sev", 1)])

    def rows_ve(v):
        v.tensor_scalar(mu_r, sumx[:], 1.0 / D, None, ALU.mult)
        v.tensor_scalar(ms_r, sumsq[:], 1.0 / D, None, ALU.mult)
        v.tensor_mul(A_r, mu_r, mu_r)
        v.tensor_sub(A_r, ms_r, A_r)
        return v.tensor_scalar(A_r, A_r, 1e-5, None, ALU.add)
    p.step("vector", rows_ve, waits=[(t + "sev", None)], incs=[(t + "var", 1)])

    def rows_sc(sc):
        return sc.activation(A_r, A_r, AF.Ln)
    p.step("scalar", rows_sc, waits=[(t + "var", None)], incs=[(t + "Aln", 1)])

    def rows_ve2(v):
        return v.tensor_scalar(A_r, A_r, -0.5, None, ALU.mult)
    p.step("vector", rows_ve2, waits=[(t + "Aln", None)], incs=[(t + "Am", 1)])

    def rows_sc2(sc):
        return sc.activation(A_r, A_r, AF.Exp)
    p.step("scalar", rows_sc2, waits=[(t + "Am", None)], incs=[(t + "A", 1)])

    def rows_b(v):
        return v.tensor_mul(B_r, mu_r, A_r)
    p.step("vector", rows_b, waits=[(t + "A", None)], incs=[(t + "B", 1)])

    for j, row in ((0, A_r), (1, B_r)):
        for n in range(4):
            def rep(te, n=n, row=row):
                return te.matmul(psR[:], p.ones_r[:], row[0:1, n * 512:(n + 1) * 512],
                                 start=True, stop=True)
            p.step("tensor", rep, waits=[(t + "B", None), (t + "repev", None)],
                   incs=[(t + "rep", 1)])

            def repev(sc, j=j, n=n):
                return sc.copy(ab_b[:, j * NT + n * 512:j * NT + (n + 1) * 512], psR[:])
            p.step("scalar", repev, waits=[(t + "rep", None)], incs=[(t + "repev", 1)])

    for k in range(NCH):
        def xn(v, k=k):
            v.tensor_mul(tmp[:], x_sb[:, k * NT:(k + 1) * NT], ab_b[:, 0:NT])
            xk = xn_bf[:, k * NT:(k + 1) * NT]
            inst = v.tensor_tensor(xk, tmp[:], ab_b[:, NT:2 * NT], ALU.subtract)
            if want_absmax:
                if k == 0:
                    inst = v.tensor_reduce(out["runmax"][:], xk, AX.X, ALU.max,
                                           apply_absolute_value=True)
                else:
                    v.tensor_reduce(mcol[:], xk, AX.X, ALU.max,
                                    apply_absolute_value=True)
                    inst = v.tensor_max(out["runmax"][:], out["runmax"][:], mcol[:])
            return inst
        p.step("vector", xn, waits=[(t + "repev", None)], incs=[(t + "xn", 1)])

    if want_absmax:
        def trm(te):
            return te.transpose(psT[:], out["runmax"][:], p.ident_f[:])
        p.step("tensor", trm, waits=[(t + "xn", None)], incs=[(t + "tr", 1)])
        out["gmax"] = p.sb(t + "gmax", [1, 1])

        def redm(v):
            return v.tensor_reduce(out["gmax"][:], psT[:], AX.X, ALU.max)
        p.step("vector", redm, waits=[(t + "tr", None)], incs=[(t + "gm", 1)])
    out["scoped"] = [t + "sqs", t + "ab", t + "tmp"]
    out["psums"] = psums
    return out


def emit_armax_scalar(p, tag, src_sb, src_sem, dst_sb):
    t = tag
    ci = p.dram(t + "ci", [1, 1])
    co = p.dram(t + "co", [1, 1])

    s1, v1 = p.sem(t + "d1"), p.av(t + "d1", 16)
    s2, v2 = p.sem(t + "cc"), p.av(t + "cc", 1)
    p.mark(t + "d1", 16)
    p.mark(t + "cc", 1)

    def go(g, s1=s1, v1=v1, s2=s2, v2=v2):
        g.dma_start(out=ci[:], in_=src_sb[:]).then_inc(s1, 16)
        g.wait_ge(s1, v1)
        g.collective_compute("AllReduce", ALU.max, replica_groups=ALL8,
                             ins=[ci.opt()], outs=[co.opt()]).then_inc(s2, 1)
        g.wait_ge(s2, v2)
        return g.dma_start(out=dst_sb[:], in_=co[:])
    p.step("gpsimd", go, waits=[(src_sem, None)], incs=[(t + "out", 16)])
    return t + "out"



def build_L0(stop="full", fx=None):
    p = Prog() if fx is None else fx["p"]
    nc = p.nc
    inp = {}

    def ein(name, shape, dtype=F32):
        inp[name] = nc.dram_tensor(p.px + name, shape, dtype,
                                   kind="ExternalInput").ap()
        return inp[name]

    if fx is None:
        x_in = ein("x", [D, NT])
    ein("pm_col", [128, 1])
    ein("pmc_col", [128, 1])
    wiT = ein("wiT", [D, 2048])
    wgT = ein("wgT", [D, 2048])
    woT = ein("woT", [2048, D])
    ein("bi_c", [128, 16]); ein("bg_c", [128, 16]); ein("bo_c", [128, 8])
    ein("loga_c", [128, 16]); ein("clip", [1, 2])
    wqg = ein("wqg", [2816, D])
    wqu = ein("wqu", [2816, D])
    wqd = ein("wqd", [D, 2816])
    if fx is None:
        xo = nc.dram_tensor("xo", [D, NT], F32, kind="ExternalOutput").ap()

    u_spill = p.dram("u_spill", [2048, NT])
    hg_spill = p.dram("hg_spill", [2048, NT], BF16)
    h2_spill = p.dram("h2_spill", [2816, NT])
    wgq_t = p.dram("wgq_t", [D, 2816], BF16)
    wuq_t = p.dram("wuq_t", [D, 2816], BF16)
    wdq_t = p.dram("wdq_t", [2816, D], BF16)
    cc_ci = p.dram("cc_ci", [128, 16])
    cc_co = p.dram("cc_co", [128, 16])

    if fx is None:
        x_sb = p.sb("x_sb", [128, NCH * NT])
        xn_bf = p.sbx("xn_bf", [128, NCH * NT], BF16)
    else:
        x_sb, xn_bf = fx["x_sb"], fx["xn_bf"]
    small = p.sb("small", [128, 96])
    bi_s, bg_s = small[:, 0:16], small[:, 16:32]
    la_s, bo_s = small[:, 32:48], small[:, 48:56]
    a_s, lna_s = small[:, 64:80], small[:, 80:96]
    pm_s = p.sb("pm_s", [128, 2])
    clip_s = p.sb("clip_s", [1, 2])
    onecol = p.sb("onecol", [1, 9])
    carr = p.sb("carr", [128, 16])
    cpm = p.sb("cpm", [128, 16])
    scol_g = p.sb("scol_g", [128, 22])
    scol_u = p.sb("scol_u", [128, 22])
    scol_d = p.sb("scol_d", [128, 8])
    sxr_g = p.sb("sxr_g", [128, 22])
    sxr_u = p.sb("sxr_u", [128, 22])
    sxr_d = p.sb("sxr_d", [128, 8])
    sx_col = p.sb("sx_col", [128, 2])
    sh_col = p.sb("sh_col", [128, 2])
    rm2 = p.sb("rm2", [128, 1])
    rmt = p.sb("rmt", [128, 1])
    gm2 = p.sb("gm2", [1, 1])

    scrA = p.sbx("scrA", [128, NT])
    scrB = p.sbx("scrB", [128, NT])
    scrC = p.sbx("scrC", [128, NT])
    scrD = p.sbx("scrD", [128, NT])
    hg_sb = p.sbx("hg_sb", [128, NT], BF16)
    wi_sl = p.sbx("wi_sl", [128, 2 * 8 * 128], BF16)
    rcol = p.sb("rcol", [128, 1])

    pb = [p.psx(f"pb{i}", [128, 512]) for i in range(4)]


    def _finish(wl):
        if fx is not None:
            return wl
        xo3_ = xo.rearrange("(k p) n -> k p n", p=128)

        def _st(sy):
            return [sy.dma_start(out=xo3_[k], in_=x_sb[:, k * NT:(k + 1) * NT])
                    for k in range(NCH)]
        p.step("sync", _st, waits=wl, incs=[("st", 16 * NCH)])
        p.step("sync", lambda e: None, waits=[("st", None)])
        p.emit()
        return p

    if fx is None:
        emit_common(p)

        x3 = x_in.rearrange("(k p) n -> k p n", p=128)

        def ldx(sy):
            return [sy.dma_start(out=x_sb[:, k * NT:(k + 1) * NT], in_=x3[k])
                    for k in range(NCH)]
        p.step("sync", ldx, incs=[("x_ok", 16 * NCH)])

    def ldsmall(sy):
        return [sy.dma_start(out=bi_s, in_=inp["bi_c"][:]),
                sy.dma_start(out=bg_s, in_=inp["bg_c"][:]),
                sy.dma_start(out=la_s, in_=inp["loga_c"][:]),
                sy.dma_start(out=bo_s, in_=inp["bo_c"][:]),
                sy.dma_start(out=clip_s[:], in_=inp["clip"][:]),
                sy.dma_start(out=pm_s[:, 0:1], in_=inp["pm_col"][:]),
                sy.dma_start(out=pm_s[:, 1:2], in_=inp["pmc_col"][:])]
    p.step("sync", ldsmall, incs=[("sm_ok", 16 * 7)])

    def acols(sc):
        sc.activation(a_s, la_s, AF.Sigmoid)
        return sc.activation(lna_s, a_s, AF.Ln)
    p.step("scalar", acols, waits=[("sm_ok", None)], incs=[("acols", 1)])

    ln = emit_ln(p, "ln0", x_sb, xn_bf, "x_ok", want_absmax=True)
    g1 = onecol[0:1, 3:4]
    emit_armax_scalar(p, "g1", ln["gmax"], "ln0gm", g1)
    p.rel(*ln["scoped"])

    if stop == "ln":
        return _finish([("ln0xn", None), ("g1out", None)])

    p.rel(*ln["psums"])
    psTR = [p.psx("psTRa", [128, 128], BF16), p.psx("psTRb", [128, 128], BF16)]
    wnat = p.sbx("wnat", [128, 2816])
    wint = p.sbx("wint", [128, 2816], BF16)
    wT_st = p.sbx("wT_st", [128, 2816], BF16)

    def emit_quant(wsrc, wdst, scol, nchk, cw, qt, prev_tev=None, prev_st=None):
        w3 = wsrc.rearrange("(c p) m -> c p m", p=128)
        wd3 = wdst.rearrange("(j p) m -> j p m", p=128)
        dbuf = 2 * cw <= 2816
        chunks = []
        for c in range(nchk):
            def chunk(c=c):
                sl = (c % 2) * cw if dbuf else 0

                def ldn(sc, c=c, sl=sl):
                    return sc.dma_start(out=wnat[:, sl:sl + cw], in_=w3[c])
                if c >= 2 or (c == 1 and not dbuf):
                    w = [(qt + "vq", c - 1 if dbuf else c)]
                elif c == 1:
                    w = ([("qgvq", 22)] if qt == "qu"
                         else [("quvq", 22)] if qt == "qd" else [])
                elif qt == "qg":
                    w = [("ln0xn", 8), ("ln0sev", 4), ("ln0repev", 8),
                         ("ln0gm", 1)]
                elif qt == "qu":
                    w = [("qgvq", 22)]
                else:
                    w = [("quvq", 22)]
                lds = qt + ("ld" if c % 2 == 0 else "ldo")
                p.step("scalar" if c % 2 == 0 else "gpsimd", ldn,
                       waits=w, incs=[(lds, 16)])

                def vq(v, c=c, sl=sl):
                    wv = wnat[:, sl:sl + cw]
                    v.tensor_reduce(scol[:, c:c + 1], wv, AX.X, ALU.max,
                                    apply_absolute_value=True)
                    v.reciprocal(rcol[:], scol[:, c:c + 1])
                    v.tensor_scalar(scol[:, c:c + 1], scol[:, c:c + 1],
                                    1.0 / 127.0, None, ALU.mult)
                    v.tensor_scalar(wv, wv, rcol[:], 127.0, ALU.mult, ALU.mult)
                    return v.tensor_scalar(wint[:, sl:sl + cw], wv, MAGIC, MAGIC,
                                           ALU.add, ALU.subtract)
                wts = [(qt + ("ld" if c % 2 == 0 else "ldo"),
                        16 * (c // 2 + 1))]
                if dbuf and c >= 2:
                    wts.append((qt + "tev", (c - 1) * (cw // 128)))
                elif not dbuf and c >= 1:
                    wts.append((qt + "tev", c * (cw // 128)))
                p.step("vector", vq, waits=wts, incs=[(qt + "vq", 1)])

                for j in range(cw // 128):
                    gidx = c * (cw // 128) + j
                    bk = gidx

                    def tr(te, c=c, j=j, bk=bk, sl=sl):
                        return te.transpose(psTR[bk % 2][:],
                                            wint[:, sl + j * 128:sl + (j + 1) * 128],
                                            p.ident_bf[:])
                    w = [(qt + "vq", c + 1)]
                    if gidx >= 2:
                        w.append((qt + "tev", gidx - 2 + 1))
                    elif prev_tev is not None:
                        w.append(prev_tev)
                    p.step("tensor", tr, waits=w, incs=[(qt + "tr", 1)])

                    def tev(sc, j=j, bk=bk):
                        return sc.copy(wT_st[:, j * 128:(j + 1) * 128],
                                       psTR[bk % 2][:])
                    w = [(qt + "tr", None)]
                    if j == 0:
                        if c >= 1:
                            w.append((qt + "st", 16 * (cw // 128) * c))
                        elif prev_st is not None:
                            w.append(prev_st)
                    p.step("scalar", tev, waits=w, incs=[(qt + "tev", 1)])

                def st(sy, c=c):
                    return [sy.dma_start(out=wd3[j][:, c * 128:(c + 1) * 128],
                                         in_=wT_st[:, j * 128:(j + 1) * 128])
                            for j in range(cw // 128)]
                p.step("sync", st, waits=[(qt + "tev", None)],
                       incs=[(qt + "st", 16 * (cw // 128))])
            chunks.append(chunk)
        return chunks

    qchunks = (emit_quant(wqg, wgq_t, scol_g, 22, 1024, "qg")
               + emit_quant(wqu, wuq_t, scol_u, 22, 1024, "qu",
                            prev_tev=("qgtev", 176), prev_st=("qgst", 16 * 176))
               + emit_quant(wqd, wdq_t, scol_d, 8, 2816, "qd",
                            prev_tev=("qutev", 176), prev_st=("qust", 16 * 176)))

    u_sb, apow = scrA, scrB
    wi3 = wiT.rearrange("(k p) h -> p k h", p=128)
    for g in range(16):
        buf = (g % 2) * 8 * 128

        def ldwi(gp, g=g, buf=buf):
            return gp.dma_start(out=wi_sl[:, buf:buf + 8 * 128],
                                in_=wi3[:, :, g * 128:(g + 1) * 128])
        p.step("gpsimd", ldwi,
               waits=([("mm_u", 4 * (g - 1))] if g >= 2 else []),
               incs=[("ld_wi", 16)])

        for n in range(4):
            def umm(te, g=g, n=n, buf=buf):
                for k in range(NCH):
                    inst = te.matmul(pb[n][:],
                                     wi_sl[:, buf + k * 128:buf + (k + 1) * 128],
                                     xn_bf[:, k * NT + n * 512:k * NT + (n + 1) * 512],
                                     start=(k == 0), stop=(k == NCH - 1))
                return inst
            w = [("ld_wi", 16 * (g + 1)), ("ln0xn", None)]
            if g >= 1:
                w.append(("ev_u", 4 * (g - 1) + n + 1))
            p.step("tensor", umm, waits=w, incs=[("mm_u", 1)])

            def uev(sc, g=g, n=n):
                return sc.activation(u_sb[:, n * 512:(n + 1) * 512], pb[n][:],
                                     AF.Identity, bias=bi_s[:, g:g + 1])
            w = [("mm_u", None), ("sm_ok", None)]
            if g >= 1 and n == 0:
                w += [("sy_u", 16 * g), ("ve_ca", g)]
            p.step("scalar", uev, waits=w, incs=[("ev_u", 1)])

        def spill(sy, g=g):
            return sy.dma_start(out=u_spill[g * 128:(g + 1) * 128, :], in_=u_sb[:])
        p.step("sync", spill, waits=[("ev_u", None)], incs=[("sy_u", 16)])

        def apwio(gp, g=g):
            return gp.iota(apow[:], pattern=[[-1, NT]], base=NT - 1, channel_multiplier=0,
                           allow_small_or_imprecise_dtypes=True)
        w = []
        if g >= 1:
            w.append(("ve_ca", g))
        p.step("gpsimd", apwio, waits=w, incs=[("io_ap", 1)])

        def apw(sc, g=g):
            return sc.activation(apow[:], apow[:], AF.Exp, scale=lna_s[:, g:g + 1])
        w = [("io_ap", None), ("acols", None)]
        if g >= 1:
            w.append(("ve_ca", g))
        p.step("scalar", apw, waits=w, incs=[("sc_ap", 1)])

        def carry(v, g=g):
            v.tensor_mul(apow[:], u_sb[:], apow[:])
            return v.tensor_reduce(carr[:, g:g + 1], apow[:], AX.X, ALU.add)
        p.step("vector", carry, waits=[("sc_ap", None), ("ev_u", None)],
               incs=[("ve_ca", 1)])

        for _ in range(4):
            if qchunks:
                qchunks.pop(0)()

    def contrib(v):
        return v.tensor_scalar(cpm[:], carr[:], pm_s[:, 1:2], None, ALU.mult)
    p.step("vector", contrib, waits=[("ve_ca", None), ("sm_ok", None)],
           incs=[("ve_cc", 1)])

    _sd, _vd = p.sem("cg_d"), p.av("cg_d", 16)
    _sc, _vc = p.sem("cg_c"), p.av("cg_c", 1)
    p.mark("cg_d", 16)
    p.mark("cg_c", 1)
    def ccarry(g, _sd=_sd, _vd=_vd, _sc=_sc, _vc=_vc):
        g.dma_start(out=cc_ci[:], in_=cpm[:]).then_inc(_sd, 16)
        g.wait_ge(_sd, _vd)
        g.collective_compute("AllReduce", ALU.add, replica_groups=PAIRS,
                             ins=[cc_ci.opt()], outs=[cc_co.opt()]).then_inc(_sc, 1)
        g.wait_ge(_sc, _vc)
        return g.dma_start(out=carr[:], in_=cc_co[:])
    p.step("gpsimd", ccarry, waits=[("ve_cc", None)], incs=[("cg_o", 16)])

    def cpm_mk(v):
        return v.tensor_scalar(cpm[:], carr[:], pm_s[:, 0:1], None, ALU.mult)
    p.step("vector", cpm_mk, waits=[("cg_o", None)], incs=[("ve_cpm", 1)])

    if stop == "loopA":
        return _finish([("sy_u", None), ("cg_o", None), ("ve_cpm", None),
                        ("g1out", None)])

    p.rel("wnat", "wint", "wT_st")

    if stop == "quant":
        return _finish([("qgst", None), ("qust", None), ("qdst", None),
                        ("cg_o", None), ("ve_cpm", None), ("g1out", None)])

    scrE = p.sbx("scrE", [128, NT])
    u_rb = [scrA, scrE]
    gate_sb, h_sb, arep = scrC, scrD, scrB
    wg3 = wgT.rearrange("(k p) h -> p k h", p=128)
    for g in range(16):
        buf = (g % 2) * 8 * 128

        def ldwg(gp, g=g, buf=buf):
            return gp.dma_start(out=wi_sl[:, buf:buf + 8 * 128],
                                in_=wg3[:, :, g * 128:(g + 1) * 128])
        p.step("gpsimd", ldwg,
               waits=[("mm_g", 4 * (g - 1))] if g >= 2 else [("mm_u", 64)],
               incs=[("ld_wg", 16)])

        def ldu(sy, g=g):
            return sy.dma_start(out=u_rb[g % 2][:],
                                in_=u_spill[g * 128:(g + 1) * 128, :])
        w = [("sy_u", 16 * 16), ("ve_ca", 16)]
        if g >= 2:
            w.append(("ve_scan", g - 1))
        p.step("sync", ldu, waits=w, incs=[("sy_ur", 16)])

        for n in range(4):
            def gmm(te, g=g, n=n, buf=buf):
                for k in range(NCH):
                    inst = te.matmul(pb[n][:],
                                     wi_sl[:, buf + k * 128:buf + (k + 1) * 128],
                                     xn_bf[:, k * NT + n * 512:k * NT + (n + 1) * 512],
                                     start=(k == 0), stop=(k == NCH - 1))
                return inst
            w = [("ld_wg", 16 * (g + 1))]
            if g >= 1:
                w.append(("ev_g", 4 * (g - 1) + n + 1))
            else:
                w.append(("ev_u", 64))
            p.step("tensor", gmm, waits=w, incs=[("mm_g", 1)])

            def gev(sc, g=g, n=n):
                return sc.activation(gate_sb[:, n * 512:(n + 1) * 512], pb[n][:],
                                     AF.Sigmoid, bias=bg_s[:, g:g + 1])
            w = [("mm_g", None)]
            if g >= 1 and n == 0:
                w.append(("ve_hg", g))
            if g == 0 and n == 0:
                w.append(("ve_ca", 16))
            p.step("scalar", gev, waits=w, incs=[("ev_g", 1)])

        def mkarep(v, g=g):
            v.memset(arep[:], 1.0)
            return v.tensor_scalar(arep[:], arep[:], a_s[:, g:g + 1], None, ALU.mult)
        w = [("acols", None)]
        if g == 0:
            w.append(("ve_ca", 16))
        if g >= 1:
            w.append(("ve_scan", g))
        p.step("vector", mkarep, waits=w, incs=[("ve_ar", 1)])

        def scan(v, g=g):
            return v.tensor_tensor_scan(h_sb[:], arep[:], u_rb[g % 2][:],
                                        cpm[:, g:g + 1], ALU.mult, ALU.add)
        w = [("ve_ar", None), ("sy_ur", 16 * (g + 1)), ("ve_cpm", 1)]
        if g >= 1:
            w.append(("ve_hg", g))
        p.step("vector", scan, waits=w, incs=[("ve_scan", 1)])

        def hg(v, g=g):
            return v.tensor_mul(hg_sb[:], h_sb[:], gate_sb[:])
        w = [("ve_scan", None), ("ev_g", 4 * g + 4)]
        if g >= 1:
            w.append(("sy_hg", 16 * g))
        p.step("vector", hg, waits=w, incs=[("ve_hg", 1)])

        def sthg(sy, g=g):
            return sy.dma_start(out=hg_spill[g * 128:(g + 1) * 128, :], in_=hg_sb[:])
        p.step("sync", sthg, waits=[("ve_hg", None)], incs=[("sy_hg", 16)])

    if stop == "loopB":
        return _finish([("sy_hg", None), ("qdst", None), ("g1out", None)])

    sigx, sxv, rinvx = onecol[0:1, 0:1], onecol[0:1, 1:2], onecol[0:1, 2:3]

    def xscal(sc):
        return sc.activation(sigx, clip_s[0:1, 0:1], AF.Sigmoid)
    p.step("scalar", xscal, waits=[("sm_ok", None)], incs=[("sc_sig", 1)])

    def xscal2(v):
        v.tensor_mul(sxv, sigx, g1)
        v.tensor_scalar(sxv, sxv, 1.0 / 127.0, None, ALU.mult)
        return v.reciprocal(rinvx, sxv)
    p.step("vector", xscal2, waits=[("sc_sig", None), ("g1out", 16)],
           incs=[("ve_sx", 1)])

    psR2 = p.psx("psR2", [128, 512])
    psC = psR2[:, 0:1]

    def reps(te):
        return te.matmul(psC, p.ones_r[:], sxv, start=True, stop=True)
    p.step("tensor", reps, waits=[("ve_sx", None), ("ln0repev", 8), ("ln0gm", 1)],
           incs=[("mm_sx", 1)])

    def repsev(sc):
        return sc.copy(sx_col[:, 0:1], psC)
    p.step("scalar", repsev, waits=[("mm_sx", None)], incs=[("ev_sx", 1)])

    ab2 = p.sbx("ab2", [128, 2 * NT])
    A2_r = ln["mu_r"]
    B2_r = ln["ms_r"]

    def mkab2(v):
        v.tensor_scalar(A2_r, ln["A_r"], rinvx[0:1, 0:1], None, ALU.mult)
        return v.tensor_scalar(B2_r, ln["B_r"], rinvx[0:1, 0:1], None, ALU.mult)
    p.step("vector", mkab2, waits=[("ve_sx", None)], incs=[("ve_ab2", 1)])

    for j in range(2):
        for n in range(4):
            def rep2(te, j=j, n=n):
                row = A2_r if j == 0 else B2_r
                return te.matmul(psR2[:], p.ones_r[:],
                                 row[0:1, n * 512:(n + 1) * 512],
                                 start=True, stop=True)
            p.step("tensor", rep2,
                   waits=[("ve_ab2", None), ("ev_ab2", None), ("ln0repev", 8),
                          ("qdst", p.c("qdst"))],
                   incs=[("mm_ab2", 1)])

            def rep2ev(sc, j=j, n=n):
                return sc.copy(ab2[:, j * NT + n * 512:j * NT + (n + 1) * 512], psR2[:])
            p.step("scalar", rep2ev, waits=[("mm_ab2", None)], incs=[("ev_ab2", 1)])

    p.rel("xn_bf")
    xq_bf = p.sbx("xq_bf", [128, NCH * NT], BF16)
    qtmp = scrB
    for k in range(NCH):
        def qx(v, k=k):
            v.tensor_mul(qtmp[:], x_sb[:, k * NT:(k + 1) * NT], ab2[:, 0:NT])
            v.tensor_sub(qtmp[:], qtmp[:], ab2[:, NT:2 * NT])
            v.tensor_scalar(qtmp[:], qtmp[:], C22, -C22, ALU.min, ALU.max)
            v.tensor_scalar(qtmp[:], qtmp[:], MAGIC, MAGIC, ALU.add, ALU.subtract)
            return v.tensor_scalar(xq_bf[:, k * NT:(k + 1) * NT], qtmp[:],
                                   -128.0, 127.0, ALU.max, ALU.min)
        w = [("ev_ab2", None)]
        if k == 0:
            w += [("ve_ar", 16), ("ve_scan", 16), ("mm_g", 64),
                  ("qdst", p.c("qdst"))]
        p.step("vector", qx, waits=w, incs=[("ve_xq", 1)])

    def mksxr(v):
        v.tensor_scalar(sxr_g[:], scol_g[:], sx_col[:, 0:1], None, ALU.mult)
        return v.tensor_scalar(sxr_u[:], scol_u[:], sx_col[:, 0:1], None, ALU.mult)
    p.step("vector", mksxr, waits=[("ev_sx", None), ("qgvq", 22), ("quvq", 22)],
           incs=[("ve_sxr", 1)])
    p.rel("wi_sl", "ab2", "ln0rows", "ln0sumx", "ln0sumsq")

    if stop == "xq":
        return _finish([("ve_xq", None), ("ve_sxr", None), ("sy_hg", None)])

    wg_i = p.sbx("wg_i", [128, 2 * 8 * 128], BF16)
    wu_i = p.sbx("wu_i", [128, 2 * 8 * 128], BF16)
    gsil, upv = scrA, scrB
    h2f = [scrC, scrD]
    wgq3 = wgq_t.rearrange("(k p) m -> p k m", p=128)
    wuq3 = wuq_t.rearrange("(k p) m -> p k m", p=128)
    for i in range(22):
        buf = (i % 2) * 8 * 128

        def ldwgi(sy, i=i, buf=buf):
            return [sy.dma_start(out=wg_i[:, buf:buf + 8 * 128],
                                 in_=wgq3[:, :, i * 128:(i + 1) * 128]),
                    sy.dma_start(out=wu_i[:, buf:buf + 8 * 128],
                                 in_=wuq3[:, :, i * 128:(i + 1) * 128])]
        w = [("qgst", 16 * 8 * (i + 1)), ("qust", 16 * 8 * (i + 1))]
        if i == 0:
            w.append(("ve_xq", 8))
        if i >= 2:
            w.append(("mm_h2", 8 * (i - 1)))
        p.step("sync", ldwgi, waits=w, incs=[("ld_wq", 32)])

        for n in range(4):
            def h2mm(te, i=i, n=n, buf=buf):
                for k in range(NCH):
                    te.matmul(pb[n % 2][:], wg_i[:, buf + k * 128:buf + (k + 1) * 128],
                              xq_bf[:, k * NT + n * 512:k * NT + (n + 1) * 512],
                              start=(k == 0), stop=(k == NCH - 1))
                for k in range(NCH):
                    inst = te.matmul(pb[2 + n % 2][:],
                                     wu_i[:, buf + k * 128:buf + (k + 1) * 128],
                                     xq_bf[:, k * NT + n * 512:k * NT + (n + 1) * 512],
                                     start=(k == 0), stop=(k == NCH - 1))
                return inst
            w = [("ld_wq", 32 * (i + 1)), ("ve_xq", 8)]
            prev = (i * 4 + n) - 2
            if prev >= 0:
                w.append(("ev_h2", 2 * prev + 2))
            elif i == 0 and n == 0:
                w.append(("ev_g", 64))
            p.step("tensor", h2mm, waits=w, incs=[("mm_h2", 2)])

            def h2ev(sc, i=i, n=n):
                sc.activation(gsil[:, n * 512:(n + 1) * 512], pb[n % 2][:],
                              AF.Silu, scale=sxr_g[:, i:i + 1])
                return sc.activation(upv[:, n * 512:(n + 1) * 512], pb[2 + n % 2][:],
                                     AF.Identity, scale=sxr_u[:, i:i + 1])
            w = [("mm_h2", None), ("ve_sxr", 1)]
            if i == 0 and n == 0:
                w.append(("ve_scan", 16))
            if n == 0 and i >= 1:
                w.append(("ve_h2", i))
            p.step("scalar", h2ev, waits=w, incs=[("ev_h2", 2)])

        def h2mul(v, i=i):
            hb = h2f[i % 2]
            v.tensor_mul(hb[:], gsil[:], upv[:])
            v.tensor_reduce(rmt[:], hb[:], AX.X, ALU.max, apply_absolute_value=True)
            if i == 0:
                return v.tensor_copy(rm2[:], rmt[:])
            return v.tensor_max(rm2[:], rm2[:], rmt[:])
        w = [("ev_h2", None)]
        if i == 0:
            w.append(("ve_hg", 16))
        if i >= 2:
            w.append(("sy_h2", 16 * (i - 1)))
        p.step("vector", h2mul, waits=w, incs=[("ve_h2", 1)])

        def sth2(sy, i=i):
            return sy.dma_start(out=h2_spill[i * 128:(i + 1) * 128, :], in_=h2f[i % 2][:])
        p.step("sync", sth2, waits=[("ve_h2", None)], incs=[("sy_h2", 16)])

    if stop == "h2":
        return _finish([("sy_h2", None)])

    wo_dd = p.sbx("wo_dd", [128, 2 * 16 * 128], BF16)
    hg_st = p.sbx("hg_st", [128, 16 * 512], BF16)
    wo3 = woT.rearrange("(g p) m -> p g m", p=128)
    hg3 = hg_spill.rearrange("(g p) n -> g p n", p=128)
    for n in range(4):
        def ldhg(sy, n=n):
            return [sy.dma_start(out=hg_st[:, g * 512:(g + 1) * 512],
                                 in_=hg3[g][:, n * 512:(n + 1) * 512])
                    for g in range(16)]
        w = [("sy_hg", 16 * 16)]
        if n == 0:
            w += [("ve_xq", 8), ("qdst", p.c("qdst")), ("mm_g", 64)]
        else:
            w.append(("mm_y", 8 * n))
        p.step("sync", ldhg, waits=w, incs=[("sy_hgl", 16 * 16)])

        for dd in range(8):
            idx = n * 8 + dd
            wbuf = (idx % 2) * 16 * 128

            def ldwo(gp, dd=dd, wbuf=wbuf):
                return gp.dma_start(out=wo_dd[:, wbuf:wbuf + 16 * 128],
                                    in_=wo3[:, :, dd * 128:(dd + 1) * 128])
            w = []
            if idx == 0:
                w = [("mm_g", 64), ("ve_xq", 8), ("qdst", p.c("qdst")),
                     ("ev_ab2", 8), ("ln0xn", 8)]
            elif idx >= 2:
                w = [("mm_y", idx - 1)]
            p.step("gpsimd", ldwo, waits=w, incs=[("ld_wo", 16)])

            def ymm(te, n=n, dd=dd, wbuf=wbuf):
                for g in range(16):
                    inst = te.matmul(pb[dd % 4][:],
                                     wo_dd[:, wbuf + g * 128:wbuf + (g + 1) * 128],
                                     hg_st[:, g * 512:(g + 1) * 512],
                                     start=(g == 0), stop=(g == 15))
                return inst
            w = [("sy_hgl", 16 * 16 * (n + 1)), ("ld_wo", 16 * (idx + 1))]
            prev = idx - 4
            if prev >= 0:
                w.append(("ev_y", prev + 1))
            elif idx == 0:
                w.append(("ev_h2", 176))
            p.step("tensor", ymm, waits=w, incs=[("mm_y", 1)])

            def yev(v, n=n, dd=dd):
                return v.scalar_tensor_tensor(
                    x_sb[:, dd * NT + n * 512:dd * NT + (n + 1) * 512],
                    pb[dd % 4][:], bo_s[:, dd:dd + 1],
                    x_sb[:, dd * NT + n * 512:dd * NT + (n + 1) * 512],
                    ALU.add, ALU.add)
            p.step("vector", yev, waits=[("mm_y", None)], incs=[("ev_y", 1)])

    if stop == "loopC":
        return _finish([("ev_y", None), ("sy_h2", None)])

    psT2 = p.psx("psT2", [1, 128])

    def trm2(te):
        return te.transpose(psT2[:], rm2[:], p.ident_f[:])
    p.step("tensor", trm2, waits=[("ve_h2", 22), ("ln0gm", 1)], incs=[("tr2", 1)])

    def redm2(v):
        return v.tensor_reduce(gm2[:], psT2[:], AX.X, ALU.max)
    p.step("vector", redm2, waits=[("tr2", None)], incs=[("gm2s", 1)])

    g2 = onecol[0:1, 7:8]
    emit_armax_scalar(p, "g2", gm2, "gm2s", g2)

    sigd, shv, rinvh = onecol[0:1, 4:5], onecol[0:1, 5:6], onecol[0:1, 6:7]

    def hscal(sc):
        return sc.activation(sigd, clip_s[0:1, 1:2], AF.Sigmoid)
    p.step("scalar", hscal, waits=[("sm_ok", None)], incs=[("sc_sigd", 1)])

    def hscal2(v):
        v.tensor_mul(shv, sigd, g2)
        v.tensor_scalar(shv, shv, 1.0 / 127.0, None, ALU.mult)
        return v.reciprocal(rinvh, shv)
    p.step("vector", hscal2, waits=[("sc_sigd", None), ("g2out", 16)],
           incs=[("ve_sh", 1)])

    def reph(te):
        return te.matmul(psR2[:, 0:2], p.ones_r[:],
                         onecol[0:1, 5:7], start=True, stop=True)
    p.step("tensor", reph, waits=[("ve_sh", None), ("ev_ab2", 8), ("ev_sx", 1)],
           incs=[("mm_sh", 1)])

    def rephev(sc):
        return sc.copy(sh_col[:, 0:2], psR2[:, 0:2])
    p.step("scalar", rephev, waits=[("mm_sh", None)], incs=[("ev_sh", 1)])

    def mksxrd(v):
        return v.tensor_scalar(sxr_d[:], scol_d[:], sh_col[:, 0:1], None, ALU.mult)
    p.step("vector", mksxrd, waits=[("ev_sh", None), ("qdvq", 8)], incs=[("ve_sxrd", 1)])

    if stop == "scales":
        return _finish([("ve_sxrd", None), ("ev_y", None), ("sy_h2", None)])

    p.rel("wg_i", "wu_i", "scrA", "scrB", "scrC", "scrD", "scrE", "hg_sb",
          "wo_dd", "hg_st")
    h2i = p.sbx("h2i", [128, 22 * 512], BF16)
    h2r = p.sbx("h2r", [128, 2 * 512])
    wd_sb = p.sbx("wd_sb", [128, 22 * 1024], BF16)
    wdq3 = wdq_t.rearrange("(k p) m -> k p m", p=128)

    def ldwd_all(gp):
        return [gp.dma_start(out=wd_sb[:, ki * 1024:(ki + 1) * 1024], in_=wdq3[ki])
                for ki in range(22)]
    p.step("gpsimd", ldwd_all,
           waits=[("qdst", p.c("qdst")), ("mm_y", 32), ("sy_h2", 16 * 22),
                  ("sy_hg", 16 * 16)],
           incs=[("ld_wd", 16 * 22)])

    for n in range(1 if stop == "down1" else 4):
        for ki in range(22):
            def ldh2(sy, n=n, ki=ki):
                return sy.dma_start(out=h2r[:, (ki % 2) * 512:(ki % 2) * 512 + 512],
                                    in_=h2_spill[ki * 128:(ki + 1) * 128,
                                                 n * 512:(n + 1) * 512])
            w = [("sy_h2", 16 * 22)]
            if n == 0 and ki == 0:
                w.append(("mm_y", 32))
            if n * 22 + ki >= 2:
                w.append(("ve_q2", n * 22 + ki - 1))
            p.step("sync", ldh2, waits=w, incs=[("sy_h2r", 16)])

            def qh2(v, n=n, ki=ki):
                src = h2r[:, (ki % 2) * 512:(ki % 2) * 512 + 512]
                v.tensor_scalar(src, src, sh_col[:, 1:2], None, ALU.mult)
                v.tensor_scalar(src, src, C22, -C22, ALU.min, ALU.max)
                v.tensor_scalar(src, src, MAGIC, MAGIC, ALU.add, ALU.subtract)
                return v.tensor_scalar(h2i[:, ki * 512:(ki + 1) * 512], src,
                                       -128.0, 127.0, ALU.max, ALU.min)
            w = [("sy_h2r", 16 * (n * 22 + ki + 1)), ("ev_sh", None)]
            if n > 0 and ki == 0:
                w.append(("mm_d", 8 * n))
            p.step("vector", qh2, waits=w, incs=[("ve_q2", 1)])

        for dd in range(8):
            def dmm(te, n=n, dd=dd):
                for ki in range(22):
                    inst = te.matmul(pb[dd % 4][:],
                                     wd_sb[:, ki * 1024 + dd * 128:ki * 1024 + (dd + 1) * 128],
                                     h2i[:, ki * 512:(ki + 1) * 512],
                                     start=(ki == 0), stop=(ki == 21))
                return inst
            w = [("ve_q2", 22 * (n + 1)), ("ld_wd", None)]
            prev = n * 8 + dd - 4
            if prev >= 0:
                w.append(("ev_d", prev + 1))
            elif n == 0 and dd < 4:
                w.append(("ev_y", 32))
            p.step("tensor", dmm, waits=w, incs=[("mm_d", 1)])

            def dev(v, n=n, dd=dd):
                return v.scalar_tensor_tensor(
                    x_sb[:, dd * NT + n * 512:dd * NT + (n + 1) * 512],
                    pb[dd % 4][:], sxr_d[:, dd:dd + 1],
                    x_sb[:, dd * NT + n * 512:dd * NT + (n + 1) * 512],
                    ALU.mult, ALU.add)
            p.step("vector", dev, waits=[("mm_d", None), ("ve_sxrd", 1), ("ev_y", 32)],
                   incs=[("ev_d", 1)])

    if stop == "down1":
        return _finish([("ev_d", None), ("ev_y", None)])

    if fx is not None:
        return [("ev_d", None), ("ev_y", None)]

    xo3 = xo.rearrange("(k p) n -> k p n", p=128)

    def store(sy):
        return [sy.dma_start(out=xo3[k], in_=x_sb[:, k * NT:(k + 1) * NT])
                for k in range(NCH)]
    p.step("sync", store, waits=[("ev_d", None), ("ev_y", None)],
           incs=[("st", 16 * NCH)])
    p.step("sync", lambda e: None, waits=[("st", None)])

    p.emit()
    return p




def build_L1(stop="full", fx=None):
    p = Prog() if fx is None else fx["p"]
    nc = p.nc
    inp = {}

    def ein(name, shape, dtype=F32):
        inp[name] = nc.dram_tensor(p.px + name, shape, dtype,
                                   kind="ExternalInput").ap()
        return inp[name]

    if fx is None:
        x_in = ein("x", [D, NT])
    ein("pm_col", [128, 1])
    ein("pmc_col", [128, 1])
    wwT = ein("wwT", [D, D])
    wrT = ein("wrT", [D, D])
    ein("dec_la", [1, 2])
    wg16T = ein("wg16T", [D, 16])
    w1 = ein("w1", [16, D, D])
    w2 = ein("w2", [16, 512, D])
    ws1 = ein("ws1", [D, 2 * D])
    ws2 = ein("ws2", [D, D])
    xo = nc.dram_tensor("xo", [D, NT], F32,
                        kind="ExternalOutput").ap() if fx is None else None

    cc1i = p.dram("cc1i", [128, 8])
    cc1o = p.dram("cc1o", [128, 8])
    cc2i = p.dram("cc2i", [D, D])
    cc2o = p.dram("cc2o", [D, D])

    if fx is None:
        x_sb = p.sb("x_sb", [128, NCH * NT])
        xn_bf = p.sb("xn_bf", [128, NCH * NT], BF16)
    else:
        x_sb, xn_bf = fx["x_sb"], fx["xn_bf"]
    pm_s = p.sb("pm_s", [128, 2])
    dl_s = p.sb("dl_s", [1, 8])
    cols = p.sb("cols", [128, 8])
    mask0 = p.sb("mask0", [128, 128])
    gcol = p.sb("gcol", [128, 2])
    xl = p.sb("xl", [128, 8])
    wk0 = p.sb("wk0", [128, 8])

    pb = [p.psx(f"pb{i}", [128, 512]) for i in range(4)]

    if fx is None:
        emit_common(p)
        x3 = x_in.rearrange("(k p) n -> k p n", p=128)

        def ldx(sy):
            return [sy.dma_start(out=x_sb[:, k * NT:(k + 1) * NT], in_=x3[k])
                    for k in range(NCH)]
        p.step("sync", ldx, incs=[("x_ok", 16 * NCH)])

    def ldsmall(sy):
        return [sy.dma_start(out=dl_s[0:1, 0:2], in_=inp["dec_la"][:]),
                sy.dma_start(out=pm_s[:, 0:1], in_=inp["pm_col"][:]),
                sy.dma_start(out=pm_s[:, 1:2], in_=inp["pmc_col"][:])]
    p.step("sync", ldsmall, waits=[("common", 1)], incs=[("sm_ok", 16 * 3)])

    def mkmask(v):
        return v.tensor_scalar(mask0[:], p.iota_f[:], p.iota_p[:], None, ALU.is_gt)
    p.step("vector", mkmask, waits=[("common", 1)], incs=[("msk", 1)])

    ln = emit_ln(p, "ln1", x_sb, xn_bf, "x_ok")
    p.rel(*ln["scoped"])
    p.rel(*ln["psums"])
    p.rel("ln1rows", "ln1sumx", "ln1sumsq")
    tsA = p.psx("tsA", [128, 128], BF16)
    tsB = p.psx("tsB", [128, 128], BF16)
    tsF = p.psx("tsF", [128, 128], F32)
    psR2 = p.psx("psR2", [128, 512], F32)

    gam, lng, alp = dl_s[0:1, 2:3], dl_s[0:1, 3:4], dl_s[0:1, 4:5]

    def sc1(sc):
        sc.activation(gam, dl_s[0:1, 0:1], AF.Sigmoid)
        sc.activation(lng, gam, AF.Ln)
        return sc.activation(alp, dl_s[0:1, 1:2], AF.Exp)
    p.step("scalar", sc1, waits=[("sm_ok", None)], incs=[("sc1", 1)])

    def repsc(te):
        return te.matmul(psR2[:, 0:2], p.ones_r[:], dl_s[0:1, 3:5],
                         start=True, stop=True)
    p.step("tensor", repsc,
           waits=[("sc1", None), ("common", 1), ("ln1repev", 8), ("ln1sev", 4)],
           incs=[("mmsc", 1)])

    def repev(sc):
        return sc.copy(cols[:, 0:2], psR2[:, 0:2])
    p.step("scalar", repev, waits=[("mmsc", None)], incs=[("evsc", 1)])

    def negl(v):
        return v.tensor_scalar(cols[:, 2:3], cols[:, 0:1], -1.0, None, ALU.mult)
    p.step("vector", negl, waits=[("evsc", None)], incs=[("nlng", 1)])

    def mkxl(v):
        for k in range(NCH):
            v.tensor_copy(xl[:, k:k + 1], xn_bf[:, k * NT + NT - 1:k * NT + NT])
        return v.tensor_scalar(xl[:], xl[:], pm_s[:, 1:2], None, ALU.mult)
    p.step("vector", mkxl, waits=[("ln1xn", None), ("sm_ok", None)], incs=[("vxl", 1)])

    _sd, _vd = p.sem("c1d"), p.av("c1d", 16)
    _sc, _vc = p.sem("c1c"), p.av("c1c", 1)
    p.mark("c1d", 16)
    p.mark("c1c", 1)
    def ccxl(g, _sd=_sd, _vd=_vd, _sc=_sc, _vc=_vc):
        g.dma_start(out=cc1i[:], in_=xl[:]).then_inc(_sd, 16)
        g.wait_ge(_sd, _vd)
        g.collective_compute("AllReduce", ALU.add, replica_groups=PAIRS,
                             ins=[cc1i.opt()], outs=[cc1o.opt()]).then_inc(_sc, 1)
        g.wait_ge(_sc, _vc)
        return g.dma_start(out=xl[:], in_=cc1o[:])
    p.step("gpsimd", ccxl, waits=[("vxl", None)], incs=[("c1o", 16)])

    def mkwk0(v):
        return v.tensor_scalar(wk0[:], xl[:], pm_s[:, 0:1], None, ALU.mult)
    p.step("vector", mkwk0, waits=[("c1o", None)], incs=[("vwk0", 1)])

    wk0t = p.sb("wk0t", [128, NCH * 128], BF16)

    def mkwk(v):
        for k in range(NCH):
            v.tensor_copy(wk0t[:, k * 128 + 1:(k + 1) * 128],
                          xn_bf[:, k * NT:k * NT + 127])
            inst = v.tensor_copy(wk0t[:, k * 128:k * 128 + 1], wk0[:, k:k + 1])
        return inst
    p.step("vector", mkwk, waits=[("vwk0", None)], incs=[("vwk", 1)])

    def wk_sl(sc, k):
        if sc == 0:
            return wk0t[:, k * 128:(k + 1) * 128]
        return xn_bf[:, k * NT + sc * 128 - 1:k * NT + (sc + 1) * 128 - 1]

    ww_sb = p.sbx("ww_sb", [128, NCH * D], BF16)
    ww3 = wwT.rearrange("(k p) m -> p k m", p=128)

    def ldww(g):
        return g.dma_start(out=ww_sb[:], in_=ww3[:, :, :])
    p.step("gpsimd", ldww, waits=[("common", 1)], incs=[("ldww", 16)])

    v_tm = p.sbx("v_tm", [128, 16 * D], BF16)
    for tc in range(16):
        for h in range(2):
            idx = tc * 2 + h

            def vmm(te, tc=tc, h=h):
                for k in range(NCH):
                    inst = te.matmul(pb[h][:],
                                     xn_bf[:, k * NT + tc * 128:k * NT + (tc + 1) * 128],
                                     ww_sb[:, k * D + h * 512:k * D + (h + 1) * 512],
                                     start=(k == 0), stop=(k == NCH - 1))
                return inst
            w = [("ldww", 16), ("ln1xn", None)]
            if idx >= 2:
                w.append(("vev", idx - 1))
            p.step("tensor", vmm, waits=w, incs=[("vmm", 1)])

            def vev(sc, tc=tc, h=h):
                return sc.copy(v_tm[:, tc * D + h * 512:tc * D + (h + 1) * 512],
                               pb[h][:])
            p.step("scalar", vev, waits=[("vmm", None)], incs=[("vev", 1)])

    p.rel("ww_sb")
    wk_tm2 = [p.sbx("wk_tmA", [128, 8 * D], BF16), p.sbx("wk_tmB", [128, 8 * D], BF16)]

    def wk_tm_sl(sc, a, b):
        return wk_tm2[sc // 8][:, (sc % 8) * D + a:(sc % 8) * D + b]
    for sc_ in range(16):
        for kc in range(NCH):
            gidx = sc_ * NCH + kc

            def wtr(te, sc_=sc_, kc=kc, gidx=gidx):
                return te.transpose(
                    (tsA if gidx % 2 == 0 else tsB)[:],
                    wk_sl(sc_, kc), p.ident_bf[:])
            w = [("vwk", 1)]
            if gidx >= 2:
                w.append(("wtev", gidx - 1))
            p.step("tensor", wtr, waits=w, incs=[("wtr", 1)])

            def wtev(v, sc_=sc_, kc=kc, gidx=gidx):
                return v.tensor_copy(wk_tm_sl(sc_, kc * 128, (kc + 1) * 128),
                                     (tsA if gidx % 2 == 0 else tsB)[:])
            p.step("vector", wtev, waits=[("wtr", None)], incs=[("wtev", 1)])

    for sc_ in range(16):
        def gscale(v, sc_=sc_):
            v.tensor_scalar(gcol[:, 0:1], cols[:, 0:1], float(2047 - sc_ * 128),
                            None, ALU.mult)
            return v.scalar_tensor_tensor(gcol[:, 1:2], p.iota_p[:], cols[:, 2:3],
                                          gcol[:, 0:1], ALU.mult, ALU.add)
        p.step("vector", gscale, waits=[("nlng", 1), ("wtev", None)],
               incs=[("vgc", 1)])

        def gexp(sc, sc_=sc_):
            return sc.activation(gcol[:, 1:2], gcol[:, 1:2], AF.Exp)
        p.step("scalar", gexp, waits=[("vgc", None)], incs=[("sgc", 1)])

        def wkscale(v, sc_=sc_):
            return v.tensor_scalar(wk_tm_sl(sc_, 0, D), wk_tm_sl(sc_, 0, D),
                                   gcol[:, 1:2], None, ALU.mult)
        p.step("vector", wkscale, waits=[("sgc", None)], incs=[("vgs", 1)])

    wt_bf = p.sbx("wt_bf", [128, NCH * D], BF16)
    for jc in range(NCH):
        for h in range(2):
            idx = jc * 2 + h

            def wbmm(te, jc=jc, h=h):
                for s_ in range(16):
                    inst = te.matmul(pb[2 + h][:],
                                     wk_tm_sl(s_, jc * 128, (jc + 1) * 128),
                                     v_tm[:, s_ * D + h * 512:s_ * D + (h + 1) * 512],
                                     start=(s_ == 0), stop=(s_ == 15))
                return inst
            w = [("vgs", 16), ("vev", 32)]
            if idx >= 2:
                w.append(("wbev", idx - 1))
            p.step("tensor", wbmm, waits=w, incs=[("wbmm", 1)])

            def wbev(sc, jc=jc, h=h):
                return sc.copy(wt_bf[:, jc * D + h * 512:jc * D + (h + 1) * 512],
                               pb[2 + h][:])
            p.step("scalar", wbev, waits=[("wbmm", None)], incs=[("wbev", 1)])

    def wtmask(v):
        for jc in range(NCH):
            inst = v.tensor_scalar(wt_bf[:, jc * D:(jc + 1) * D],
                                   wt_bf[:, jc * D:(jc + 1) * D],
                                   pm_s[:, 1:2], None, ALU.mult)
        return inst
    p.step("vector", wtmask, waits=[("wbev", None)], incs=[("wtm", 1)])

    cc2i3 = cc2i.rearrange("(k p) m -> k p m", p=128)
    cc2o3 = cc2o.rearrange("(k p) m -> k p m", p=128)

    _sd, _vd = p.sem("c2d"), p.av("c2d", 16 * NCH)
    _sc, _vc = p.sem("c2c"), p.av("c2c", 1)
    p.mark("c2d", 16 * NCH)
    p.mark("c2c", 1)
    def ccwt(g, _sd=_sd, _vd=_vd, _sc=_sc, _vc=_vc):
        for jc in range(NCH):
            g.dma_start(out=cc2i3[jc], in_=wt_bf[:, jc * D:(jc + 1) * D]).then_inc(
                _sd, 16)
        g.wait_ge(_sd, _vd)
        g.collective_compute("AllReduce", ALU.add, replica_groups=PAIRS,
                             ins=[cc2i.opt()], outs=[cc2o.opt()]).then_inc(_sc, 1)
        g.wait_ge(_sc, _vc)
        return [g.dma_start(out=wt_bf[:, jc * D:(jc + 1) * D], in_=cc2o3[jc])
                for jc in range(NCH)]
    p.step("gpsimd", ccwt, waits=[("wtm", None)], incs=[("c2o", 16 * NCH)])

    if stop == "attn1":
        return _mk_finish(p, x_sb, xo)([("c2o", None), ("vev", 32)])

    p.rel("wk_tmA", "wk_tmB")
    reads2 = [p.sbx("readsA", [128, 8 * D], BF16), p.sbx("readsB", [128, 8 * D], BF16)]

    def reads_sl(tc, a, b):
        return reads2[tc // 8][:, (tc % 8) * D + a:(tc % 8) * D + b]
    dec_f = p.sbx("dec_f", [128, 2 * 128])
    pbf = p.sbx("pbf", [128, 2 * 128], BF16)
    pair = 0
    for tc in range(16):
        for s_ in range(tc + 1):
            par = pair % 2

            def amm(te, tc=tc, s_=s_):
                for k in range(NCH):
                    inst = te.matmul(pb[0][:, 0:128],
                                     wk_sl(s_, k),
                                     xn_bf[:, k * NT + tc * 128:k * NT + (tc + 1) * 128],
                                     start=(k == 0), stop=(k == NCH - 1),
                                     skip_group_check=True)
                return inst
            w = [("vwk", 1)]
            if pair == 0:
                w.append(("vev", 32))
            if pair >= 1:
                w.append(("vp", pair))
            p.step("tensor", amm, waits=w, incs=[("amm", 1)])

            def mkdec(v, tc=tc, s_=s_, par=par):
                d0 = float(tc * 128 - 1 - s_ * 128)
                v.tensor_scalar(gcol[:, 0:1], p.iota_p[:], -1.0, d0,
                                ALU.mult, ALU.add)
                v.tensor_mul(gcol[:, 0:1], gcol[:, 0:1], cols[:, 0:1])
                dsl = dec_f[:, par * 128:(par + 1) * 128]
                v.tensor_scalar(dsl, p.iota_f[:], cols[:, 0:1], gcol[:, 0:1],
                                ALU.mult, ALU.add)
                return v.tensor_scalar(dsl, dsl, 0.0, None, ALU.min)
            w = [("nlng", 1)]
            if pair >= 2:
                w.append(("vp", pair - 1))
            p.step("vector", mkdec, waits=w, incs=[("vdc", 1)])

            def dexp(sc, par=par):
                return sc.activation(dec_f[:, par * 128:(par + 1) * 128],
                                     dec_f[:, par * 128:(par + 1) * 128], AF.Exp)
            p.step("scalar", dexp, waits=[("vdc", None)], incs=[("sdc", 1)])

            def mkp(v, tc=tc, s_=s_, par=par):
                dslice = dec_f[:, par * 128:(par + 1) * 128]
                if tc == s_:
                    v.tensor_mul(dslice, dslice, mask0[:])
                return v.tensor_tensor(pbf[:, par * 128:(par + 1) * 128],
                                       pb[0][:, 0:128], dslice, ALU.mult)
            w = [("sdc", None), ("amm", None)]
            if pair >= 2:
                w.append(("rmm", 2 * (pair - 1)))
            p.step("vector", mkp, waits=w, incs=[("vp", 1)])

            def rmm(te, tc=tc, s_=s_, par=par):
                for h in range(2):
                    inst = te.matmul(pb[1 + h][:],
                                     pbf[:, par * 128:(par + 1) * 128],
                                     v_tm[:, s_ * D + h * 512:s_ * D + (h + 1) * 512],
                                     start=(s_ == 0), stop=(s_ == tc),
                                     skip_group_check=True)
                return inst
            w = [("vp", None)]
            if tc == 0:
                w.append(("vev", 32))
            if tc >= 1 and s_ == 0:
                w.append(("rev", 2 * tc))
            p.step("tensor", rmm, waits=w, incs=[("rmm", 2)])
            pair += 1

        def rev(sc, tc=tc):
            sc.copy(reads_sl(tc, 0, 512), pb[1][:])
            return sc.copy(reads_sl(tc, 512, D), pb[2][:])
        p.step("scalar", rev, waits=[("rmm", None)], incs=[("rev", 2)])

    p.rel("v_tm")

    if stop == "quad":
        return _mk_finish(p, x_sb, xo)([("rev", None), ("c2o", None)])

    gam_b = p.sbx("gam_b", [128, NT])

    def gio(g):
        return g.iota(gam_b[:], pattern=[[1, NT]], base=0, channel_multiplier=0,
                      allow_small_or_imprecise_dtypes=True)
    p.step("gpsimd", gio, waits=[("common", 1)], incs=[("gio", 1)])

    def gexp2(sc):
        return sc.activation(gam_b[:], gam_b[:], AF.Exp, scale=cols[:, 0:1])
    p.step("scalar", gexp2, waits=[("gio", None), ("evsc", 1)], incs=[("sg2", 1)])

    def gpm(v):
        return v.tensor_scalar(gam_b[:], gam_b[:], pm_s[:, 0:1], None, ALU.mult)
    p.step("vector", gpm, waits=[("sg2", None)], incs=[("vg2", 1)])

    rfm2 = [p.sbx("rfmA", [128, 4 * NT], BF16), p.sbx("rfmB", [128, 4 * NT], BF16)]

    def rfm_sl(ec, a, b):
        return rfm2[ec // 4][:, (ec % 4) * NT + a:(ec % 4) * NT + b]
    for ic in range(NCH):
        for n in range(4):
            idx = ic * 4 + n

            def imm(te, ic=ic, n=n):
                for jc in range(NCH):
                    inst = te.matmul(pb[3][:],
                                     wt_bf[:, jc * D + ic * 128:jc * D + (ic + 1) * 128],
                                     xn_bf[:, jc * NT + n * 512:jc * NT + (n + 1) * 512],
                                     start=(jc == 0), stop=(jc == NCH - 1))
                return inst
            w = [("c2o", None)]
            if idx >= 1:
                w.append(("iev", idx))
            p.step("tensor", imm, waits=w, incs=[("imm", 1)])

            def iev(v, ic=ic, n=n):
                return v.tensor_tensor(rfm_sl(ic, n * 512, (n + 1) * 512),
                                       pb[3][:], gam_b[:, n * 512:(n + 1) * 512],
                                       ALU.mult)
            p.step("vector", iev, waits=[("imm", None), ("vg2", 1)], incs=[("iev", 1)])

    for tc in range(16):
        for ec in range(NCH):
            gidx = tc * NCH + ec

            def rtr(te, tc=tc, ec=ec, gidx=gidx):
                return te.transpose((tsA if gidx % 2 == 0 else tsB)[:],
                                    reads_sl(tc, ec * 128, (ec + 1) * 128),
                                    p.ident_bf[:])
            w = [("rev", 32), ("wtev", 128)]
            if gidx >= 2:
                w.append(("rtev", gidx - 1))
            p.step("tensor", rtr, waits=w, incs=[("rtr", 1)])

            def rtev(v, tc=tc, ec=ec, gidx=gidx):
                sl = rfm_sl(ec, tc * 128, (tc + 1) * 128)
                return v.tensor_tensor(sl, sl, (tsA if gidx % 2 == 0 else tsB)[:],
                                       ALU.add)
            p.step("vector", rtev, waits=[("rtr", None), ("iev", 32)],
                   incs=[("rtev", 1)])

    if stop == "dbgrfm":
        def cprfm(v):
            for k in range(NCH):
                inst = v.tensor_copy(x_sb[:, k * NT:(k + 1) * NT],
                                     rfm_sl(k, 0, NT))
            return inst
        p.step("vector", cprfm, waits=[("rtev", None), ("iev", None)],
               incs=[("dbg", 1)])
        return _mk_finish(p, x_sb, xo)([("dbg", 1)])

    p.rel("readsA", "readsB", "dec_f", "pbf")

    wr_sb = p.sbx("wr_sb", [128, NCH * D], BF16)
    wr3 = wrT.rearrange("(k p) m -> p k m", p=128)

    def ldwr(g):
        return g.dma_start(out=wr_sb[:], in_=wr3[:, :, :])
    p.step("gpsimd", ldwr, waits=[("rtr", 128)], incs=[("ldwr", 16)])

    def xadd(v):
        for k in range(NCH):
            inst = v.tensor_tensor(x_sb[:, k * NT:(k + 1) * NT],
                                   x_sb[:, k * NT:(k + 1) * NT],
                                   xn_bf[:, k * NT:(k + 1) * NT], ALU.add)
        return inst
    p.step("vector", xadd, waits=[("ln1xn", None)], incs=[("vxa", 1)])

    for dc in range(NCH):
        for n in range(4):
            idx = dc * 4 + n

            def romm(te, dc=dc, n=n, idx=idx):
                for kc in range(NCH):
                    inst = te.matmul(pb[idx % 2][:],
                                     wr_sb[:, kc * D + dc * 128:kc * D + (dc + 1) * 128],
                                     rfm_sl(kc, n * 512, (n + 1) * 512),
                                     start=(kc == 0), stop=(kc == NCH - 1))
                return inst
            w = [("ldwr", 16), ("rtev", 128)]
            if idx >= 2:
                w.append(("roev", idx - 1))
            p.step("tensor", romm, waits=w, incs=[("romm", 1)])

            def roev(v, dc=dc, n=n):
                return v.scalar_tensor_tensor(
                    x_sb[:, dc * NT + n * 512:dc * NT + (n + 1) * 512],
                    pb[(dc * 4 + n) % 2][:],
                    cols[:, 1:2],
                    x_sb[:, dc * NT + n * 512:dc * NT + (n + 1) * 512],
                    ALU.mult, ALU.add)
            p.step("vector", roev, waits=[("romm", None), ("vxa", 1)],
                   incs=[("roev", 1)])

    p.rel("rfmA", "rfmB", "wr_sb", "gam_b", "wt_bf")

    if stop == "attn":
        return _mk_finish(p, x_sb, xo)([("roev", None)])

    wg_sb = p.sb("wg_sb", [128, NCH * 16], BF16)
    wg3 = wg16T.rearrange("(k p) m -> p k m", p=128)

    def ldwg16(g):
        return g.dma_start(out=wg_sb[:], in_=wg3[:, :, :])
    p.step("gpsimd", ldwg16, waits=[("romm", 32), ("roev", 32)],
           incs=[("ldwg16", 16)])

    scFM = p.sb("scFM", [16, NT])
    for n in range(4):
        def smm(te, n=n):
            for kc in range(NCH):
                inst = te.matmul(pb[2][0:16, 0:512],
                                 wg_sb[:, kc * 16:(kc + 1) * 16],
                                 xn_bf[:, kc * NT + n * 512:kc * NT + (n + 1) * 512],
                                 start=(kc == 0), stop=(kc == NCH - 1))
            return inst
        w = [("ldwg16", 16), ("roev", 32)]
        if n >= 1:
            w.append(("sev2", n))
        p.step("tensor", smm, waits=w, incs=[("smm2", 1)])

        def sev2(sc, n=n):
            return sc.copy(scFM[0:16, n * 512:(n + 1) * 512], pb[2][0:16, :])
        p.step("scalar", sev2, waits=[("smm2", None)], incs=[("sev2", 1)])

    cw_tm = p.sb("cw_tm", [128, 16 * 16])
    rnk = p.sb("rnk", [128, 16])
    scc = p.sb("scc", [128, 4])
    for tc in range(16):
        def str_(te, tc=tc):
            return te.transpose(tsF[:, 0:16],
                                scFM[0:16, tc * 128:(tc + 1) * 128],
                                p.ident_f[0:16, 0:16])
        w = [("sev2", 4)]
        if tc >= 1:
            w.append(("stv", tc))
        p.step("tensor", str_, waits=w, incs=[("str", 1)])

        def stv(v, tc=tc):
            return v.tensor_copy(cw_tm[:, tc * 16:(tc + 1) * 16], tsF[:, 0:16])
        p.step("vector", stv, waits=[("str", None)], incs=[("stv", 1)])

        def smax(v, tc=tc):
            t16 = cw_tm[:, tc * 16:(tc + 1) * 16]
            v.tensor_reduce(scc[:, 0:1], t16, AX.X, ALU.max)
            return v.tensor_scalar(scc[:, 1:2], scc[:, 0:1], -1.0, None, ALU.mult)
        p.step("vector", smax, waits=[("stv", None)], incs=[("vsm", 1)])

        def sexp(sc, tc=tc):
            t16 = cw_tm[:, tc * 16:(tc + 1) * 16]
            return sc.activation(t16, t16, AF.Exp, bias=scc[:, 1:2])
        p.step("scalar", sexp, waits=[("vsm", None)], incs=[("ssx", 1)])

        def snorm(v, tc=tc):
            t16 = cw_tm[:, tc * 16:(tc + 1) * 16]
            v.tensor_reduce(scc[:, 2:3], t16, AX.X, ALU.add)
            v.reciprocal(scc[:, 3:4], scc[:, 2:3])
            v.tensor_scalar(t16, t16, scc[:, 3:4], None, ALU.mult)
            v.memset(rnk[:], 0.0)
            for e in range(16):
                v.scalar_tensor_tensor(rnk[:], t16, t16[:, e:e + 1], rnk[:],
                                       ALU.is_lt, ALU.add)
            v.tensor_scalar(rnk[:], rnk[:], 2.0, None, ALU.is_lt)
            return v.tensor_mul(t16, t16, rnk[:])
        p.step("vector", snorm, waits=[("ssx", None)], incs=[("vsn", 1)])

    cwFM = p.sb("cwFM", [16, NT])
    for tc in range(16):
        def ctr(te, tc=tc):
            return te.transpose(tsF[0:16, :], cw_tm[:, tc * 16:(tc + 1) * 16],
                                p.ident_f[:])
        w = [("vsn", 16)]
        if tc >= 1:
            w.append(("ctv", tc))
        p.step("tensor", ctr, waits=w, incs=[("ctr", 1)])

        def ctv(v, tc=tc):
            return v.tensor_copy(cwFM[0:16, tc * 128:(tc + 1) * 128], tsF[0:16, :])
        p.step("vector", ctv, waits=[("ctr", None)], incs=[("ctv", 1)])

    if stop == "route":
        return _mk_finish(p, x_sb, xo)([("ctv", None), ("roev", None)])

    p.rel("scFM")
    cwrow = p.sb("cwrow", [1, NT])
    cwrep = p.sb("cwrep", [128, NT])
    xfc = p.sb("xfc", [128, NCH * NT], BF16)
    he_bf = p.sb("he_bf", [128, 4 * NT], BF16)
    gs_f = p.sb("gs_f", [128, 512])
    wsl1 = p.sb("wsl1", [128, 2 * NCH * 128], BF16)
    wsl2 = p.sb("wsl2", [128, 2 * 4 * 128], BF16)
    w13 = w1.rearrange("e (k p) m -> e p k m", p=128)
    w23 = w2.rearrange("e (k p) m -> e p k m", p=128)

    for e in range(16):
        def cwr(sy, e=e):
            return sy.dma_start(out=cwrow[:], in_=cwFM[e:e + 1, :])
        w = [("ctv", 16)]
        if e >= 1:
            w.append(("vxf", e * NCH))
        p.step("sync", cwr, waits=w, incs=[("cwr", 16)])

        for n in range(4):
            def crep(te, e=e, n=n):
                return te.matmul(psR2[:], p.ones_r[:],
                                 cwrow[0:1, n * 512:(n + 1) * 512],
                                 start=True, stop=True)
            w = [("cwr", 16 * (e + 1))]
            if e * 4 + n >= 1:
                w.append(("crev", e * 4 + n))
            if e >= 1 and n == 0:
                w.append(("vxf", NCH * e))
            p.step("tensor", crep, waits=w, incs=[("crmm", 1)])

            def crev(sc, e=e, n=n):
                return sc.copy(cwrep[:, n * 512:(n + 1) * 512], psR2[:])
            p.step("scalar", crev, waits=[("crmm", None)], incs=[("crev", 1)])

        def mkxfc(v, e=e):
            for k in range(NCH):
                inst = v.tensor_tensor(xfc[:, k * NT:(k + 1) * NT],
                                       xn_bf[:, k * NT:(k + 1) * NT],
                                       cwrep[:], ALU.mult)
            return inst
        w = [("crev", 4 * (e + 1))]
        if e >= 1:
            w.append(("umm", e * 32))
        p.step("vector", mkxfc, waits=w, incs=[("vxf", NCH)])

        for m in range(8):
            gm = e * 8 + m
            wbuf = (gm % 2) * NCH * 128

            def ldw1(g, e=e, m=m, wbuf=wbuf):
                return g.dma_start(out=wsl1[:, wbuf:wbuf + NCH * 128],
                                   in_=w13[e][:, :, m * 128:(m + 1) * 128])
            w = []
            if gm >= 2:
                w.append(("umm", (gm - 1) * 4))
            p.step("gpsimd", ldw1, waits=w, incs=[("ldw1", 16)])

            for n in range(4):
                idx = m * 4 + n

                def umm(te, e=e, m=m, n=n, wbuf=wbuf):
                    rhs_src = xfc if m < 4 else xn_bf
                    for k in range(NCH):
                        inst = te.matmul(pb[n % 2][:],
                                         wsl1[:, wbuf + k * 128:wbuf + (k + 1) * 128],
                                         rhs_src[:, k * NT + n * 512:k * NT + (n + 1) * 512],
                                         start=(k == 0), stop=(k == NCH - 1))
                    return inst
                w = [("ldw1", 16 * (gm + 1))]
                if m < 4:
                    w.append(("vxf", NCH * (e + 1)))
                prev = e * 32 + idx - 2
                if prev >= e * 32:
                    w.append(("uev", prev + 2))
                elif e >= 1:
                    w.append(("uev", e * 32))
                p.step("tensor", umm, waits=w, incs=[("umm", 1)])

                if m < 4:
                    def uev(sc, e=e, m=m, n=n):
                        return sc.copy(he_bf[:, m * NT + n * 512:m * NT + (n + 1) * 512],
                                       pb[n % 2][:])
                    w2_ = [("umm", None)]
                    if e >= 1 and m == 0 and n == 0:
                        w2_.append(("zmm", 32 * e))
                    p.step("scalar", uev, waits=w2_, incs=[("uev", 1)])
                else:
                    def uev(sc, e=e, m=m, n=n):
                        return sc.activation(gs_f[:], pb[n % 2][:], AF.Silu)
                    p.step("scalar", uev,
                           waits=[("umm", None), ("vhe", e * 16 + (m - 4) * 4 + n)],
                           incs=[("uev", 1)])

                    def hemul(v, e=e, m=m, n=n):
                        sl = he_bf[:, (m - 4) * NT + n * 512:(m - 4) * NT + (n + 1) * 512]
                        return v.tensor_tensor(sl, sl, gs_f[:], ALU.mult)
                    p.step("vector", hemul, waits=[("uev", None)], incs=[("vhe", 1)])

        for dc in range(NCH):
            gd = e * 8 + dc
            zbuf = (gd % 2) * 4 * 128

            def ldw2(g, e=e, dc=dc, zbuf=zbuf):
                return g.dma_start(out=wsl2[:, zbuf:zbuf + 4 * 128],
                                   in_=w23[e][:, :, dc * 128:(dc + 1) * 128])
            w = []
            if gd >= 2:
                w.append(("zmm", (gd - 1) * 4))
            p.step("gpsimd", ldw2, waits=w, incs=[("ldw2", 16)])

            for n in range(4):
                idx = e * 32 + dc * 4 + n

                def zmm(te, e=e, dc=dc, n=n, zbuf=zbuf):
                    for k in range(4):
                        inst = te.matmul(pb[2 + n % 2][:],
                                         wsl2[:, zbuf + k * 128:zbuf + (k + 1) * 128],
                                         he_bf[:, k * NT + n * 512:k * NT + (n + 1) * 512],
                                         start=(k == 0), stop=(k == 3))
                    return inst
                w = [("ldw2", 16 * (gd + 1)), ("vhe", 16 * (e + 1))]
                prev = idx - 2
                if prev >= 0:
                    w.append(("zev", prev + 1))
                p.step("tensor", zmm, waits=w, incs=[("zmm", 1)])

                def zev(v, e=e, dc=dc, n=n):
                    return v.scalar_tensor_tensor(
                        x_sb[:, dc * NT + n * 512:dc * NT + (n + 1) * 512],
                        pb[2 + n % 2][:], 0.0,
                        x_sb[:, dc * NT + n * 512:dc * NT + (n + 1) * 512],
                        ALU.add, ALU.add)
                p.step("vector", zev, waits=[("zmm", None)], incs=[("zev", 1)])

    ws13 = ws1.rearrange("(k p) m -> p k m", p=128)
    ws23 = ws2.rearrange("(k p) m -> p k m", p=128)
    hs_bf = xfc

    for m in range(16):
        wbuf = (m % 2) * NCH * 128

        def ldws1(g, m=m, wbuf=wbuf):
            return g.dma_start(out=wsl1[:, wbuf:wbuf + NCH * 128],
                               in_=ws13[:, :, m * 128:(m + 1) * 128])
        w = [("umm", 16 * 32)] if m < 2 else [("ummS", (m - 1) * 4)]
        p.step("gpsimd", ldws1, waits=w, incs=[("ldws1", 16)])

        for n in range(4):
            idx = m * 4 + n

            def ummS(te, m=m, n=n, wbuf=wbuf):
                for k in range(NCH):
                    inst = te.matmul(pb[n % 2][:],
                                     wsl1[:, wbuf + k * 128:wbuf + (k + 1) * 128],
                                     xn_bf[:, k * NT + n * 512:k * NT + (n + 1) * 512],
                                     start=(k == 0), stop=(k == NCH - 1))
                return inst
            w = [("ldws1", 16 * (m + 1)), ("zmm", 16 * 32), ("vxf", 16 * NCH)]
            if idx == 0:
                w.append(("uev", 512))
            prev = idx - 2
            if prev >= 0:
                w.append(("uevS", prev + 2))
            p.step("tensor", ummS, waits=w, incs=[("ummS", 1)])

            if m < 8:
                def uevS(sc, m=m, n=n):
                    return sc.copy(hs_bf[:, m * NT + n * 512:m * NT + (n + 1) * 512],
                                   pb[n % 2][:])
                p.step("scalar", uevS, waits=[("ummS", None)], incs=[("uevS", 1)])
            else:
                def uevS(sc, m=m, n=n):
                    return sc.activation(gs_f[:], pb[n % 2][:], AF.Silu)
                p.step("scalar", uevS, waits=[("ummS", None), ("vheS", (m - 8) * 4 + n)],
                       incs=[("uevS", 1)])

                def hemulS(v, m=m, n=n):
                    sl = hs_bf[:, (m - 8) * NT + n * 512:(m - 8) * NT + (n + 1) * 512]
                    return v.tensor_tensor(sl, sl, gs_f[:], ALU.mult)
                p.step("vector", hemulS, waits=[("uevS", None)], incs=[("vheS", 1)])

    for dc in range(NCH):
        zbuf = (dc % 2) * 4 * 128

        def ldws2(g, dc=dc, zbuf=zbuf):
            return g.dma_start(out=wsl1[:, zbuf * 2:zbuf * 2 + NCH * 128],
                               in_=ws23[:, :, dc * 128:(dc + 1) * 128])
        w = [("ummS", 64)] if dc < 2 else [("zmmS", (dc - 1) * 4)]
        p.step("gpsimd", ldws2, waits=w, incs=[("ldws2", 16)])

        for n in range(4):
            idx = dc * 4 + n

            def zmmS(te, dc=dc, n=n, zbuf=zbuf):
                for k in range(NCH):
                    inst = te.matmul(pb[2 + n % 2][:],
                                     wsl1[:, zbuf * 2 + k * 128:zbuf * 2 + (k + 1) * 128],
                                     hs_bf[:, k * NT + n * 512:k * NT + (n + 1) * 512],
                                     start=(k == 0), stop=(k == NCH - 1))
                return inst
            w = [("ldws2", 16 * (dc + 1)), ("vheS", 32)]
            if idx == 0:
                w.append(("zev", 512))
            prev = idx - 2
            if prev >= 0:
                w.append(("zevS", prev + 1))
            p.step("tensor", zmmS, waits=w, incs=[("zmmS", 1)])

            def zevS(v, dc=dc, n=n):
                return v.scalar_tensor_tensor(
                    x_sb[:, dc * NT + n * 512:dc * NT + (n + 1) * 512],
                    pb[2 + n % 2][:], 0.0,
                    x_sb[:, dc * NT + n * 512:dc * NT + (n + 1) * 512],
                    ALU.add, ALU.add)
            p.step("vector", zevS, waits=[("zmmS", None), ("zev", 16 * 32)],
                   incs=[("zevS", 1)])

    return _mk_finish(p, x_sb, xo)([("zevS", None), ("roev", None)])




def build_L2(stop="full", fx=None):
    p = Prog() if fx is None else fx["p"]
    nc = p.nc
    inp = {}

    def ein(name, shape, dtype=F32):
        inp[name] = nc.dram_tensor(p.px + name, shape, dtype,
                                   kind="ExternalInput").ap()
        return inp[name]

    if fx is None:
        x_in = ein("x", [D, NT])
    wqT = ein("wqT", [D, D])
    wkT = ein("wkT", [D, D])
    wvT = ein("wvT", [D, D])
    woT = ein("woT", [D, D])
    ein("bq_c", [128, 8]); ein("bk_c", [128, 8])
    ein("bv_c", [128, 8]); ein("bo_c", [128, 8])
    ein("projT", [64, 256])
    ein("gwT", [D, 16])
    ein("ebias", [1, 16])
    wge = ein("wge", [16, D, D])
    wue = ein("wue", [16, D, D])
    wde = ein("wde", [16, D, D])
    wgs = ein("wgs", [D, D])
    wus = ein("wus", [D, D])
    wds = ein("wds", [D, D])
    xo = nc.dram_tensor("xo", [D, NT], F32,
                        kind="ExternalOutput").ap() if fx is None else None

    ccci = p.dram("ccci", [128, 2080])
    ccco = p.dram("ccco", [128, 2080])

    if fx is None:
        x_sb = p.sb("x_sb", [128, NCH * NT])
        xn_bf = p.sb("xn_bf", [128, NCH * NT], BF16)
    else:
        x_sb, xn_bf = fx["x_sb"], fx["xn_bf"]
    bcols = p.sb("bcols", [128, 32])
    bq_s, bk_s = bcols[:, 0:8], bcols[:, 8:16]
    bv_s, bo_s = bcols[:, 16:24], bcols[:, 24:32]
    dn_col = p.sb("dn_col", [128, 1])
    stab_c = p.sb("stab_c", [128, 2])
    dg_all = p.sb("dg_all", [128, 16 * 16])
    onec = p.sb("onec", [1, 4])
    prj_sb = p.sb("prj_sb", [128, 256], BF16)
    ctx_all = p.sb("ctx_all", [128, 16 * 130])
    ctx_bf = p.sb("ctx_bf", [128, 16 * 130], BF16)

    pb = [p.psx(f"pb{i}", [128, 512]) for i in range(4)]

    if fx is None:
        emit_common(p)
        x3 = x_in.rearrange("(k p) n -> k p n", p=128)

        def ldx(sy):
            return [sy.dma_start(out=x_sb[:, k * NT:(k + 1) * NT], in_=x3[k])
                    for k in range(NCH)]
        p.step("sync", ldx, incs=[("x_ok", 16 * NCH)])

    def ldsmall(sy):
        return [sy.dma_start(out=bq_s, in_=inp["bq_c"][:]),
                sy.dma_start(out=bk_s, in_=inp["bk_c"][:]),
                sy.dma_start(out=bv_s, in_=inp["bv_c"][:]),
                sy.dma_start(out=bo_s, in_=inp["bo_c"][:])]
    p.step("sync", ldsmall, waits=[("common", 1)], incs=[("sm_ok", 16 * 4)])

    def ldprj(g):
        return [g.dma_start(out=prj_sb[0:64, :], in_=inp["projT"][:]),
                g.dma_start(out=prj_sb[64:128, :], in_=inp["projT"][:])]
    p.step("gpsimd", ldprj, waits=[("common", 1)], incs=[("ldprj", 32)])

    def mkdn(v):
        return v.memset(dn_col[:], 64.0 ** -0.25)
    p.step("vector", mkdn, waits=[("common", 1)], incs=[("mkdn", 1)])

    ln = emit_ln(p, "ln2", x_sb, xn_bf, "x_ok")
    p.rel(*ln["scoped"])
    p.rel(*ln["psums"])
    p.rel("ln2rows", "ln2sumx", "ln2sumsq")
    tsA = p.psx("tsA", [128, 128], BF16)
    tsB = p.psx("tsB", [128, 128], BF16)
    tsF = p.psx("tsF", [128, 128], F32)
    psR2 = p.psx("psR2", [128, 512], F32)
    psT2 = tsF[0:1, 0:128]

    k_fm = p.sbx("k_fm", [128, NCH * NT], BF16)
    wk_sb = p.sbx("wk_sb", [128, NCH * D], BF16)
    wk3 = wkT.rearrange("(k p) m -> p k m", p=128)
    wv3 = wvT.rearrange("(k p) m -> p k m", p=128)

    def ldwk(g):
        return g.dma_start(out=wk_sb[:], in_=wk3[:, :, :])
    p.step("gpsimd", ldwk, waits=[("ln2xn", 8)], incs=[("ldwk", 16)])

    for oc in range(NCH):
        for n in range(4):
            idx = oc * 4 + n

            def kmm(te, oc=oc, n=n, idx=idx):
                for kc in range(NCH):
                    inst = te.matmul(pb[idx % 2][:],
                                     wk_sb[:, kc * D + oc * 128:kc * D + (oc + 1) * 128],
                                     xn_bf[:, kc * NT + n * 512:kc * NT + (n + 1) * 512],
                                     start=(kc == 0), stop=(kc == NCH - 1))
                return inst
            w = [("ldwk", 16), ("ln2xn", None)]
            if idx >= 2:
                w.append(("kev", idx - 1))
            p.step("tensor", kmm, waits=w, incs=[("kmm", 1)])

            def kev(sc, oc=oc, n=n, idx=idx):
                return sc.activation(k_fm[:, oc * NT + n * 512:oc * NT + (n + 1) * 512],
                                     pb[idx % 2][:], AF.Identity,
                                     bias=bk_s[:, oc:oc + 1])
            p.step("scalar", kev, waits=[("kmm", None), ("sm_ok", None)],
                   incs=[("kev", 1)])

    v_tm = p.sbx("v_tm", [128, 16 * D], BF16)

    def ldwv(g):
        return g.dma_start(out=wk_sb[:], in_=wv3[:, :, :])
    p.step("gpsimd", ldwv, waits=[("kmm", 32)], incs=[("ldwv", 16)])

    for tc in range(16):
        for h2 in range(2):
            idx = tc * 2 + h2

            def vmm(te, tc=tc, h2=h2, idx=idx):
                for kc in range(NCH):
                    inst = te.matmul(pb[2 + idx % 2][:],
                                     xn_bf[:, kc * NT + tc * 128:kc * NT + (tc + 1) * 128],
                                     wk_sb[:, kc * D + h2 * 512:kc * D + (h2 + 1) * 512],
                                     start=(kc == 0), stop=(kc == NCH - 1))
                return inst
            w = [("ldwv", 16)]
            if idx >= 2:
                w.append(("vev", idx - 1))
            p.step("tensor", vmm, waits=w, incs=[("vmm", 1)])

            def vev(sc, tc=tc, h2=h2, idx=idx):
                return sc.copy(v_tm[:, tc * D + h2 * 512:tc * D + (h2 + 1) * 512],
                               pb[2 + idx % 2][:])
            p.step("scalar", vev, waits=[("vmm", None)], incs=[("vev", 1)])

    p.rel("wk_sb")

    def k_h(h, a, b):
        return k_fm[(h % 2) * 64:(h % 2) * 64 + 64,
                    (h // 2) * NT + a:(h // 2) * NT + b]

    ksq = p.sbx("ksq", [128, NT], BF16)
    for h in range(16):
        hb = (h % 2) * 64

        def sqk(sc, h=h, hb=hb):
            return sc.activation(ksq[hb:hb + 64, :], k_h(h, 0, NT), AF.Square)
        w = [("kev", 32)]
        if h >= 1:
            w.append(("dtr", h * 16))
        p.step("scalar", sqk, waits=w, incs=[("sqk", 1)])

        for tc in range(16):
            gidx = h * 16 + tc

            def dtr(te, tc=tc, gidx=gidx, hb=hb):
                return te.transpose((tsA if gidx % 2 == 0 else tsB)[:, 0:64],
                                    ksq[hb:hb + 64, tc * 128:(tc + 1) * 128],
                                    p.ident_bf[hb:hb + 64, hb:hb + 64])
            w = [("sqk", h + 1)]
            if gidx >= 2:
                w.append(("dred", gidx - 1))
            p.step("tensor", dtr, waits=w, incs=[("dtr", 1)])

            def dred(v, h=h, tc=tc, gidx=gidx):
                return v.tensor_reduce(dg_all[:, tc * 16 + h:tc * 16 + h + 1],
                                       (tsA if gidx % 2 == 0 else tsB)[:, 0:64],
                                       AX.X, ALU.add)
            p.step("vector", dred, waits=[("dtr", None)], incs=[("dred", 1)])

    def locmin(v):
        return v.tensor_reduce(stab_c[:, 1:2], dg_all[:], AX.X, ALU.min)
    p.step("vector", locmin, waits=[("dred", None)], incs=[("lmn", 1)])

    def trmn(te):
        return te.transpose(psT2, stab_c[:, 1:2], p.ident_f[:])
    p.step("tensor", trmn, waits=[("lmn", None), ("ln2gm", 0)], incs=[("trmn", 1)])

    def redmn(v):
        v.tensor_reduce(onec[0:1, 0:1], psT2, AX.X, ALU.min)
        return v.tensor_scalar(onec[0:1, 1:2], onec[0:1, 0:1], -0.5, None, ALU.mult)
    p.step("vector", redmn, waits=[("trmn", None)], incs=[("redmn", 1)])

    emit_armax_scalar(p, "st2", onec[0:1, 1:2], "redmn", onec[0:1, 2:3])

    def repst(te):
        return te.matmul(psR2[:, 0:1], p.ones_r[:], onec[0:1, 2:3],
                         start=True, stop=True)
    p.step("tensor", repst, waits=[("st2out", 16), ("ln2repev", 8)],
           incs=[("repst", 1)])

    def repstev(sc):
        return sc.copy(stab_c[:, 0:1], psR2[:, 0:1])
    p.step("scalar", repstev, waits=[("repst", None)], incs=[("evst", 1)])

    def mkball(v):
        v.tensor_scalar(dg_all[:], dg_all[:], -0.5, None, ALU.mult)
        return v.tensor_scalar(dg_all[:], dg_all[:], stab_c[:, 0:1], None,
                               ALU.subtract)
    p.step("vector", mkball, waits=[("evst", None)], incs=[("ball", 1)])

    kfT = p.sbx("kfT", [128, 16 * 256], BF16)
    vhx = p.sbx("vhx", [128, 16 * 65], BF16)
    for h in range(16):
        def mkvhx(v, h=h):
            v.memset(vhx[:], 1.0)
            for tc in range(16):
                inst = v.tensor_copy(
                    vhx[:, tc * 65:tc * 65 + 64],
                    v_tm[:, tc * D + (h // 2) * 128 + (h % 2) * 64:
                         tc * D + (h // 2) * 128 + (h % 2) * 64 + 64])
            return inst
        w = [("vev", 32)]
        if h >= 1:
            w.append(("cmm", 2 * h))
        p.step("vector", mkvhx, waits=w, incs=[("vhx", 1)])

        for tc in range(16):
            gidx = h * 16 + tc

            def kfmm(te, h=h, tc=tc):
                hb = (h % 2) * 64
                return te.matmul(pb[0][:, 0:256], k_h(h, tc * 128, (tc + 1) * 128),
                                 prj_sb[hb:hb + 64, :], start=True, stop=True)
            w = [("ldprj", 32), ("ball", 1)]
            if gidx >= 1:
                w.append(("kfev", gidx))
            p.step("tensor", kfmm, waits=w, incs=[("kfmm", 1)])

            def kfev(sc, h=h, tc=tc):
                return sc.activation(kfT[:, tc * 256:(tc + 1) * 256],
                                     pb[0][:, 0:256], AF.Exp,
                                     bias=dg_all[:, tc * 16 + h:tc * 16 + h + 1],
                                     scale=dn_col[:, 0:1])
            p.step("scalar", kfev, waits=[("kfmm", None)], incs=[("kfev", 1)])

            def kfeps(v, tc=tc):
                sl = kfT[:, tc * 256:(tc + 1) * 256]
                return v.tensor_scalar(sl, sl, 1e-3, 0.0625, ALU.add, ALU.mult)
            p.step("vector", kfeps, waits=[("kfev", None)], incs=[("kfeps", 1)])

        for mc in range(2):
            def cmm(te, h=h, mc=mc):
                for tc in range(16):
                    inst = te.matmul(pb[1 + mc][:, 0:65],
                                     kfT[:, tc * 256 + mc * 128:tc * 256 + (mc + 1) * 128],
                                     vhx[:, tc * 65:(tc + 1) * 65],
                                     start=(tc == 0), stop=(tc == 15))
                return inst
            w = [("kfeps", 16 * (h + 1)), ("vhx", h + 1)]
            if h >= 1:
                w.append(("cev", 2 * h))
            p.step("tensor", cmm, waits=w, incs=[("cmm", 1)])

            def cev(sc, h=h, mc=mc):
                return sc.copy(ctx_all[:, h * 130 + mc * 65:h * 130 + (mc + 1) * 65],
                               pb[1 + mc][:, 0:65])
            p.step("scalar", cev, waits=[("cmm", None)], incs=[("cev", 1)])

    p.rel("ksq", "v_tm")

    _sd, _vd = p.sem("ccd"), p.av("ccd", 16)
    _sc, _vc = p.sem("ccc"), p.av("ccc", 1)
    p.mark("ccd", 16)
    p.mark("ccc", 1)
    def ccctx(g, _sd=_sd, _vd=_vd, _sc=_sc, _vc=_vc):
        g.dma_start(out=ccci[:, 0:2080], in_=ctx_all[:]).then_inc(_sd, 16)
        g.wait_ge(_sd, _vd)
        g.collective_compute("AllReduce", ALU.add, replica_groups=PAIRS,
                             ins=[ccci.opt()], outs=[ccco.opt()]).then_inc(_sc, 1)
        g.wait_ge(_sc, _vc)
        return g.dma_start(out=ctx_all[:], in_=ccco[:, 0:2080])
    p.step("gpsimd", ccctx, waits=[("cev", None)], incs=[("cco", 16)])

    def ctxbf(v):
        return v.tensor_copy(ctx_bf[:], ctx_all[:])
    p.step("vector", ctxbf, waits=[("cco", None)], incs=[("cbf", 1)])

    if stop == "ctx":
        return _mk_finish(p, x_sb, xo)([("cbf", None)])

    p.rel("k_fm", "kfT", "vhx")
    q_fm = p.sbx("q_fm", [128, NCH * NT], BF16)
    wq_2 = [p.sbx("wq_sbA", [128, 4 * D], BF16), p.sbx("wq_sbB", [128, 4 * D], BF16)]

    def wq_sl(kc, a, b):
        return wq_2[kc // 4][:, (kc % 4) * D + a:(kc % 4) * D + b]
    wq3 = wqT.rearrange("(k p) m -> p k m", p=128)

    def ldwq(g):
        return [g.dma_start(out=wq_2[0][:], in_=wq3[:, 0:4, :]),
                g.dma_start(out=wq_2[1][:], in_=wq3[:, 4:8, :])]
    p.step("gpsimd", ldwq, waits=[("kfmm", 256), ("cmm", 32), ("dtr", 256)],
           incs=[("ldwq", 32)])

    for oc in range(NCH):
        for n in range(4):
            idx = oc * 4 + n

            def qmm(te, oc=oc, n=n, idx=idx):
                for kc in range(NCH):
                    inst = te.matmul(pb[idx % 2][:],
                                     wq_sl(kc, oc * 128, (oc + 1) * 128),
                                     xn_bf[:, kc * NT + n * 512:kc * NT + (n + 1) * 512],
                                     start=(kc == 0), stop=(kc == NCH - 1))
                return inst
            w = [("ldwq", 32), ("cev", 32)]
            if idx >= 2:
                w.append(("qev", idx - 1))
            p.step("tensor", qmm, waits=w, incs=[("qmm", 1)])

            def qev(sc, oc=oc, n=n, idx=idx):
                return sc.activation(q_fm[:, oc * NT + n * 512:oc * NT + (n + 1) * 512],
                                     pb[idx % 2][:], AF.Identity,
                                     bias=bq_s[:, oc:oc + 1])
            p.step("scalar", qev, waits=[("qmm", None), ("sm_ok", None)],
                   incs=[("qev", 1)])

    def q_h(h, a, b):
        return q_fm[(h % 2) * 64:(h % 2) * 64 + 64,
                    (h // 2) * NT + a:(h // 2) * NT + b]

    attn_fm = p.sbx("attn_fm", [128, NCH * NT], BF16)
    qf_sb = p.sbx("qf_sb", [128, 2 * NT], BF16)
    onum_sb = p.sbx("onum_sb", [128, 512], BF16)
    rrep_sb = p.sbx("rrep_sb", [128, 512])
    den_r = p.sb("den_r", [1, 512])

    for h in range(16):
        for mc in range(2):
            for n in range(4):
                idx = h * 8 + mc * 4 + n

                def qfmm(te, h=h, mc=mc, n=n):
                    hb = (h % 2) * 64
                    return te.matmul(pb[0][:],
                                     prj_sb[hb:hb + 64, mc * 128:(mc + 1) * 128],
                                     q_h(h, n * 512, (n + 1) * 512),
                                     start=True, stop=True)
                w = [("qev", 32)]
                if idx >= 1:
                    w.append(("qfev", idx))
                p.step("tensor", qfmm, waits=w, incs=[("qfmm", 1)])

                def qfev(sc, mc=mc, n=n):
                    return sc.activation(qf_sb[:, mc * NT + n * 512:mc * NT + (n + 1) * 512],
                                         pb[0][:], AF.Exp, scale=dn_col[:, 0:1])
                p.step("scalar", qfev, waits=[("qfmm", None)], incs=[("qfev", 1)])

                def qfeps(v, mc=mc, n=n):
                    sl = qf_sb[:, mc * NT + n * 512:mc * NT + (n + 1) * 512]
                    return v.tensor_scalar(sl, sl, 1e-3, 0.0625, ALU.add, ALU.mult)
                p.step("vector", qfeps, waits=[("qfev", None)], incs=[("qfeps", 1)])

        for n in range(4):
            sidx = h * 4 + n

            def onmm(te, h=h, n=n):
                hb = (h % 2) * 64
                for mc in range(2):
                    te.matmul(pb[1][hb:hb + 64, :],
                              ctx_bf[:, h * 130 + mc * 65:h * 130 + mc * 65 + 64],
                              qf_sb[:, mc * NT + n * 512:mc * NT + (n + 1) * 512],
                              start=(mc == 0), stop=(mc == 1), skip_group_check=True,
                              tile_position=(0, hb))
                for mc in range(2):
                    inst = te.matmul(pb[2][0:1, :],
                                     ctx_bf[:, h * 130 + mc * 65 + 64:h * 130 + mc * 65 + 65],
                                     qf_sb[:, mc * NT + n * 512:mc * NT + (n + 1) * 512],
                                     start=(mc == 0), stop=(mc == 1),
                                     skip_group_check=True)
                return inst
            w = [("qfeps", 8 * (h + 1)), ("cbf", 1)]
            if sidx >= 1:
                w.append(("att", sidx))
            p.step("tensor", onmm, waits=w, incs=[("onmm", 1)])

            def onev(sc, h=h, n=n):
                hb = (h % 2) * 64
                return sc.copy(onum_sb[hb:hb + 64, 0:512], pb[1][hb:hb + 64, 0:512])
            p.step("scalar", onev, waits=[("onmm", None)], incs=[("onev", 1)])

            def denev(v, n=n):
                v.tensor_copy(den_r[0:1, :], pb[2][0:1, :])
                return v.reciprocal(den_r[0:1, :], den_r[0:1, :])
            p.step("vector", denev, waits=[("onmm", None)], incs=[("denev", 1)])

            def rrep(te, h=h, n=n):
                hb = (h % 2) * 64
                return te.matmul(pb[3][hb:hb + 64, :], p.ones_r[0:1, 0:64],
                                 den_r[0:1, :], start=True, stop=True,
                                 tile_position=(0, hb))
            p.step("tensor", rrep, waits=[("denev", None)], incs=[("rrep", 1)])

            def rrepev(sc, h=h, n=n):
                hb = (h % 2) * 64
                return sc.copy(rrep_sb[hb:hb + 64, :], pb[3][hb:hb + 64, :])
            p.step("scalar", rrepev, waits=[("rrep", None)], incs=[("rrepev", 1)])

            def attv(v, h=h, n=n):
                hb = (h % 2) * 64
                sl = attn_fm[hb:hb + 64,
                             (h // 2) * NT + n * 512:(h // 2) * NT + (n + 1) * 512]
                v.tensor_tensor(sl, onum_sb[hb:hb + 64, 0:512],
                                rrep_sb[hb:hb + 64, :], ALU.mult)
                return v.tensor_scalar(
                    sl, sl,
                    bv_s[(h % 2) * 64:(h % 2) * 64 + 64, h // 2:h // 2 + 1],
                    None, ALU.add)
            p.step("vector", attv, waits=[("onev", None), ("rrepev", None)],
                   incs=[("att", 1)])


    if stop == "heads":
        return _mk_finish(p, x_sb, xo)([("att", None)])

    p.rel("qf_sb")
    wo3 = woT.rearrange("(k p) m -> p k m", p=128)

    def ldwo(g):
        return [g.dma_start(out=wq_2[0][:], in_=wo3[:, 0:4, :]),
                g.dma_start(out=wq_2[1][:], in_=wo3[:, 4:8, :])]
    p.step("gpsimd", ldwo, waits=[("qmm", 32)], incs=[("ldwo", 32)])

    for dc in range(NCH):
        for n in range(4):
            idx = dc * 4 + n

            def womm(te, dc=dc, n=n, idx=idx):
                for kc in range(NCH):
                    inst = te.matmul(pb[idx % 2][:],
                                     wq_sl(kc, dc * 128, (dc + 1) * 128),
                                     attn_fm[:, kc * NT + n * 512:kc * NT + (n + 1) * 512],
                                     start=(kc == 0), stop=(kc == NCH - 1))
                return inst
            w = [("ldwo", 32), ("att", 64)]
            if idx >= 2:
                w.append(("woev", idx - 1))
            p.step("tensor", womm, waits=w, incs=[("womm", 1)])

            def woev(v, dc=dc, n=n, idx=idx):
                return v.scalar_tensor_tensor(
                    x_sb[:, dc * NT + n * 512:dc * NT + (n + 1) * 512],
                    pb[idx % 2][:], bo_s[:, dc:dc + 1],
                    x_sb[:, dc * NT + n * 512:dc * NT + (n + 1) * 512],
                    ALU.add, ALU.add)
            p.step("vector", woev, waits=[("womm", None)], incs=[("woev", 1)])

    if stop == "attn":
        return _mk_finish(p, x_sb, xo)([("woev", None)])

    p.rel("attn_fm", "onum_sb", "rrep_sb", "wq_sbA", "wq_sbB", "q_fm",
          "ctx_all", "ctx_bf")
    wg_sb = p.sb("wg_sb", [128, NCH * 16], BF16)
    ebias_b = p.sb("ebias_b", [128, 16])
    wgg3 = inp["gwT"].rearrange("(k p) m -> p k m", p=128)

    def ldwg16(g):
        return g.dma_start(out=wg_sb[:], in_=wgg3[:, :, :])
    p.step("gpsimd", ldwg16, waits=[("womm", 32), ("woev", 32)],
           incs=[("ldwg16", 16)])

    def ldeb(sy):
        return sy.dma_start(out=ebias_b[0:1, :], in_=inp["ebias"][:])
    p.step("sync", ldeb, waits=[("womm", 32), ("att", 64)], incs=[("ldeb", 16)])

    def repeb(te):
        return te.matmul(psR2[:, 0:16], p.ones_r[:], ebias_b[0:1, :],
                         start=True, stop=True)
    p.step("tensor", repeb, waits=[("ldeb", 16), ("repst", 1)], incs=[("repeb", 1)])

    def repebv(sc):
        return sc.copy(ebias_b[:], psR2[:, 0:16])
    p.step("scalar", repebv, waits=[("repeb", None)], incs=[("repebv", 1)])

    scFM = p.sb("scFM", [16, NT])
    for n in range(4):
        def smm(te, n=n):
            for kc in range(NCH):
                inst = te.matmul(pb[2][0:16, 0:512],
                                 wg_sb[:, kc * 16:(kc + 1) * 16],
                                 xn_bf[:, kc * NT + n * 512:kc * NT + (n + 1) * 512],
                                 start=(kc == 0), stop=(kc == NCH - 1))
            return inst
        w = [("ldwg16", 16), ("woev", 32)]
        if n >= 1:
            w.append(("sev2", n))
        p.step("tensor", smm, waits=w, incs=[("smm2", 1)])

        def sev2(sc, n=n):
            return sc.copy(scFM[0:16, n * 512:(n + 1) * 512], pb[2][0:16, :])
        p.step("scalar", sev2, waits=[("smm2", None)], incs=[("sev2", 1)])

    cw_tm = p.sb("cw_tm", [128, 16 * 16])
    sel_tm = p.sb("sel_tm", [128, 16])
    rnk = p.sb("rnk", [128, 16])
    scc = p.sb("scc", [128, 4])
    for tc in range(16):
        def str_(te, tc=tc):
            return te.transpose(tsF[:, 0:16],
                                scFM[0:16, tc * 128:(tc + 1) * 128],
                                p.ident_f[0:16, 0:16])
        w = [("sev2", 4), ("dtr", 256)]
        if tc >= 1:
            w.append(("stv", tc))
        p.step("tensor", str_, waits=w, incs=[("str", 1)])

        def stv(v, tc=tc):
            return v.tensor_copy(cw_tm[:, tc * 16:(tc + 1) * 16], tsF[:, 0:16])
        p.step("vector", stv, waits=[("str", None)], incs=[("stv", 1)])

        def sgn(sc, tc=tc):
            t16 = cw_tm[:, tc * 16:(tc + 1) * 16]
            return sc.activation(t16, t16, AF.Sigmoid)
        p.step("scalar", sgn, waits=[("stv", None)], incs=[("sgn", 1)])

        def snorm(v, tc=tc):
            t16 = cw_tm[:, tc * 16:(tc + 1) * 16]
            v.tensor_tensor(sel_tm[:], t16, ebias_b[:], ALU.add)
            v.memset(rnk[:], 0.0)
            for e in range(16):
                v.scalar_tensor_tensor(rnk[:], sel_tm[:], sel_tm[:, e:e + 1], rnk[:],
                                       ALU.is_lt, ALU.add)
            v.tensor_scalar(rnk[:], rnk[:], 8.0, None, ALU.is_lt)
            v.tensor_mul(t16, t16, rnk[:])
            v.tensor_reduce(scc[:, 0:1], t16, AX.X, ALU.add)
            v.reciprocal(scc[:, 1:2], scc[:, 0:1])
            v.tensor_scalar(scc[:, 1:2], scc[:, 1:2], 2.826, None, ALU.mult)
            return v.tensor_scalar(t16, t16, scc[:, 1:2], None, ALU.mult)
        p.step("vector", snorm, waits=[("sgn", None), ("repebv", 1)],
               incs=[("vsn", 1)])

    cwFM = p.sb("cwFM", [16, NT])
    for tc in range(16):
        def ctr(te, tc=tc):
            return te.transpose(tsF[0:16, :], cw_tm[:, tc * 16:(tc + 1) * 16],
                                p.ident_f[:])
        w = [("vsn", 16)]
        if tc >= 1:
            w.append(("ctv", tc))
        p.step("tensor", ctr, waits=w, incs=[("ctr", 1)])

        def ctv(v, tc=tc):
            return v.tensor_copy(cwFM[0:16, tc * 128:(tc + 1) * 128], tsF[0:16, :])
        p.step("vector", ctv, waits=[("ctr", None)], incs=[("ctv", 1)])

    if stop == "route":
        return _mk_finish(p, x_sb, xo)([("ctv", None), ("woev", None)])

    p.rel("scFM")
    cwrow = p.sb("cwrow", [1, NT])
    cwrep = p.sb("cwrep", [128, NT])
    xfc2 = [p.sb("xfcA", [128, 4 * NT], BF16), p.sb("xfcB", [128, 4 * NT], BF16)]
    he2 = [p.sb("heA", [128, 4 * NT], BF16), p.sb("heB", [128, 4 * NT], BF16)]

    def xfc_sl(k, a, b):
        return xfc2[k // 4][:, (k % 4) * NT + a:(k % 4) * NT + b]

    def he_sl(k, a, b):
        return he2[k // 4][:, (k % 4) * NT + a:(k % 4) * NT + b]
    gs_f = p.sb("gs_f", [128, 512])
    wsl1 = p.sb("wsl1", [128, 2 * NCH * 128], BF16)
    wge4 = wge.rearrange("e (k p) m -> e p k m", p=128)
    wue4 = wue.rearrange("e (k p) m -> e p k m", p=128)
    wde4 = wde.rearrange("e (k p) m -> e p k m", p=128)
    wgs3 = wgs.rearrange("(k p) m -> p k m", p=128)
    wus3 = wus.rearrange("(k p) m -> p k m", p=128)
    wds3 = wds.rearrange("(k p) m -> p k m", p=128)

    for e in range(17):
        shared = (e == 16)
        if not shared:
            def cwr(sy, e=e):
                return sy.dma_start(out=cwrow[:], in_=cwFM[e:e + 1, :])
            w = [("ctv", 16)]
            if e >= 1:
                w.append(("vxf", e * NCH))
            p.step("sync", cwr, waits=w, incs=[("cwr", 16)])

            for n in range(4):
                def crep(te, e=e, n=n):
                    return te.matmul(psR2[:], p.ones_r[:],
                                     cwrow[0:1, n * 512:(n + 1) * 512],
                                     start=True, stop=True)
                w = [("cwr", 16 * (e + 1)), ("repebv", 1)]
                if e * 4 + n >= 1:
                    w.append(("crev", e * 4 + n))
                if e >= 1 and n == 0:
                    w.append(("vxf", NCH * e))
                p.step("tensor", crep, waits=w, incs=[("crmm", 1)])

                def crev(sc, e=e, n=n):
                    return sc.copy(cwrep[:, n * 512:(n + 1) * 512], psR2[:])
                p.step("scalar", crev, waits=[("crmm", None)], incs=[("crev", 1)])

            def mkxfc(v, e=e):
                for k in range(NCH):
                    inst = v.tensor_tensor(xfc_sl(k, 0, NT),
                                           xn_bf[:, k * NT:(k + 1) * NT],
                                           cwrep[:], ALU.mult)
                return inst
            w = [("crev", 4 * (e + 1))]
            if e >= 1:
                w.append(("umm2", 32 * e))
            p.step("vector", mkxfc, waits=w, incs=[("vxf", NCH)])

        for m in range(NCH):
            gm = e * NCH + m
            wbuf = (gm % 2) * NCH * 128

            def ldg(g, e=e, m=m, wbuf=wbuf):
                src = wgs3[:, :, m * 128:(m + 1) * 128] if e == 16 else \
                    wge4[e][:, :, m * 128:(m + 1) * 128]
                return g.dma_start(out=wsl1[:, wbuf:wbuf + NCH * 128], in_=src)
            w = []
            if gm == 0:
                w = [("womm", 32), ("woev", 32), ("att", 64)]
            if gm >= 2:
                w.append(("umm", (gm - 1) * 4))
            if e >= 1 and m < 2:
                w.append(("zmm", 32 * e))
            p.step("gpsimd", ldg, waits=w, incs=[("ldw1", 16)])

            for n in range(4):
                def umm(te, e=e, m=m, n=n, wbuf=wbuf, gm=gm):
                    for k in range(NCH):
                        inst = te.matmul(pb[n % 2][:],
                                         wsl1[:, wbuf + k * 128:wbuf + (k + 1) * 128],
                                         xn_bf[:, k * NT + n * 512:k * NT + (n + 1) * 512],
                                         start=(k == 0), stop=(k == NCH - 1))
                    return inst
                w = [("ldw1", 16 * (gm + 1))]
                if gm == 0 and n < 2:
                    w.append(("woev", 32))
                if e >= 1 and m == 0 and n < 2:
                    w.append(("zev", 32 * e))
                t = gm * 4 + n - 2
                if t >= 0:
                    w.append(("uev", t + 1))
                p.step("tensor", umm, waits=w, incs=[("umm", 1)])

                def uev(sc, e=e, m=m, n=n):
                    return sc.activation(gs_f[:], pb[n % 2][:], AF.Silu)
                wv = [("umm", None)]
                if e * 32 + m * 4 + n >= 1:
                    wv.append(("vhe", e * 32 + m * 4 + n))
                p.step("scalar", uev, waits=wv, incs=[("uev", 1)])

                def hemul(v, e=e, m=m, n=n):
                    sl = he_sl(m, n * 512, (n + 1) * 512)
                    return v.tensor_copy(sl, gs_f[:])
                wv = [("uev", None)]
                if e >= 1 and m == 0 and n == 0:
                    wv.append(("zmm", 32 * e))
                p.step("vector", hemul, waits=wv, incs=[("vhe", 1)])

        for m in range(NCH):
            gm = e * NCH + m
            wbuf = (gm % 2) * NCH * 128

            def ldu(g, e=e, m=m, wbuf=wbuf):
                src = wus3[:, :, m * 128:(m + 1) * 128] if e == 16 else \
                    wue4[e][:, :, m * 128:(m + 1) * 128]
                return g.dma_start(out=wsl1[:, wbuf:wbuf + NCH * 128], in_=src)
            w = [("umm", (e + 1) * 32)] if m < 2 else [("umm2", (e * NCH + m - 1) * 4)]
            p.step("gpsimd", ldu, waits=w, incs=[("ldw2", 16)])

            for n in range(4):
                def umm2(te, e=e, m=m, n=n, wbuf=wbuf, gm=gm, shared=shared):
                    for k in range(NCH):
                        src = (xn_bf[:, k * NT + n * 512:k * NT + (n + 1) * 512]
                               if shared else xfc_sl(k, n * 512, (n + 1) * 512))
                        inst = te.matmul(pb[2 + n % 2][:],
                                         wsl1[:, wbuf + k * 128:wbuf + (k + 1) * 128],
                                         src,
                                         start=(k == 0), stop=(k == NCH - 1))
                    return inst
                w = [("ldw2", 16 * (gm + 1))]
                if not shared:
                    w.append(("vxf", NCH * (e + 1)))
                t = gm * 4 + n - 2
                if t >= 0:
                    w.append(("uev2", t + 1))
                elif e == 0 and m == 0:
                    w.append(("att", 64))
                p.step("tensor", umm2, waits=w, incs=[("umm2", 1)])

                def uev2(v, e=e, m=m, n=n):
                    sl = he_sl(m, n * 512, (n + 1) * 512)
                    return v.tensor_tensor(sl, sl, pb[2 + n % 2][:], ALU.mult)
                p.step("vector", uev2, waits=[("umm2", None), ("vhe", e * 32 + m * 4 + n + 1)],
                       incs=[("uev2", 1)])

        for dc in range(NCH):
            gd = e * NCH + dc
            zbuf = (gd % 2) * NCH * 128

            def ldd(g, e=e, dc=dc, zbuf=zbuf):
                src = wds3[:, :, dc * 128:(dc + 1) * 128] if e == 16 else \
                    wde4[e][:, :, dc * 128:(dc + 1) * 128]
                return g.dma_start(out=wsl1[:, zbuf:zbuf + NCH * 128], in_=src)
            w = [("umm2", (e + 1) * 32)] if dc < 2 else [("zmm", (e * NCH + dc - 1) * 4)]
            p.step("gpsimd", ldd, waits=w, incs=[("ldw3", 16)])

            for n in range(4):
                idx = e * 32 + dc * 4 + n

                def zmm(te, e=e, dc=dc, n=n, zbuf=zbuf, gd=gd):
                    for k in range(NCH):
                        inst = te.matmul(pb[n % 2][:],
                                         wsl1[:, zbuf + k * 128:zbuf + (k + 1) * 128],
                                         he_sl(k, n * 512, (n + 1) * 512),
                                         start=(k == 0), stop=(k == NCH - 1))
                    return inst
                w = [("ldw3", 16 * (gd + 1)), ("uev2", 32 * (e + 1))]
                prev = idx - 2
                if prev >= 0:
                    w.append(("zev", prev + 1))
                p.step("tensor", zmm, waits=w, incs=[("zmm", 1)])

                def zev(v, e=e, dc=dc, n=n):
                    return v.scalar_tensor_tensor(
                        x_sb[:, dc * NT + n * 512:dc * NT + (n + 1) * 512],
                        pb[n % 2][:], 0.0,
                        x_sb[:, dc * NT + n * 512:dc * NT + (n + 1) * 512],
                        ALU.add, ALU.add)
                p.step("vector", zev, waits=[("zmm", None)], incs=[("zev", 1)])

    return _mk_finish(p, x_sb, xo)([("zev", None), ("woev", None)])



def build_L3(stop="full", fx=None):
    p = Prog() if fx is None else fx["p"]
    nc = p.nc
    inp = {}

    def ein(name, shape, dtype=F32):
        inp[name] = nc.dram_tensor(p.px + name, shape, dtype,
                                   kind="ExternalInput").ap()
        return inp[name]

    if fx is None:
        x_in = ein("x", [D, NT])
    wq = ein("wq", [D, 64])
    wk = ein("wk", [D, 64])
    wv = ein("wv", [D, 64])
    rwT = ein("rwT", [D, 8])
    gwe = ein("gwe", [8, D, 4096])
    uwe = ein("uwe", [8, D, 4096])
    dwe = ein("dwe", [8, 4096, D])
    xo = nc.dram_tensor("xo", [D, NT], F32,
                        kind="ExternalOutput").ap() if fx is None else None

    kag_i = p.dram("kagi", [64, NT])
    kag_o = p.dram("kago", [128, NT])
    vag_i = p.dram("vagi", [64, NT])
    vag_o = p.dram("vago", [128, NT])
    mx_i = p.dram("mxi", [128, 32])
    mx_o = p.dram("mxo", [128, 32])
    es_i = p.dram("esi", [128, 32])
    es_o = p.dram("eso", [128, 32])
    cst_i = p.dram("csti", [1, 16])
    cst_o = p.dram("csto", [1, 16])

    if fx is None:
        x_sb = p.sb("x_sb", [128, NCH * NT])
        xn_bf = p.sb("xn_bf", [128, NCH * NT], BF16)
    else:
        x_sb, xn_bf = fx["x_sb"], fx["xn_bf"]

    if fx is None:
        emit_common(p)
        x3 = x_in.rearrange("(k p) n -> k p n", p=128)

        def ldx(sy):
            return [sy.dma_start(out=x_sb[:, k * NT:(k + 1) * NT], in_=x3[k])
                    for k in range(NCH)]
        p.step("sync", ldx, incs=[("x_ok", 16 * NCH)])

    ln = emit_ln(p, "ln3", x_sb, xn_bf, "x_ok")
    p.rel(*ln["scoped"])
    p.rel(*ln["psums"])
    p.rel("ln3rows", "ln3sumx", "ln3sumsq")

    if stop == "ln":
        return _mk_finish(p, x_sb, xo)([("ln3xn", 8)])

    wqkv_sb = p.sbx("wqkv_sb", [128, 3 * NCH * 64], BF16)
    q2 = p.sbx("q2", [128, NT], BF16)
    k_loc = p.sbx("k_loc", [64, NT])
    v_loc = p.sbx("v_loc", [64, NT])
    kag_sb = p.sbx("kag_sb", [128, NT], BF16)
    vag_sb = p.sbx("vag_sb", [128, NT], BF16)
    vT_sb = p.sbx("vT_sb", [128, 32 * 64], BF16)
    vTs = p.sbx("vTs", [128, 32 * 64], BF16)
    av2 = p.sbx("av2", [128, NT])
    es2 = p.sbx("es2", [128, 2 * 512])
    stats = p.sbx("stats", [128, 128])
    rs_sb = p.sbx("rs_sb", [128, 32])
    scw = p.sbx("scw", [128, 8])
    PT2 = [p.sbx("ptA", [128, 512], BF16), p.sbx("ptB", [128, 512], BF16)]

    pa = [p.psx("pa0", [128, 512]), p.psx("pa1", [128, 512])]
    pav = p.psx("pav", [64, 512])
    ptr = [p.psx("ptr0", [128, 64], BF16), p.psx("ptr1", [128, 64], BF16)]

    wq3 = wq.rearrange("(k p) m -> p k m", p=128)
    wk3 = wk.rearrange("(k p) m -> p k m", p=128)
    wv3 = wv.rearrange("(k p) m -> p k m", p=128)

    def ldqkv(g):
        return [g.dma_start(out=wqkv_sb[:, 0:NCH * 64], in_=wq3[:, :, :]),
                g.dma_start(out=wqkv_sb[:, NCH * 64:2 * NCH * 64], in_=wk3[:, :, :]),
                g.dma_start(out=wqkv_sb[:, 2 * NCH * 64:3 * NCH * 64], in_=wv3[:, :, :])]
    p.step("gpsimd", ldqkv, waits=[("common", 1)], incs=[("ldqkv", 48)])

    for gi in range(12):
        which, s = gi // 4, gi % 4

        def qkvm(te, which=which, s=s, gi=gi):
            for kc in range(NCH):
                inst = te.matmul(pa[s % 2][0:64, :],
                                 wqkv_sb[:, which * NCH * 64 + kc * 64:
                                         which * NCH * 64 + (kc + 1) * 64],
                                 xn_bf[:, kc * NT + s * 512:kc * NT + (s + 1) * 512],
                                 start=(kc == 0), stop=(kc == NCH - 1))
            return inst
        w = [("ldqkv", 48), ("ln3xn", 8)]
        if gi >= 2:
            w.append(("qkv_ev", gi - 1))
        p.step("tensor", qkvm, waits=w, incs=[("qkvm", 1)])

        def qkve(sc, which=which, s=s):
            if which == 0:
                return sc.copy(q2[0:64, s * 512:(s + 1) * 512], pa[s % 2][0:64, :])
            dst = k_loc if which == 1 else v_loc
            return sc.copy(dst[:, s * 512:(s + 1) * 512], pa[s % 2][0:64, :])
        p.step("scalar", qkve, waits=[("qkvm", None)], incs=[("qkv_ev", 1)])

    def q2cp(sy):
        return sy.dma_start(out=q2[64:128, :], in_=q2[0:64, :])
    p.step("sync", q2cp, waits=[("qkv_ev", 4)], incs=[("qcp", 16)])

    _sd, _vd = p.sem("agkd"), p.av("agkd", 16)
    _sc, _vc = p.sem("agkc"), p.av("agkc", 1)
    p.mark("agkd", 16)
    p.mark("agkc", 1)
    def agk(g, _sd=_sd, _vd=_vd, _sc=_sc, _vc=_vc):
        g.dma_start(out=kag_i[:], in_=k_loc[:]).then_inc(_sd, 16)
        g.wait_ge(_sd, _vd)
        g.collective_compute("AllGather", ALU.bypass, replica_groups=PAIRS,
                             ins=[kag_i.opt()], outs=[kag_o.opt()]).then_inc(
            _sc, 1)
        g.wait_ge(_sc, _vc)
        return g.dma_start(out=kag_sb[:], in_=kag_o[:])
    p.step("gpsimd", agk, waits=[("qkv_ev", 8)], incs=[("agk", 16)])

    _sd, _vd = p.sem("agvd"), p.av("agvd", 16)
    _sc, _vc = p.sem("agvc"), p.av("agvc", 1)
    p.mark("agvd", 16)
    p.mark("agvc", 1)
    def agv(g, _sd=_sd, _vd=_vd, _sc=_sc, _vc=_vc):
        g.dma_start(out=vag_i[:], in_=v_loc[:]).then_inc(_sd, 16)
        g.wait_ge(_sd, _vd)
        g.collective_compute("AllGather", ALU.bypass, replica_groups=PAIRS,
                             ins=[vag_i.opt()], outs=[vag_o.opt()]).then_inc(
            _sc, 1)
        g.wait_ge(_sc, _vc)
        return g.dma_start(out=vag_sb[:], in_=vag_o[:])
    p.step("gpsimd", agv, waits=[("qkv_ev", 12)], incs=[("agv", 16)])

    def kchunk(c):
        hb, cc = (c // 16) * 64, c % 16
        return kag_sb[hb:hb + 64, cc * 128:(cc + 1) * 128], hb

    for c in range(32):
        for s in range(4):
            a1 = c * 4 + s
            ksl, hb = kchunk(c)

            def am1(te, ksl=ksl, hb=hb, s=s):
                return te.matmul(pa[s % 2][:], ksl,
                                 q2[hb:hb + 64, s * 512:(s + 1) * 512],
                                 start=True, stop=True)
            w = []
            if a1 < 2:
                w = [("agk", 16), ("qcp", 16), ("qkv_ev", 12)]
            else:
                w.append(("p1v", a1 - 1))
            p.step("tensor", am1, waits=w, incs=[("p1m", 1)])

            def rv1(v, c=c, s=s):
                return v.tensor_reduce(scw[:, s:s + 1], pa[s % 2][:], AX.X,
                                       ALU.max)
            p.step("vector", rv1, waits=[("p1m", None)], incs=[("p1v", 1)])
            if s == 3:
                def rv1c(v, c=c):
                    return v.tensor_reduce(stats[:, c:c + 1], scw[:, 0:4],
                                           AX.X, ALU.max)
                p.step("vector", rv1c, incs=[("p1c", 1)])

    _sd, _vd = p.sem("amxd"), p.av("amxd", 16)
    _sc, _vc = p.sem("amxc"), p.av("amxc", 1)
    p.mark("amxd", 16)
    p.mark("amxc", 1)
    def armx(g, _sd=_sd, _vd=_vd, _sc=_sc, _vc=_vc):
        g.dma_start(out=mx_i[:], in_=stats[:, 0:32]).then_inc(_sd, 16)
        g.wait_ge(_sd, _vd)
        g.collective_compute("AllReduce", ALU.max, replica_groups=PAIRS,
                             ins=[mx_i.opt()], outs=[mx_o.opt()]).then_inc(
            _sc, 1)
        g.wait_ge(_sc, _vc)
        return g.dma_start(out=stats[:, 32:64], in_=mx_o[:])
    p.step("gpsimd", armx, waits=[("p1c", 32)], incs=[("amx", 16)])

    def vng(v):
        return v.tensor_scalar(stats[:, 64:96], stats[:, 32:64], -1.0, None,
                               ALU.mult)
    p.step("vector", vng, waits=[("amx", 16)], incs=[("vng", 1)])

    for c in range(32):
        for s in range(4):
            a2 = c * 4 + s
            ksl, hb = kchunk(c)

            def am2(te, ksl=ksl, hb=hb, s=s):
                return te.matmul(pa[s % 2][:], ksl,
                                 q2[hb:hb + 64, s * 512:(s + 1) * 512],
                                 start=True, stop=True)
            w = []
            if a2 < 2:
                w = [("p1v", 128)]
            else:
                w.append(("p2a", a2 - 1))
            p.step("tensor", am2, waits=w, incs=[("p2m", 1)])

            def pa2(sc, c=c, s=s, a2=a2):
                return sc.activation(es2[:, (a2 % 2) * 512:(a2 % 2) * 512 + 512],
                                     pa[s % 2][:], AF.Exp,
                                     bias=stats[:, 64 + c:64 + c + 1])
            w = [("p2m", None), ("vng", 1)]
            if a2 >= 1:
                w.append(("p2v", a2 - 1))
            p.step("scalar", pa2, waits=w, incs=[("p2a", 1)])

            def pv2(v, c=c, s=s, a2=a2):
                return v.tensor_reduce(scw[:, 4 + s:5 + s],
                                       es2[:, (a2 % 2) * 512:(a2 % 2) * 512 + 512],
                                       AX.X, ALU.add)
            p.step("vector", pv2, waits=[("p2a", None)], incs=[("p2v", 1)])
            if s == 3:
                def pv2c(v, c=c):
                    return v.tensor_reduce(stats[:, 96 + c:97 + c], scw[:, 4:8],
                                           AX.X, ALU.add)
                p.step("vector", pv2c, incs=[("p2c", 1)])

    _sd, _vd = p.sem("aesd"), p.av("aesd", 16)
    _sc, _vc = p.sem("aesc"), p.av("aesc", 1)
    p.mark("aesd", 16)
    p.mark("aesc", 1)
    def ares(g, _sd=_sd, _vd=_vd, _sc=_sc, _vc=_vc):
        g.dma_start(out=es_i[:], in_=stats[:, 96:128]).then_inc(_sd, 16)
        g.wait_ge(_sd, _vd)
        g.collective_compute("AllReduce", ALU.add, replica_groups=PAIRS,
                             ins=[es_i.opt()], outs=[es_o.opt()]).then_inc(
            _sc, 1)
        g.wait_ge(_sc, _vc)
        return g.dma_start(out=stats[:, 96:128], in_=es_o[:])
    p.step("gpsimd", ares, waits=[("p2c", 32)], incs=[("aes", 16)])

    def vrs(v):
        return v.reciprocal(rs_sb[:], stats[:, 96:128])
    p.step("vector", vrs, waits=[("aes", 16)], incs=[("vrs", 1)])

    for c in range(32):
        hb, cc = (c // 16) * 64, c % 16

        def vtr(te, hb=hb, cc=cc, c=c):
            return te.transpose(ptr[c % 2][:],
                                vag_sb[hb:hb + 64, cc * 128:(cc + 1) * 128],
                                p.ident_bf[hb:hb + 64, hb:hb + 64])
        w = [("agv", 16), ("p2m", 128)]
        if c >= 2:
            w.append(("vtv", c - 1))
        p.step("tensor", vtr, waits=w, incs=[("vtr", 1)])

        def vtv(v, c=c):
            return v.tensor_copy(vT_sb[:, c * 64:(c + 1) * 64], ptr[c % 2][:])
        p.step("vector", vtv, waits=[("vtr", None)], incs=[("vtv", 1)])

    def vts(v):
        for c in range(32):
            inst = v.tensor_scalar(vTs[:, c * 64:(c + 1) * 64],
                                   vT_sb[:, c * 64:(c + 1) * 64],
                                   rs_sb[:, c:c + 1], None, ALU.mult)
        return inst
    p.step("vector", vts, waits=[("vtv", 32), ("vrs", 1)], incs=[("vts", 1)])

    for s in range(4):
        def emit_avm(c, s=s):
            i3 = s * 32 + c

            def avm(te, c=c, s=s):
                return te.matmul(pav[:], vTs[:, c * 64:(c + 1) * 64],
                                 PT2[c % 2][:], start=(c == 0), stop=(c == 31),
                                 skip_group_check=True)
            w = [("p3e", i3 + 1), ("vts", 1)]
            if c == 0 and s >= 1:
                w.append(("avev", s))
            p.step("tensor", avm, waits=w, incs=[("avm", 1)])

        for c in range(32):
            i3 = s * 32 + c
            ksl, hb = kchunk(c)

            def am3(te, ksl=ksl, hb=hb, c=c, s=s):
                return te.matmul(pa[c % 2][:], ksl,
                                 q2[hb:hb + 64, s * 512:(s + 1) * 512],
                                 start=True, stop=True, skip_group_check=True)
            w = []
            if i3 < 2:
                w = [("p2a", 128)]
            else:
                w.append(("p3e", i3 - 1))
            p.step("tensor", am3, waits=w, incs=[("p3m", 1)])

            def pe3(sc, c=c, s=s):
                return sc.activation(PT2[c % 2][:], pa[c % 2][:], AF.Exp,
                                     bias=stats[:, 64 + c:64 + c + 1])
            w = [("p3m", None)]
            if i3 >= 2:
                w.append(("avm", i3 - 1))
            p.step("scalar", pe3, waits=w, incs=[("p3e", 1)])

            if c >= 1:
                emit_avm(c - 1)
        emit_avm(31)

        def avev(sc, s=s):
            return sc.copy(av2[0:64, s * 512:(s + 1) * 512], pav[:])
        p.step("scalar", avev, waits=[("avm", (s + 1) * 32)], incs=[("avev", 1)])

    def avcp(sy):
        return sy.dma_start(out=av2[64:128, :], in_=av2[0:64, :])
    p.step("sync", avcp, waits=[("avev", 4)], incs=[("avcp", 16)])

    def attadd(v):
        for kc in range(NCH):
            inst = v.tensor_add(x_sb[:, kc * NT:(kc + 1) * NT],
                                x_sb[:, kc * NT:(kc + 1) * NT], av2[:])
        return inst
    p.step("vector", attadd, waits=[("avcp", 16)], incs=[("att", 8)])

    p.rel("wqkv_sb", "q2", "k_loc", "v_loc", "kag_sb", "vag_sb", "vT_sb",
          "vTs", "av2", "es2", "stats", "rs_sb", "scw", "ptA", "ptB")
    p.rel("ptr1", "ptr0", "pav", "pa1", "pa0")

    if stop == "attn":
        return _mk_finish(p, x_sb, xo)([("att", 8)])

    psR2 = p.psx("psR2", [128, 512])
    tsF = p.psx("tsF", [128, 128])
    psc = p.psx("psc", [8, 512])
    psCt = p.psx("psCt", [1, 8])
    psWs = p.psx("psWs", [1, 8])

    rwT_sb = p.sbx("rwT_sb", [128, 64], BF16)
    scFM = p.sbx("scFM", [8, NT])
    rt_s = p.sbx("rt_s", [128, 64])
    rcol3 = p.sbx("rcol3", [128, 4])
    mask_tm = p.sbx("mask_tm", [128, 128])
    wr_tm = p.sbx("wr_tm", [128, 128])
    cw_tm2 = p.sbx("cw_tm2", [128, 128])
    crow = p.sbx("crow", [1, 32])
    mwrep = p.sbx("mwrep", [128, 8])
    cwFM = p.sb("cwFM", [8, NT])
    rw3 = rwT.rearrange("(k p) m -> p k m", p=128)

    def ldrw(g):
        return g.dma_start(out=rwT_sb[:], in_=rw3[:, :, :])
    p.step("gpsimd", ldrw, waits=[("att", 8)], incs=[("ldrw", 16)])

    for s in range(4):
        def scm(te, s=s):
            for kc in range(NCH):
                inst = te.matmul(psc[:], rwT_sb[:, kc * 8:(kc + 1) * 8],
                                 xn_bf[:, kc * NT + s * 512:kc * NT + (s + 1) * 512],
                                 start=(kc == 0), stop=(kc == NCH - 1))
            return inst
        w = [("ldrw", 16)]
        if s == 0:
            w += [("avev", 4), ("vtv", 32)]
        else:
            w.append(("sce", s))
        p.step("tensor", scm, waits=w, incs=[("scm", 1)])

        def sce(sc, s=s):
            return sc.copy(scFM[0:8, s * 512:(s + 1) * 512], psc[:])
        p.step("scalar", sce, waits=[("scm", None)], incs=[("sce", 1)])

    for t in range(16):
        def rtt(te, t=t):
            return te.transpose(tsF[:, 0:8], scFM[0:8, t * 128:(t + 1) * 128],
                                p.ident_f[0:8, 0:8])
        w = [("sce", 4)]
        if t == 0:
            w += [("avev", 4), ("vtv", 32)]
        else:
            w.append(("rtv1", t))
        p.step("tensor", rtt, waits=w, incs=[("rtt", 1)])

        def rtv1(v, t=t):
            v.tensor_copy(rt_s[:, 0:8], tsF[:, 0:8])
            v.tensor_reduce(rcol3[:, 0:1], rt_s[:, 0:8], AX.X, ALU.max)
            return v.tensor_scalar(rcol3[:, 1:2], rcol3[:, 0:1], -1.0, None,
                                   ALU.mult)
        w = [("rtt", None)]
        if t >= 1:
            w.append(("rte", t))
        p.step("vector", rtv1, waits=w, incs=[("rtv1", 1)])

        def rte(sc, t=t):
            return sc.activation(rt_s[:, 8:16], rt_s[:, 0:8], AF.Exp,
                                 bias=rcol3[:, 1:2])
        p.step("scalar", rte, waits=[("rtv1", None)], incs=[("rte", 1)])

        def rtv2(v, t=t):
            v.tensor_reduce(rcol3[:, 2:3], rt_s[:, 8:16], AX.X, ALU.add)
            v.reciprocal(rcol3[:, 3:4], rcol3[:, 2:3])
            v.tensor_scalar(rt_s[:, 16:24], rt_s[:, 8:16], rcol3[:, 3:4], None,
                            ALU.mult)
            v.memset(rt_s[:, 24:32], 0.0)
            for e in range(8):
                v.scalar_tensor_tensor(rt_s[:, 24:32], rt_s[:, 16:24],
                                       rt_s[:, 16 + e:17 + e], rt_s[:, 24:32],
                                       ALU.is_lt, ALU.add)
            v.tensor_scalar(mask_tm[:, t * 8:(t + 1) * 8], rt_s[:, 24:32],
                            2.0, None, ALU.is_lt)
            v.tensor_mul(rt_s[:, 16:24], rt_s[:, 16:24],
                         mask_tm[:, t * 8:(t + 1) * 8])
            v.tensor_reduce(rcol3[:, 2:3], rt_s[:, 16:24], AX.X, ALU.add)
            v.reciprocal(rcol3[:, 3:4], rcol3[:, 2:3])
            return v.tensor_scalar(wr_tm[:, t * 8:(t + 1) * 8], rt_s[:, 16:24],
                                   rcol3[:, 3:4], None, ALU.mult)
        p.step("vector", rtv2, waits=[("rte", None)], incs=[("rtv2", 1)])

    for t in range(16):
        def rsm(te, t=t):
            te.matmul(psCt[:], p.ones_col[:], mask_tm[:, t * 8:(t + 1) * 8],
                      start=(t == 0), stop=(t == 15), skip_group_check=True)
            return te.matmul(psWs[:], p.ones_col[:], wr_tm[:, t * 8:(t + 1) * 8],
                             start=(t == 0), stop=(t == 15), skip_group_check=True)
        p.step("tensor", rsm, waits=[("rtv2", t + 1)], incs=[("rsm", 2)])

    def rse(sc):
        sc.copy(crow[0:1, 0:8], psCt[:])
        return sc.copy(crow[0:1, 8:16], psWs[:])
    p.step("scalar", rse, waits=[("rsm", 32)], incs=[("rse", 1)])

    _sd, _vd = p.sem("astd"), p.av("astd", 16)
    _sc, _vc = p.sem("astc"), p.av("astc", 1)
    p.mark("astd", 16)
    p.mark("astc", 1)
    def arst(g, _sd=_sd, _vd=_vd, _sc=_sc, _vc=_vc):
        g.dma_start(out=cst_i[:], in_=crow[0:1, 0:16]).then_inc(_sd, 16)
        g.wait_ge(_sd, _vd)
        g.collective_compute("AllReduce", ALU.add, replica_groups=ALL8,
                             ins=[cst_i.opt()], outs=[cst_o.opt()]).then_inc(
            _sc, 1)
        g.wait_ge(_sc, _vc)
        return g.dma_start(out=crow[0:1, 0:16], in_=cst_o[:])
    p.step("gpsimd", arst, waits=[("rse", 1)], incs=[("ast", 16)])

    def rmw(v):
        v.tensor_scalar(crow[0:1, 16:24], crow[0:1, 0:8], 1.0, None, ALU.max)
        v.reciprocal(crow[0:1, 16:24], crow[0:1, 16:24])
        v.tensor_mul(crow[0:1, 16:24], crow[0:1, 8:16], crow[0:1, 16:24])
        v.tensor_scalar(crow[0:1, 24:32], crow[0:1, 0:8], 1.0, None, ALU.min)
        return v.tensor_mul(crow[0:1, 16:24], crow[0:1, 16:24],
                            crow[0:1, 24:32])
    p.step("vector", rmw, waits=[("ast", 16)], incs=[("rmw", 1)])

    def rmrm(te):
        return te.matmul(psR2[:, 0:8], p.ones_r[:], crow[0:1, 16:24],
                         start=True, stop=True)
    p.step("tensor", rmrm, waits=[("rmw", 1)], incs=[("rmrm", 1)])

    def rmre(sc):
        return sc.copy(mwrep[:], psR2[:, 0:8])
    p.step("scalar", rmre, waits=[("rmrm", 1)], incs=[("rmre", 1)])

    for t in range(16):
        def rcw(v, t=t):
            return v.tensor_mul(cw_tm2[:, t * 8:(t + 1) * 8],
                                mask_tm[:, t * 8:(t + 1) * 8], mwrep[:])
        p.step("vector", rcw, waits=[("rmre", 1)], incs=[("rcw", 1)])

        def rct(te, t=t):
            return te.transpose(tsF[0:8, 0:128], cw_tm2[:, t * 8:(t + 1) * 8],
                                p.ident_f[:])
        w = [("rcw", t + 1)]
        if t >= 1:
            w.append(("rcv", t))
        p.step("tensor", rct, waits=w, incs=[("rct", 1)])

        def rcv(sc, t=t):
            return sc.copy(cwFM[0:8, t * 128:(t + 1) * 128], tsF[0:8, 0:128])
        p.step("scalar", rcv, waits=[("rct", None)], incs=[("rcv", 1)])

    p.rel("rwT_sb", "scFM", "rt_s", "rcol3", "mask_tm", "wr_tm", "cw_tm2")
    p.rel("psWs", "psCt", "psc", "tsF")

    if stop == "route":
        return _mk_finish(p, x_sb, xo)([("rcv", 16), ("att", 8)])

    cwrow = p.sb("cwrow", [1, NT])
    cwrep = p.sb("cwrep", [128, NT])
    xfc2 = [p.sb("xfcA", [128, 4 * NT], BF16), p.sb("xfcB", [128, 4 * NT], BF16)]
    he2 = [p.sb("heA", [128, 4 * NT], BF16), p.sb("heB", [128, 4 * NT], BF16)]

    def xfc_sl(k, a, b):
        return xfc2[k // 4][:, (k % 4) * NT + a:(k % 4) * NT + b]

    def he_sl(k, a, b):
        return he2[k // 4][:, (k % 4) * NT + a:(k % 4) * NT + b]

    gs2 = p.sb("gs2", [128, 2 * 512])
    wsl1 = p.sb("wsl1", [128, 2 * NCH * 128], BF16)
    gwe4 = gwe.rearrange("e (k p) m -> e p k m", p=128)
    uwe4 = uwe.rearrange("e (k p) m -> e p k m", p=128)
    dwe4 = dwe.rearrange("e (k p) m -> e p k m", p=128)

    pb = [p.psx(f"pb{i}", [128, 512]) for i in range(4)]

    wuser = []
    b01u = []
    b23u = []
    nld = [0]
    gsi = [0]

    def _ld(g_src, extra=()):
        j = nld[0]

        def ld(g, g_src=g_src):
            return g.dma_start(out=wsl1[:, (j % 2) * NCH * 128:
                                         (j % 2) * NCH * 128 + NCH * 128],
                               in_=g_src)
        w = list(extra)
        if j >= 2:
            w.append(wuser[j - 2])
        else:
            w.append(("rcv", 16))
        p.step("gpsimd", ld, waits=w, incs=[("ldw", 16)])
        nld[0] += 1
        return j

    for e in range(8):
        def cwr(sy, e=e):
            return sy.dma_start(out=cwrow[:], in_=cwFM[e:e + 1, :])
        w = [("rcv", 16)]
        if e >= 1:
            w.append(("crm", 4 * e))
        p.step("sync", cwr, waits=w, incs=[("cwr", 16)])

        for n in range(4):
            j = 4 * e + n

            def crep(te, e=e, n=n):
                return te.matmul(psR2[:], p.ones_r[:],
                                 cwrow[0:1, n * 512:(n + 1) * 512],
                                 start=True, stop=True)
            w = [("cwr", 16 * (e + 1))]
            w.append(("crv", j) if j >= 1 else ("rmre", 1))
            p.step("tensor", crep, waits=w, incs=[("crm", 1)])

            def crev(sc, e=e, n=n):
                return sc.copy(cwrep[:, n * 512:(n + 1) * 512], psR2[:])
            w = [("crm", None)]
            if e >= 1 and n == 0:
                w.append(("vxf", 8 * e))
            p.step("scalar", crev, waits=w, incs=[("crv", 1)])

        def mkxfc(v, e=e):
            for kc in range(NCH):
                inst = v.tensor_tensor(xfc_sl(kc, 0, NT),
                                       xn_bf[:, kc * NT:(kc + 1) * NT],
                                       cwrep[:], ALU.mult)
            return inst
        p.step("vector", mkxfc, waits=[("crv", 4 * (e + 1)), ("umm", None)],
               incs=[("vxf", NCH)])

        for q in range(4):
            for m in range(NCH):
                mi = q * NCH + m
                j = _ld(gwe4[e][:, :, mi * 128:(mi + 1) * 128])
                for n in range(4):
                    def gmm(te, j=j, n=n):
                        for kc in range(NCH):
                            inst = te.matmul(
                                pb[n % 2][:],
                                wsl1[:, (j % 2) * NCH * 128 + kc * 128:
                                     (j % 2) * NCH * 128 + (kc + 1) * 128],
                                xn_bf[:, kc * NT + n * 512:kc * NT + (n + 1) * 512],
                                start=(kc == 0), stop=(kc == NCH - 1))
                        return inst
                    j01 = len(b01u)
                    w = [("ldw", 16 * (j + 1))]
                    if j01 >= 2:
                        w.append(b01u[j01 - 2])
                    p.step("tensor", gmm, waits=w, incs=[("gmm", 1)])

                    gi = gsi[0]

                    def gev(sc, gi=gi, n=n):
                        return sc.activation(gs2[:, (gi % 2) * 512:
                                                 (gi % 2) * 512 + 512],
                                             pb[n % 2][:], AF.Silu)
                    w = [("gmm", None)]
                    if gi >= 1:
                        w.append(("hev", gi - 1))
                    p.step("scalar", gev, waits=w, incs=[("gev", 1)])
                    b01u.append(("gev", p.c("gev")))

                    def hemul(v, gi=gi, m=m, n=n):
                        return v.tensor_copy(he_sl(m, n * 512, (n + 1) * 512),
                                             gs2[:, (gi % 2) * 512:
                                                 (gi % 2) * 512 + 512])
                    w = [("gev", None)]
                    if m == 0 and n == 0:
                        w.append(("zmm", p.c("zmm")))
                    p.step("vector", hemul, waits=w, incs=[("hev", 1)])
                    gsi[0] += 1
                wuser.append(("gmm", p.c("gmm")))

            for m in range(NCH):
                mi = q * NCH + m
                j = _ld(uwe4[e][:, :, mi * 128:(mi + 1) * 128])
                for n in range(4):
                    def umm(te, j=j, n=n):
                        for kc in range(NCH):
                            inst = te.matmul(
                                pb[2 + n % 2][:],
                                wsl1[:, (j % 2) * NCH * 128 + kc * 128:
                                     (j % 2) * NCH * 128 + (kc + 1) * 128],
                                xfc_sl(kc, n * 512, (n + 1) * 512),
                                start=(kc == 0), stop=(kc == NCH - 1))
                        return inst
                    j23 = len(b23u)
                    w = [("ldw", 16 * (j + 1))]
                    if m == 0 and n == 0:
                        w.append(("vxf", 8 * (e + 1)))
                    if j23 >= 2:
                        w.append(b23u[j23 - 2])
                    p.step("tensor", umm, waits=w, incs=[("umm", 1)])

                    def uev(v, m=m, n=n):
                        sl = he_sl(m, n * 512, (n + 1) * 512)
                        return v.tensor_tensor(sl, sl, pb[2 + n % 2][:], ALU.mult)
                    p.step("vector", uev, waits=[("umm", None), ("hev", None)],
                           incs=[("uev", 1)])
                    b23u.append(("uev", p.c("uev")))
                wuser.append(("umm", p.c("umm")))

            for dc in range(NCH):
                j = _ld(dwe4[e][:, q * NCH:(q + 1) * NCH,
                                dc * 128:(dc + 1) * 128])
                for n in range(4):
                    def zmm(te, j=j, n=n):
                        for kc in range(NCH):
                            inst = te.matmul(
                                pb[n % 2][:],
                                wsl1[:, (j % 2) * NCH * 128 + kc * 128:
                                     (j % 2) * NCH * 128 + (kc + 1) * 128],
                                he_sl(kc, n * 512, (n + 1) * 512),
                                start=(kc == 0), stop=(kc == NCH - 1))
                        return inst
                    j01 = len(b01u)
                    w = [("ldw", 16 * (j + 1))]
                    if dc == 0 and n == 0:
                        w.append(("uev", p.c("uev")))
                    if j01 >= 2:
                        w.append(b01u[j01 - 2])
                    p.step("tensor", zmm, waits=w, incs=[("zmm", 1)])

                    def zev(v, dc=dc, n=n):
                        return v.scalar_tensor_tensor(
                            x_sb[:, dc * NT + n * 512:dc * NT + (n + 1) * 512],
                            pb[n % 2][:], 0.0,
                            x_sb[:, dc * NT + n * 512:dc * NT + (n + 1) * 512],
                            ALU.add, ALU.add)
                    w = [("zmm", None)]
                    if e == 0 and q == 0 and dc == 0 and n == 0:
                        w.append(("att", 8))
                    p.step("vector", zev, waits=w, incs=[("zev", 1)])
                    b01u.append(("zev", p.c("zev")))
                wuser.append(("zmm", p.c("zmm")))

    return _mk_finish(p, x_sb, xo)([("zev", None)])


def _stage_L3(inputs, core, x_cur):
    b, th = core // 2, core % 2
    x = x_cur[b, th * NT:(th + 1) * NT, :]
    tr = lambda a: np.ascontiguousarray(np.asarray(a).T)
    return {
        "x": np.ascontiguousarray(np.asarray(x).T),
        "wq": np.asarray(inputs["L3_attn_Wq"]),
        "wk": np.asarray(inputs["L3_attn_Wk"]),
        "wv": np.asarray(inputs["L3_attn_Wv"]),
        "rwT": tr(inputs["L3_ffn_router_w"]),
        "gwe": np.ascontiguousarray(np.transpose(
            np.asarray(inputs["L3_ffn_gate_w"]), (0, 2, 1))),
        "uwe": np.ascontiguousarray(np.transpose(
            np.asarray(inputs["L3_ffn_up_w"]), (0, 2, 1))),
        "dwe": np.ascontiguousarray(np.transpose(
            np.asarray(inputs["L3_ffn_down_w"]), (0, 2, 1))),
    }


def _stage_L2(inputs, core, x_cur):
    b, th = core // 2, core % 2
    x = x_cur[b, th * NT:(th + 1) * NT, :]
    tr = lambda a: np.ascontiguousarray(np.asarray(a).T)
    return {
        "x": np.ascontiguousarray(x.T),
        "wqT": tr(inputs["L2_attn_Wq"]),
        "wkT": tr(inputs["L2_attn_Wk"]),
        "wvT": tr(inputs["L2_attn_Wv"]),
        "woT": tr(inputs["L2_attn_Wo"]),
        "bq_c": np.ascontiguousarray(inputs["L2_attn_bq"].reshape(8, 128).T),
        "bk_c": np.ascontiguousarray(inputs["L2_attn_bk"].reshape(8, 128).T),
        "bv_c": np.ascontiguousarray(inputs["L2_attn_bv"].reshape(8, 128).T),
        "bo_c": np.ascontiguousarray(inputs["L2_attn_bo"].reshape(8, 128).T),
        "projT": tr(inputs["L2_attn_proj"]),
        "gwT": tr(inputs["L2_ffn_gate_w"]),
        "ebias": inputs["L2_ffn_expert_bias"].reshape(1, 16),
        "wge": np.ascontiguousarray(np.transpose(inputs["L2_ffn_w_gate"], (0, 2, 1))),
        "wue": np.ascontiguousarray(np.transpose(inputs["L2_ffn_w_up"], (0, 2, 1))),
        "wde": np.ascontiguousarray(np.transpose(inputs["L2_ffn_w_down"], (0, 2, 1))),
        "wgs": tr(inputs["L2_ffn_shared_gate_w"]),
        "wus": tr(inputs["L2_ffn_shared_up_w"]),
        "wds": tr(inputs["L2_ffn_shared_down_w"]),
    }


def _mk_finish(p, x_sb, xo):
    def fin(wl):
        if xo is None:
            return wl
        xo3 = xo.rearrange("(k p) n -> k p n", p=128)

        def _st(sy):
            return [sy.dma_start(out=xo3[k], in_=x_sb[:, k * NT:(k + 1) * NT])
                    for k in range(NCH)]
        p.step("sync", _st, waits=wl, incs=[("stf", 16 * NCH)])
        p.step("sync", lambda e: None, waits=[("stf", None)])
        p.emit()
        return p
    return fin



def _stage_L0(inputs, core, x_cur):
    b, th = core // 2, core % 2
    x = x_cur[b, th * NT:(th + 1) * NT, :]
    return {
        "x": np.ascontiguousarray(x.T),
        "pm_col": np.full((128, 1), float(th), np.float32),
        "pmc_col": np.full((128, 1), 1.0 - float(th), np.float32),
        "wiT": np.ascontiguousarray(inputs["L0_attn_Wi"].T),
        "wgT": np.ascontiguousarray(inputs["L0_attn_Wg"].T),
        "woT": np.ascontiguousarray(inputs["L0_attn_Wo"].T),
        "bi_c": np.ascontiguousarray(inputs["L0_attn_bi"].reshape(16, 128).T),
        "bg_c": np.ascontiguousarray(inputs["L0_attn_bg"].reshape(16, 128).T),
        "bo_c": np.ascontiguousarray(inputs["L0_attn_bo"].reshape(8, 128).T),
        "loga_c": np.ascontiguousarray(inputs["L0_attn_log_a"].reshape(16, 128).T),
        "clip": np.stack([inputs["L0_ffn_clip_x"],
                          inputs["L0_ffn_clip_down"]], 1).astype(np.float32),
        "wqg": inputs["L0_ffn_w_gate"],
        "wqu": inputs["L0_ffn_w_up"],
        "wqd": inputs["L0_ffn_w_down"],
    }


def _stage_L1(inputs, core, x_cur):
    b, th = core // 2, core % 2
    x = x_cur[b, th * NT:(th + 1) * NT, :]
    return {
        "x": np.ascontiguousarray(x.T),
        "pm_col": np.full((128, 1), float(th), np.float32),
        "pmc_col": np.full((128, 1), 1.0 - float(th), np.float32),
        "wwT": np.ascontiguousarray(inputs["L1_attn_W_write"].T),
        "wrT": np.ascontiguousarray(inputs["L1_attn_W_read"].T),
        "dec_la": np.array([[inputs["L1_attn_decay"], inputs["L1_attn_log_alpha"]]],
                           np.float32),
        "wg16T": np.ascontiguousarray(inputs["L1_ffn_Wg"].T),
        "w1": inputs["L1_ffn_W1"],
        "w2": inputs["L1_ffn_W2"],
        "ws1": inputs["L1_ffn_Ws1"],
        "ws2": inputs["L1_ffn_Ws2"],
    }


def build_fused(upto=4):
    p = Prog()
    nc = p.nc
    x_in = nc.dram_tensor("x", [D, NT], F32, kind="ExternalInput").ap()
    xo = nc.dram_tensor("xo", [D, NT], F32, kind="ExternalOutput").ap()
    x_sb = p.sb("x_sb", [128, NCH * NT])
    xn_bf = p.sb("xn_bf", [128, NCH * NT], BF16)
    barc = p.sb("barc", [1, 1])
    emit_common(p)
    x3 = x_in.rearrange("(k p) n -> k p n", p=128)
    builders = [build_L0, build_L1, build_L2, build_L3][:upto]
    prev_wl = None
    prev_px = None
    for i, build in enumerate(builders):
        p.px = f"L{i}_"
        if i == 0:
            def ldx(sy):
                return [sy.dma_start(out=x_sb[:, k * NT:(k + 1) * NT],
                                     in_=x3[k]) for k in range(NCH)]
            p.step("sync", ldx, incs=[("x_ok", 16 * NCH)])
        else:
            p.step("vector", lambda v: v.memset(barc[:], 0.0),
                   waits=prev_wl, incs=[("x_ok", 1)])
        p.step("vector", lambda v: v.memset(barc[:], 0.0),
               waits=[("/common", 1)], incs=[("common", 1)])
        if prev_px is not None:
            p.free_sems(prev_px)
        wl = build(fx={"p": p, "x_sb": x_sb, "xn_bf": xn_bf})
        prev_wl = [("/" + p.px + n, p.c(n) if v is None else v)
                   for (n, v) in wl]
        p.rel(*[n for n in list(p._scoped) if n.startswith(p.px)])
        p.rel(*[n for n in list(p._sbal) if n.startswith(p.px)])
        p.reclaim("/xn_bf")
        prev_px = p.px
    p.px = ""

    xo3 = xo.rearrange("(k p) n -> k p n", p=128)

    def store(sy):
        return [sy.dma_start(out=xo3[k], in_=x_sb[:, k * NT:(k + 1) * NT])
                for k in range(NCH)]
    p.step("sync", store, waits=prev_wl, incs=[("stf", 16 * NCH)])
    p.step("sync", lambda e: None, waits=[("stf", None)])
    p.emit()
    return p


def _stage_fused(inputs, core, x_cur, upto=4):
    stages = [_stage_L0, _stage_L1, _stage_L2, _stage_L3][:upto]
    m = {}
    for i, stage in enumerate(stages):
        sm = stage(inputs, core, x_cur)
        for k, v in sm.items():
            if k == "x":
                if i == 0:
                    m["x"] = v
            else:
                m[f"L{i}_{k}"] = v
    return m


def kernel(**inputs):
    x_cur = np.asarray(inputs["L0_attn_x"], np.float32)
    p = build_fused()
    in_maps = [_stage_fused(inputs, c, x_cur) for c in range(8)]
    res = run_bass_kernel_spmd(p.nc, in_maps, core_ids=list(range(8)))
    p.close()
    out = np.empty((4, 4096, 1024), np.float32)
    for c in range(8):
        b, th = c // 2, c % 2
        out[b, th * NT:(th + 1) * NT, :] = res.results[c]["xo"].T
    return out


def kernel_multi(**inputs):
    x_cur = np.asarray(inputs["L0_attn_x"], np.float32)
    layers = [(build_L0, _stage_L0), (build_L1, _stage_L1),
              (build_L2, _stage_L2), (build_L3, _stage_L3)]
    for build, stage in layers:
        p = build()
        in_maps = [stage(inputs, c, x_cur) for c in range(8)]
        res = run_bass_kernel_spmd(p.nc, in_maps, core_ids=list(range(8)))
        p.close()
        out = np.empty((4, 4096, 1024), np.float32)
        for c in range(8):
            b, th = c // 2, c % 2
            out[b, th * NT:(th + 1) * NT, :] = res.results[c]["xo"].T
        x_cur = out
    return x_cur
